# version 07
# baseline (speedup 1.0000x reference)
import numpy as np
import ml_dtypes
import contextlib
import concourse.bass as bass
import concourse.mybir as mybir
from concourse.bass_utils import run_bass_kernel_spmd

F32 = mybir.dt.float32
BF16 = mybir.dt.bfloat16
I32 = mybir.dt.int32
AF = mybir.ActivationFunctionType
ALU = mybir.AluOpType
AX = mybir.AxisListType
BF = ml_dtypes.bfloat16

D = 2048
T = 4096
NCORES = 8
NT = 8
H0 = 5632
H0C = 44
H0P = 6144
HI = 8192
EPS_LN = 1e-5
DK = 128
NHEAD = 16
NKV = 4
QSCALE = 1.0 / np.sqrt(128.0)


def core_rows(c):
    s, k = divmod(c, 4)
    return np.concatenate([np.arange(s * T + (4 * j + k) * 128,
                                     s * T + (4 * j + k) * 128 + 128) for j in range(8)])

def img_tokens(x2d, c):
    r = core_rows(c)
    w = x2d.shape[1]
    out = np.empty((128, NT * w), x2d.dtype)
    for t in range(NT):
        out[:, t * w:(t + 1) * w] = x2d[r[t * 128:(t + 1) * 128]]
    return np.ascontiguousarray(out)

def unimg_tokens(img, w):
    out = np.empty((NT * 128, w), img.dtype)
    for t in range(NT):
        out[t * 128:(t + 1) * 128] = img[:, t * w:(t + 1) * w]
    return out

def wimg(wT):
    K, N = wT.shape
    nk = (K + 127) // 128
    out = np.zeros((128, nk * N), wT.dtype)
    for a in range(nk):
        rows = wT[a * 128:(a + 1) * 128]
        out[:rows.shape[0], a * N:(a + 1) * N] = rows
    return np.ascontiguousarray(out)


class Sem:
    def __init__(self, es, nc, name):
        self.h = es.enter_context(nc.semaphore(name))
        self.n = 0
    def inc(self, instr, k=1):
        instr.then_inc(self.h, k)
        self.n += k
        return self.n
    def wait(self, eng, val=None):
        eng.wait_ge(self.h, self.n if val is None else val)

def emit_ln(v, sc, S, tile_fn, n_tiles, width, stats, mvall, negm, sd, rstd, eps_tile):
    nchunk = (width + 511) // 512
    for t in range(n_tiles):
        xv = tile_fn(t)
        s0 = t * nchunk * 6
        for ch in range(nchunk):
            ist = v.bn_stats(stats[:, s0 + 6 * ch:s0 + 6 * ch + 6], xv[:, ch * 512:min(width, (ch + 1) * 512)])
            S['v'].wait(v, S['v'].inc(ist))
        ia = v.bn_aggr(mvall[:, 2 * t:2 * t + 2], stats[:, s0:s0 + 6 * nchunk])
        S['v'].wait(v, S['v'].inc(ia))
    for t in range(n_tiles):
        i1 = v.tensor_scalar(negm[:, t:t + 1], mvall[:, 2 * t:2 * t + 1], -1.0, None, ALU.mult)
    val = S['v'].inc(i1)
    S['v'].wait(sc, val)
    for t in range(n_tiles):
        i2 = sc.activation(sd[:, t:t + 1], mvall[:, 2 * t + 1:2 * t + 2], AF.Sqrt,
                           bias=eps_tile[:, 0:1], scale=1.0)
    sval = S['s'].inc(i2)
    S['s'].wait(v, sval)
    i3 = v.reciprocal(rstd[:, 0:n_tiles], sd[:, 0:n_tiles])
    rv = S['v'].inc(i3)
    S['v'].wait(v, rv)
    return rv

def emit_ln_transpose(nc, S, v, sy, sc, te, x_in, xt, xlnt, xlnT3, sidb,
                      stats, mvall, negm, sd, rstd, eps_tile, ptrb):
    ALUl = ALU
    apply_done = []
    evac_done = []
    base_ldx = S['ldx'].n
    for t in range(2):
        S['ldx'].inc(sy.dma_start(out=xt[:, (t % 2) * D:(t % 2 + 1) * D],
                                  in_=x_in[:, t * D:(t + 1) * D]), 16)
    for t in range(NT):
        tb = (t % 2) * D
        S['ldx'].wait(v, base_ldx + 16 * (t + 1))
        emit_ln(v, sc, S, lambda _t: xt[:, tb:tb + D], 1, D,
                stats[:, t * 24:], mvall[:, 2 * t:], negm[:, t:], sd[:, t:],
                rstd[:, t:], eps_tile)
        ia = v.tensor_scalar(xlnt[:, tb:tb + D], xt[:, tb:tb + D],
                             negm[:, t:t + 1], rstd[:, t:t + 1], ALUl.add, ALUl.mult)
        apply_done.append(S['v'].inc(ia))
        if t + 2 < NT:
            S['v'].wait(sy, apply_done[t])
            S['ldx'].inc(sy.dma_start(out=xt[:, ((t + 2) % 2) * D:((t + 2) % 2 + 1) * D],
                                      in_=x_in[:, (t + 2) * D:(t + 3) * D]), 16)
        for q in range(4):
            gi = t * 4 + q
            bank = ptrb[gi % 2]
            S['v'].wait(te, apply_done[-1])
            if gi >= 2:
                S['v'].wait(te, evac_done[gi - 2])
            for i in range(4):
                dc = q * 4 + i
                ins = te.transpose(bank[:, i * 128:(i + 1) * 128],
                                   xlnt[:, tb + dc * 128:tb + (dc + 1) * 128], sidb[:])
            tv = S['te'].inc(ins)
            S['te'].wait(v, tv)
            dst = xlnT3[:, 4 * q:4 * q + 4, t * 128:(t + 1) * 128]
            evac_done.append(S['v'].inc(v.tensor_copy(dst, bank[:])))
    return S['v'].n


def build_mega(nc):
    x_in = nc.dram_tensor("x_img", [128, NT * D], F32, kind="ExternalInput").ap()
    wkvv_in = nc.dram_tensor("wkvv", [128, 16 * 256], F32, kind="ExternalInput").ap()
    bkvv_in = nc.dram_tensor("bkvv", [1, 256], F32, kind="ExternalInput").ap()
    identb_in = nc.dram_tensor("identb", [128, 128], BF16, kind="ExternalInput").ap()
    ident32_in = nc.dram_tensor("ident32", [128, 128], F32, kind="ExternalInput").ap()
    w1cs_in = nc.dram_tensor("w1cs", [128, 12288], F32, kind="ExternalInput").ap()
    w2cs_in = nc.dram_tensor("w2cs", [128, 12288], F32, kind="ExternalInput").ap()
    w3rs_in = nc.dram_tensor("w3rs", [128, 12288], F32, kind="ExternalInput").ap()
    lw_in = nc.dram_tensor("lw", [128, 116 * 512], F32, kind="ExternalInput").ap()
    cos_in = nc.dram_tensor("cosd", [64, 1024], F32, kind="ExternalInput").ap()
    sin_in = nc.dram_tensor("sind", [64, 1024], F32, kind="ExternalInput").ap()
    mask_in = nc.dram_tensor("mask", [128, 512], BF16, kind="ExternalInput").ap()
    xo_out = nc.dram_tensor("xo", [128, NT * D], F32, kind="ExternalOutput").ap()
    pabs_d = nc.dram_tensor("pabs_d", [1, 8], F32, kind="Internal").ap()
    pall_d = nc.dram_tensor("pall_d", [1, 8], F32, kind="Internal",
                            addr_space="Shared").ap()
    gam_d = nc.dram_tensor("gam_d", [1, 4], F32, kind="Internal").ap()
    q1c_d = nc.dram_tensor("q1c_d", [128, 12288], BF16, kind="Internal").ap()
    q2c_d = nc.dram_tensor("q2c_d", [128, 12288], BF16, kind="Internal").ap()
    q3c_d = nc.dram_tensor("q3c_d", [128, 12288], BF16, kind="Internal").ap()
    G1 = nc.dram_tensor("G1", [1024, 12288], BF16, kind="Internal",
                        addr_space="Shared").ap()
    G2 = nc.dram_tensor("G2", [1024, 12288], BF16, kind="Internal",
                        addr_space="Shared").ap()
    G3 = nc.dram_tensor("G3", [1024, 12288], BF16, kind="Internal",
                        addr_space="Shared").ap()
    x1_d = nc.dram_tensor("x1_d", [128, NT * D], F32, kind="Internal").ap()
    x2_d = nc.dram_tensor("x2_d", [128, NT * D], F32, kind="Internal").ap()
    x3_d = nc.dram_tensor("x3_d", [128, NT * D], F32, kind="Internal").ap()
    q_d = nc.dram_tensor("q_d", [128, 16 * 1024], BF16, kind="Internal").ap()
    kc_d = nc.dram_tensor("kc_d", [128, 4096], BF16, kind="Internal").ap()
    vc_d = nc.dram_tensor("vc_d", [1024, 512], BF16, kind="Internal").ap()
    KG = nc.dram_tensor("KG", [512, 4096], BF16, kind="Internal").ap()
    VG = nc.dram_tensor("VG", [4096, 512], BF16, kind="Internal").ap()
    nb_d = nc.dram_tensor("nb_d", [1, 16], F32, kind="Internal").ap()
    lwc_d = nc.dram_tensor("lwc_d", [128, 116 * 512], BF16, kind="Internal").ap()
    GW = nc.dram_tensor("GW", [1024, 116 * 512], BF16, kind="Internal",
                        addr_space="Shared").ap()

    with contextlib.ExitStack() as es:
        sb = lambda n_, s_, d_: es.enter_context(nc.sbuf_tensor(n_, s_, d_))
        sidb = sb("sidb", [128, 128], BF16)
        eps_tile = sb("eps_tile", [128, 1], F32)
        bar_t = sb("bar_t", [1, 1], F32)
        stats = sb("stats", [128, NT * 24], F32)
        mvall = sb("mvall", [128, 16], F32)
        negm = sb("negm", [128, 8], F32)
        sd_ = sb("sd_", [128, 8], F32)
        rstd = sb("rstd", [128, 8], F32)
        SV = Sem(es, nc, "sem_v")
        SS = Sem(es, nc, "sem_s")
        STE = Sem(es, nc, "sem_te")
        SGE = Sem(es, nc, "sem_g")
        ENG = {'v': SV, 's': SS, 'te': STE, 'g': SGE}
        mk = lambda pref, keys: {**{k: Sem(es, nc, f"sem_{pref}_{k}") for k in keys}, **ENG}
        SP = mk("p", ['ld'])
        SW = mk("w", ['ld', 'st'])
        SQ = mk("q", ['ld', 'st'])
        SA = mk("a", ['ld', 'st'])
        SC_ = mk("c", ['ldx', 'ldw', 'ldw3', 'ldxr', 'st'])
        SD = mk("d", ['ldx', 'ld', 'ldw', 'st'])
        SE = mk("e", ['ld', 'ldc', 'ldxr', 'st', 'gb'])
        SF = mk("f", ['ldx', 'ldw', 'ldwd', 'ldxr', 'st'])
        ALL_SEMS = []
        for dct in (SP, SW, SQ, SA, SC_, SD, SE, SF):
            for k, s_ in dct.items():
                if s_ not in ALL_SEMS:
                    ALL_SEMS.append(s_)
        ccg = es.enter_context(nc.semaphore("ccg"))
        ccw1 = es.enter_context(nc.semaphore("ccw1"))
        ccw2 = es.enter_context(nc.semaphore("ccw2"))
        ccw3 = es.enter_context(nc.semaphore("ccw3"))
        cck = es.enter_context(nc.semaphore("cck"))
        ccv = es.enter_context(nc.semaphore("ccv"))
        ccL = es.enter_context(nc.semaphore("ccL"))

        block = es.enter_context(nc.Block())

        @block.vector
        def _(v):
            sy, sc, te, g = nc.sync, nc.scalar, nc.tensor, nc.gpsimd

            def barrier():
                for s_ in ALL_SEMS:
                    if s_ is not SV:
                        s_.wait(v)
                SV.wait(v)
                bv = SV.inc(v.memset(bar_t[:], 0.0))
                for eng in (sy, sc, te, g):
                    SV.wait(eng, bv)

            SP['ld'].inc(sy.dma_start(out=sidb[:], in_=identb_in[:, :]), 16)
            pl_all = SP['ld'].n
            pv = SV.inc(v.memset(eps_tile[:], EPS_LN))
            for eng in (te, sc, g, v):
                SP['ld'].wait(eng, pl_all)
                SV.wait(eng, pv)

            def gw_tile(u, nu):
                r, o = divmod(u, 116)
                return GW[r * 128:(r + 1) * 128, o * 512:(o + nu) * 512]

            esW = contextlib.ExitStack()
            PCW = 29 * 512
            lwf = esW.enter_context(nc.sbuf_tensor("lwfW", [128, 2 * PCW], F32))
            lwb = esW.enter_context(nc.sbuf_tensor("lwbW", [128, PCW], BF16))
            cvW = []
            for piece in range(4):
                if piece >= 2:
                    SS.wait(sy, cvW[piece - 2])
                lv = SW['ld'].inc(sy.dma_start(
                    out=lwf[:, (piece % 2) * PCW:(piece % 2 + 1) * PCW],
                    in_=lw_in[:, piece * PCW:(piece + 1) * PCW]), 16)
                SW['ld'].wait(sc, lv)
                if piece >= 1:
                    SW['st'].wait(sc, 16 * piece)
                ic_ = sc.activation(lwb[:], lwf[:, (piece % 2) * PCW:(piece % 2 + 1) * PCW],
                                    AF.Copy, bias=0.0, scale=1.0)
                cv = SS.inc(ic_)
                cvW.append(cv)
                SS.wait(g, cv)
                SW['st'].inc(g.dma_start(out=lwc_d[:, piece * PCW:(piece + 1) * PCW],
                                         in_=lwb[:]), 16)
            SW['st'].wait(g)
            g.collective_compute("AllGather", ALU.bypass,
                                 replica_groups=[list(range(NCORES))],
                                 ins=[lwc_d.opt()], outs=[GW.opt()]).then_inc(ccL)
            esW.close()
            barrier()

            esQ = contextlib.ExitStack()
            sbQ = lambda n_, s_, d_: esQ.enter_context(nc.sbuf_tensor(n_, s_, d_))
            ws = sbQ("wsQ", [128, 6144], F32)
            wi = sbQ("wiQ", [128, 6144], F32)
            wqb = sbQ("wqbQ", [128, 6144], BF16)
            absacc = sbQ("absaccQ", [128, 8], F32)
            mcol = sbQ("mcolQ", [128, 4], F32)
            pabs_sb = sbQ("pabsQ", [1, 8], F32)
            pall_sb = sbQ("pallQ", [1, 8], F32)
            gamr = sbQ("gamrQ", [1, 4], F32)
            rcol = sbQ("rcolQ", [128, 4], F32)
            pieces = [(w1cs_in, 0, 0), (w1cs_in, 6144, 1),
                      (w2cs_in, 0, 2), (w2cs_in, 6144, 3),
                      (w3rs_in, 0, 4), (w3rs_in, 6144, 5)]
            red_done = []
            for pi, (srcap, c0, col) in enumerate(pieces):
                if pi > 0:
                    SV.wait(sy, red_done[-1])
                lv = SQ['ld'].inc(sy.dma_start(out=ws[:], in_=srcap[:, c0:c0 + 6144]), 16)
                SQ['ld'].wait(v, lv)
                ins = v.tensor_reduce(absacc[:, col:col + 1], ws[:],
                                      axis=AX.X, op=ALU.add, apply_absolute_value=True)
                red_done.append(SV.inc(ins))
                SV.wait(v, red_done[-1])
            for m in range(3):
                ins = v.tensor_tensor(mcol[:, m:m + 1], absacc[:, 2 * m:2 * m + 1],
                                      absacc[:, 2 * m + 1:2 * m + 2], op=ALU.add)
            SV.inc(ins)
            mz = SV.inc(v.memset(pabs_sb[:], 0.0))
            SV.wait(g, SV.n)
            gv = SGE.inc(g.tensor_reduce(pabs_sb[0:1, 0:3], mcol[:, 0:3],
                                         axis=AX.C, op=ALU.add))
            SGE.wait(g, gv)
            sv_ = SQ['st'].inc(g.dma_start(out=pabs_d[0:1, :], in_=pabs_sb[:]), 16)
            SQ['st'].wait(g, sv_)
            g.collective_compute("AllReduce", ALU.add,
                                 replica_groups=[list(range(NCORES))],
                                 ins=[pabs_d.opt()], outs=[pall_d.opt()]).then_inc(ccg)
            sy.wait_ge(ccg, 1)
            lv = SQ['ld'].inc(sy.dma_start(out=pall_sb[:], in_=pall_d[0:1, :]), 16)
            SQ['ld'].wait(v, lv)
            i = v.tensor_scalar(gamr[:, 0:3], pall_sb[:, 0:3], 1.0 / (H0 * D), 1e-8,
                                ALU.mult, ALU.add)
            SV.wait(v, SV.inc(i))
            i = v.reciprocal(gamr[:, 0:3], gamr[:, 0:3])
            iv = SV.inc(i)
            SV.wait(g, iv)
            gv2 = SQ['st'].inc(g.dma_start(out=gam_d[0:1, 0:3], in_=gamr[:, 0:3]), 16)
            SQ['st'].wait(g, gv2)
            gv2 = SQ['st'].inc(g.dma_start(
                out=rcol[:, 0:3],
                in_=gam_d[0:1, 0:3].squeeze(0).partition_broadcast(128)), 16)
            SQ['st'].wait(v, gv2)
            for m, (src, dst) in enumerate([(w1cs_in, q1c_d), (w2cs_in, q2c_d),
                                            (w3rs_in, q3c_d)]):
                for half in range(2):
                    lv = SQ['ld'].inc(sy.dma_start(out=ws[:], in_=src[:, half * 6144:(half + 1) * 6144]), 16)
                    SQ['ld'].wait(v, lv)
                    v.tensor_scalar(wi[:], ws[:], rcol[:, m:m + 1], 0.5, ALU.mult, ALU.is_ge)
                    v.tensor_scalar(ws[:], ws[:], rcol[:, m:m + 1], -0.5, ALU.mult, ALU.is_le)
                    i = v.tensor_tensor(wqb[:], wi[:], ws[:], op=ALU.subtract)
                    qv2 = SV.inc(i)
                    SV.wait(g, qv2)
                    sv2 = SQ['st'].inc(g.dma_start(out=dst[:, half * 6144:(half + 1) * 6144],
                                                   in_=wqb[:]), 16)
                    SQ['st'].wait(sy, sv2)
                    SQ['st'].wait(v, sv2)
            SQ['st'].wait(g)
            for (cin, cout, ch) in [(q1c_d, G1, ccw1), (q2c_d, G2, ccw2), (q3c_d, G3, ccw3)]:
                g.collective_compute("AllGather", ALU.bypass,
                                     replica_groups=[list(range(NCORES))],
                                     ins=[cin.opt()], outs=[cout.opt()]).then_inc(ch)
            esQ.close()
            barrier()

            esA = contextlib.ExitStack()
            sbA = lambda n_, s_, d_: esA.enter_context(nc.sbuf_tensor(n_, s_, d_))
            xs = sbA("xsA", [128, NT * D], F32)
            xln = sbA("xlnA", [128, NT * D], BF16)
            xlnT = sbA("xlnTA", [128, 16 * 1024], BF16)
            wkv32 = sbA("wkv32A", [128, 4096], F32)
            wkvb = sbA("wkvbA", [128, 4096], BF16)
            bkv32 = sbA("bkv32A", [1, 256], F32)
            bkvb = sbA("bkvbA", [1, 256], BF16)
            vfm = sbA("vfmA", [128, 2048], F32)
            vT = sbA("vTA", [128, NT * 256], F32)
            lnv = sbA("lnvA", [128, 2 * 256], F32)
            sid32 = sbA("sid32A", [128, 128], F32)
            ones_row = sbA("onesrA", [1, 512], BF16)
            ptrbA = [esA.enter_context(nc.psum_tensor("ptrbA0", [128, 512], BF16)),
                     esA.enter_context(nc.psum_tensor("ptrbA1", [128, 512], BF16))]
            pvb = [esA.enter_context(nc.psum_tensor("pvbA0", [128, 512], F32)),
                   esA.enter_context(nc.psum_tensor("pvbA1", [128, 512], F32))]
            ptr32A = [esA.enter_context(nc.psum_tensor("ptr32A0", [128, 512], F32)),
                      esA.enter_context(nc.psum_tensor("ptr32A1", [128, 512], F32))]
            xlnT3 = xlnT[:].rearrange("p (a c) -> p a c", c=1024)
            vT3 = vT[:].rearrange("p (t c) -> p t c", c=256)
            SA['ld'].inc(sy.dma_start(out=xs[:], in_=x_in[:, :]), 16)
            SA['ld'].inc(sy.dma_start(out=wkv32[:], in_=wkvv_in[:, :]), 16)
            SA['ld'].inc(sy.dma_start(out=bkv32[:], in_=bkvv_in[:, :]), 16)
            SA['ld'].inc(sy.dma_start(out=sid32[:], in_=ident32_in[:, :]), 16)
            ld_all = SA['ld'].n
            orv = SV.inc(v.memset(ones_row[:], 1.0))
            SV.wait(te, orv)
            SA['ld'].wait(sc, ld_all)
            sc.activation(wkvb[:], wkv32[:], AF.Copy, bias=0.0, scale=1.0)
            cast_done = SS.inc(sc.activation(bkvb[:], bkv32[:], AF.Copy, bias=0.0, scale=1.0))
            SA['ld'].wait(v, 16)
            emit_ln(v, sc, {'v': SV, 's': SS}, lambda t: xs[:, t * D:(t + 1) * D], NT, D,
                    stats, mvall, negm, sd_, rstd, eps_tile)
            for t in range(NT):
                i = v.tensor_scalar(xln[:, t * D:(t + 1) * D], xs[:, t * D:(t + 1) * D],
                                    negm[:, t:t + 1], rstd[:, t:t + 1], ALU.add, ALU.mult)
            xln_done = SV.inc(i)
            SA['ld'].wait(te, ld_all)
            SV.wait(te, xln_done)
            evac_vals = []
            gi = 0
            for t in range(NT):
                for q in range(4):
                    bank = ptrbA[gi % 2]
                    if gi >= 2:
                        SV.wait(te, evac_vals[gi - 2])
                    for i in range(4):
                        dc = q * 4 + i
                        ins = te.transpose(bank[:, i * 128:(i + 1) * 128],
                                           xln[:, t * D + dc * 128:t * D + (dc + 1) * 128],
                                           sidb[:])
                    tv = STE.inc(ins)
                    STE.wait(v, tv)
                    dst = xlnT3[:, 4 * q:4 * q + 4, t * 128:(t + 1) * 128]
                    evac_vals.append(SV.inc(v.tensor_copy(dst, bank[:])))
                    gi += 1
            xlnT_done = SV.n
            SS.wait(te, cast_done)
            SV.wait(te, xlnT_done)
            vp_vals = []
            for idx, (vc, tb) in enumerate([(a, b) for a in range(2) for b in range(2)]):
                bank = pvb[idx % 2]
                if idx >= 2:
                    SV.wait(te, vp_vals[idx - 2])
                for dc in range(16):
                    te.matmul(bank[:], wkvb[:, dc * 256 + vc * 128:dc * 256 + (vc + 1) * 128],
                              xlnT3[:, dc, tb * 512:(tb + 1) * 512],
                              start=(dc == 0), stop=False)
                ins = te.matmul(bank[:], bkvb[:, vc * 128:(vc + 1) * 128], ones_row[:],
                                start=False, stop=True)
                tv = STE.inc(ins)
                STE.wait(v, tv)
                vp_vals.append(SV.inc(v.tensor_copy(
                    vfm[:, vc * 1024 + tb * 512:vc * 1024 + (tb + 1) * 512], bank[:])))
            SV.wait(te)
            vt_vals = []
            for grp in range(4):
                vc, th = divmod(grp, 2)
                bank = ptr32A[grp % 2]
                if grp >= 2:
                    SV.wait(te, vt_vals[grp - 2])
                for i in range(4):
                    t = th * 4 + i
                    ins = te.transpose(bank[:, i * 128:(i + 1) * 128],
                                       vfm[:, vc * 1024 + t * 128:vc * 1024 + (t + 1) * 128],
                                       sid32[:])
                tv = STE.inc(ins)
                STE.wait(v, tv)
                dst = vT3[:, th * 4:th * 4 + 4, vc * 128:(vc + 1) * 128]
                vt_vals.append(SV.inc(v.tensor_copy(dst, bank[:])))
            emit_ln(v, sc, {'v': SV, 's': SS}, lambda t: vT[:, t * 256:(t + 1) * 256], NT, 256,
                    stats, mvall, negm, sd_, rstd, eps_tile)
            for t in range(NT):
                lb = (t % 2) * 256
                v.tensor_scalar(lnv[:, lb:lb + 256], vT[:, t * 256:(t + 1) * 256],
                                negm[:, t:t + 1], rstd[:, t:t + 1], ALU.add, ALU.mult)
                for gq in range(4):
                    for h in range(8):
                        c0 = t * D + gq * 512 + h * 64
                        i = v.tensor_tensor(xs[:, c0:c0 + 64], xs[:, c0:c0 + 64],
                                            lnv[:, lb + gq * 64:lb + gq * 64 + 64], op=ALU.add)
            x_done = SV.inc(i)
            SV.wait(g, x_done)
            SA['st'].inc(g.dma_start(out=x1_d[:, :], in_=xs[:]), 16)
            SA['st'].wait(g)
            esA.close()
            barrier()

            esC = contextlib.ExitStack()
            sbC = lambda n_, s_, d_: esC.enter_context(nc.sbuf_tensor(n_, s_, d_))
            xtC = sbC("xtC", [128, 2 * D], F32)
            xlntC = sbC("xlntC", [128, 2 * D], BF16)
            xlnTC = sbC("xlnTC", [128, 16 * 1024], BF16)
            hidT = sbC("hidTC", [128, H0C * 1024], BF16)
            w1t = sbC("w1tC", [128, 2 * 2048], BF16)
            w2t = sbC("w2tC", [128, 2 * 2048], BF16)
            w3t = sbC("w3tC", [128, 2 * 2048], BF16)
            sg = sbC("sgC", [128, 2 * 512], F32)
            xoutC = sbC("xoutC", [128, 2 * D], F32)
            xin2C = sbC("xin2C", [128, 2 * D], F32)
            psC = contextlib.ExitStack()
            ptrbC = [psC.enter_context(nc.psum_tensor("ptrbC0", [128, 512], BF16)),
                     psC.enter_context(nc.psum_tensor("ptrbC1", [128, 512], BF16))]
            pf = [psC.enter_context(nc.psum_tensor(f"pfC{i}", [128, 512], F32)) for i in range(4)]
            xlnT3C = xlnTC[:].rearrange("p (a c) -> p a c", c=1024)
            sy.wait_ge(ccw1, 1)
            sy.wait_ge(ccw2, 1)
            sy.wait_ge(ccw3, 1)
            emit_ln_transpose(nc, SC_, v, sy, sc, te, x1_d, xtC, xlntC, xlnT3C, sidb,
                              stats, mvall, negm, sd_, rstd, eps_tile, ptrbC)
            xlnT_doneC = SV.n
            SV.wait(te, xlnT_doneC)
            def g1tile(hc):
                r, hcl = divmod(hc, 6)
                return (G1[r * 128:(r + 1) * 128, hcl * 2048:(hcl + 1) * 2048],
                        G2[r * 128:(r + 1) * 128, hcl * 2048:(hcl + 1) * 2048])
            grp_sg = []
            grp_hid = []
            grp_mm = []
            for gi2 in range(2 * H0C):
                hc, tbk = divmod(gi2, 2)
                if tbk == 0:
                    if hc >= 2:
                        STE.wait(sy, grp_mm[(hc - 2) * 2 + 1][1])
                    s1, s2 = g1tile(hc)
                    SC_['ldw'].inc(sy.dma_start(out=w1t[:, (hc % 2) * 2048:(hc % 2 + 1) * 2048],
                                                in_=s1), 16)
                    SC_['ldw'].inc(sy.dma_start(out=w2t[:, (hc % 2) * 2048:(hc % 2 + 1) * 2048],
                                                in_=s2), 16)
                b1, b2 = pf[(gi2 % 2) * 2], pf[(gi2 % 2) * 2 + 1]
                wof = (hc % 2) * 2048
                SC_['ldw'].wait(te, 32 * (hc + 1))
                if gi2 >= 2:
                    SV.wait(te, grp_hid[gi2 - 2])
                for dc in range(16):
                    i1 = te.matmul(b1[:], w1t[:, wof + dc * 128:wof + (dc + 1) * 128],
                                   xlnT3C[:, dc, tbk * 512:(tbk + 1) * 512],
                                   start=(dc == 0), stop=(dc == 15))
                p1v = STE.inc(i1)
                for dc in range(16):
                    i2 = te.matmul(b2[:], w2t[:, wof + dc * 128:wof + (dc + 1) * 128],
                                   xlnT3C[:, dc, tbk * 512:(tbk + 1) * 512],
                                   start=(dc == 0), stop=(dc == 15))
                p2v = STE.inc(i2)
                grp_mm.append((p1v, p2v))
                STE.wait(sc, p1v)
                if gi2 >= 2:
                    SV.wait(sc, grp_hid[gi2 - 2])
                isg = sc.activation(sg[:, (gi2 % 2) * 512:(gi2 % 2 + 1) * 512], b1[:],
                                    AF.Sigmoid, bias=0.0, scale=1.0)
                grp_sg.append(SS.inc(isg))
                STE.wait(v, p2v)
                SS.wait(v, grp_sg[-1])
                ih = v.scalar_tensor_tensor(
                    hidT[:, hc * 1024 + tbk * 512:hc * 1024 + (tbk + 1) * 512],
                    b2[:], 1.0, sg[:, (gi2 % 2) * 512:(gi2 % 2 + 1) * 512],
                    ALU.mult, ALU.mult)
                grp_hid.append(SV.inc(ih))
            hid_done = SV.n
            psC.close()
            px = [esC.enter_context(nc.psum_tensor(f"pxC{i}", [128, 512], F32)) for i in range(8)]
            SV.wait(te, hid_done)
            pair_apply = []
            w3_cons = []
            for pair in range(4):
                if pair >= 1:
                    SV.wait(sy, pair_apply[pair - 1])
                for i in range(2):
                    t = pair * 2 + i
                    SC_['ldxr'].inc(sy.dma_start(
                        out=xin2C[:, i * D:(i + 1) * D],
                        in_=x1_d[:, t * D:(t + 1) * D]), 16)
                for hc in range(H0C):
                    di = pair * H0C + hc
                    if di >= 2:
                        STE.wait(sy, w3_cons[di - 2])
                    r3, a3 = divmod(hc, 6)
                    SC_['ldw3'].inc(sy.dma_start(out=w3t[:, (di % 2) * 2048:(di % 2 + 1) * 2048],
                                                 in_=G3[r3 * 128:(r3 + 1) * 128,
                                                        a3 * 2048:(a3 + 1) * 2048]), 16)
                    SC_['ldw3'].wait(te, 16 * (di + 1))
                    if hc == 0 and pair > 0:
                        SV.wait(te, pair_apply[pair - 1])
                    for tt in range(2):
                        t = pair * 2 + tt
                        for dg in range(4):
                            iw = te.matmul(px[tt * 4 + dg][:],
                                           hidT[:, hc * 1024 + t * 128:hc * 1024 + (t + 1) * 128],
                                           w3t[:, (di % 2) * 2048 + dg * 512:(di % 2) * 2048 + (dg + 1) * 512],
                                           start=(hc == 0), stop=(hc == H0C - 1))
                    w3_cons.append(STE.inc(iw))
                mm_done = w3_cons[-1]
                STE.wait(v, mm_done)
                STE.wait(sc, mm_done)
                for tt in range(2):
                    t = pair * 2 + tt
                    s0 = t * 24
                    for dg in range(4):
                        ist = v.bn_stats(stats[:, s0 + 6 * dg:s0 + 6 * dg + 6], px[tt * 4 + dg][:])
                        SV.wait(v, SV.inc(ist))
                    ia = v.bn_aggr(mvall[:, 2 * t:2 * t + 2], stats[:, s0:s0 + 24])
                    SV.wait(v, SV.inc(ia))
                    i1 = v.tensor_scalar(negm[:, t:t + 1], mvall[:, 2 * t:2 * t + 1], -1.0, None, ALU.mult)
                    nv = SV.inc(i1)
                    SV.wait(sc, nv)
                    i2 = sc.activation(sd_[:, t:t + 1], mvall[:, 2 * t + 1:2 * t + 2], AF.Sqrt,
                                       bias=eps_tile[:, 0:1], scale=1.0)
                    sv = SS.inc(i2)
                    SS.wait(v, sv)
                    i3 = v.reciprocal(rstd[:, t:t + 1], sd_[:, t:t + 1])
                    SV.wait(v, SV.inc(i3))
                    for dg in range(4):
                        v.tensor_scalar(xoutC[:, tt * D + dg * 512:tt * D + (dg + 1) * 512],
                                        px[tt * 4 + dg][:], negm[:, t:t + 1], rstd[:, t:t + 1],
                                        ALU.add, ALU.mult)
                    SC_['ldxr'].wait(v, 16 * (pair * 2 + tt + 1))
                    iadd = v.tensor_tensor(xoutC[:, tt * D:(tt + 1) * D],
                                           xoutC[:, tt * D:(tt + 1) * D],
                                           xin2C[:, tt * D:(tt + 1) * D],
                                           op=ALU.add)
                    av = SV.inc(iadd)
                    SV.wait(g, av)
                    SC_['st'].inc(g.dma_start(out=x2_d[:, t * D:(t + 1) * D],
                                              in_=xoutC[:, tt * D:(tt + 1) * D]), 16)
                pair_apply.append(SV.n)
                SC_['st'].wait(v)
            SC_['st'].wait(g)
            esC.close()
            barrier()

            esD = contextlib.ExitStack()
            sbD = lambda n_, s_, d_: esD.enter_context(nc.sbuf_tensor(n_, s_, d_))
            xtD = sbD("xtD", [128, 2 * D], F32)
            xlntD = sbD("xlntD", [128, 2 * D], BF16)
            xlnTD = sbD("xlnTD", [128, 16 * 1024], BF16)
            wsb = sbD("wsbD", [128, 2 * 2048], BF16)
            qf = sbD("qfD", [128, 2 * 1024], F32)
            ct = sbD("ctD", [128, 1024], F32)
            st_ = sbD("stD", [128, 1024], F32)
            tt1 = sbD("tt1D", [128, 2 * 1024], F32)
            tt2 = sbD("tt2D", [128, 2 * 1024], F32)
            qsw = sbD("qswD", [128, 2 * 1024], F32)
            sgn = sbD("sgnD", [128, 1], F32)
            qro = sbD("qroD", [128, 16 * 1024], BF16)
            kro = sbD("kroD", [128, 4 * 1024], BF16)
            vwb = sbD("vwbD", [128, 16 * 512], BF16)
            vt = sbD("vtD", [128, 8 * 512], BF16)
            ptrbD = [esD.enter_context(nc.psum_tensor("ptrbD0", [128, 512], BF16)),
                     esD.enter_context(nc.psum_tensor("ptrbD1", [128, 512], BF16))]
            pp = [esD.enter_context(nc.psum_tensor(f"ppD{i}", [128, 512], F32)) for i in range(4)]
            xlnT3D = xlnTD[:].rearrange("p (a c) -> p a c", c=1024)
            v.memset(sgn[0:64, :], -1.0)
            v.memset(sgn[64:128, :], 1.0)
            SD['ld'].inc(sy.dma_start(out=ct[0:64, :], in_=cos_in[:, :]), 16)
            SD['ld'].inc(sy.dma_start(out=ct[64:128, :], in_=cos_in[:, :]), 16)
            SD['ld'].inc(sy.dma_start(out=st_[0:64, :], in_=sin_in[:, :]), 16)
            SD['ld'].inc(sy.dma_start(out=st_[64:128, :], in_=sin_in[:, :]), 16)
            ld_allD = SD['ld'].n
            emit_ln_transpose(nc, SD, v, sy, sc, te, x2_d, xtD, xlntD, xlnT3D, sidb,
                              stats, mvall, negm, sd_, rstd, eps_tile, ptrbD)
            xlnT_doneD = SV.n
            SV.wait(te, xlnT_doneD)
            SD['ld'].wait(v, ld_allD)
            sy.wait_ge(ccL, 1)
            rope_done = []
            evq = []
            mmq = []
            for ci in range(20):
                is_q = ci < 16
                u0 = 4 * ci if is_q else 64 + 4 * (ci - 16)
                dstf = qro if is_q else kro
                dof = (ci if is_q else ci - 16) * 1024
                wb = (ci % 2)
                if ci >= 2:
                    STE.wait(sy, mmq[2 * (ci - 2) + 1])
                SD['ldw'].inc(sy.dma_start(out=wsb[:, wb * 2048:(wb + 1) * 2048],
                                           in_=gw_tile(u0, 4)), 16)
                for tb in range(2):
                    pb = pp[(2 * ci + tb) % 4]
                    if 2 * ci + tb >= 4:
                        SS.wait(te, evq[2 * ci + tb - 4])
                    SD['ldw'].wait(te, 16 * (ci + 1))
                    for dc in range(16):
                        im = te.matmul(pb[:], wsb[:, wb * 2048 + dc * 128:wb * 2048 + (dc + 1) * 128],
                                       xlnT3D[:, dc, tb * 512:(tb + 1) * 512],
                                       start=(dc == 0), stop=(dc == 15))
                    mmq.append(STE.inc(im))
                    STE.wait(sc, mmq[-1])
                    if ci >= 2 and tb == 0:
                        SV.wait(sc, rope_done[ci - 2])
                    qb = (ci % 2) * 1024
                    iev = sc.activation(qf[:, qb + tb * 512:qb + (tb + 1) * 512], pb[:],
                                        AF.Copy, bias=0.0, scale=(QSCALE if is_q else 1.0))
                    evq.append(SS.inc(iev))
                SS.wait(v, evq[-1])
                qb = (ci % 2) * 1024
                b = (ci % 2) * 1024
                v.tensor_copy(qsw[0:64, b:b + 1024], qf[64:128, qb:qb + 1024])
                v.tensor_copy(qsw[64:128, b:b + 1024], qf[0:64, qb:qb + 1024])
                v.tensor_tensor(tt1[:, b:b + 1024], qf[:, qb:qb + 1024], ct[:], op=ALU.mult)
                v.tensor_tensor(tt2[:, b:b + 1024], qsw[:, b:b + 1024], st_[:], op=ALU.mult)
                ir = v.scalar_tensor_tensor(dstf[:, dof:dof + 1024], tt2[:, b:b + 1024],
                                            sgn[:, 0:1], tt1[:, b:b + 1024],
                                            ALU.mult, ALU.add)
                rope_done.append(SV.inc(ir))
            SD['ldw'].inc(sy.dma_start(out=vwb[:], in_=gw_tile(144, 16)), 16)
            ldw_allD = SD['ldw'].n
            SV.wait(te)
            SS.wait(te)
            SD['ldw'].wait(te, ldw_allD)
            evv = []
            for t in range(NT):
                pb = pp[t % 4]
                if t >= 4:
                    SV.wait(te, evv[t - 4])
                for dc in range(16):
                    im = te.matmul(pb[:], xlnT3D[:, dc, t * 128:(t + 1) * 128],
                                   vwb[:, dc * 512:(dc + 1) * 512],
                                   start=(dc == 0), stop=(dc == 15))
                mv_ = STE.inc(im)
                STE.wait(v, mv_)
                iev = v.tensor_copy(vt[:, t * 512:(t + 1) * 512], pb[:])
                evv.append(SV.inc(iev))
            SV.wait(g)
            SD['st'].inc(g.dma_start(out=q_d[:, :], in_=qro[:]), 16)
            SD['st'].inc(g.dma_start(out=kc_d[:, :], in_=kro[:]), 16)
            SD['st'].inc(g.dma_start(out=vc_d.rearrange("(t p) d -> p t d", p=128),
                                     in_=vt[:].rearrange("p (t d) -> p t d", d=512)), 16)
            SD['st'].wait(g)
            groups2 = [[0, 1, 2, 3], [4, 5, 6, 7]]
            g.collective_compute("AllGather", ALU.bypass, replica_groups=groups2,
                                 ins=[kc_d.opt()], outs=[KG.opt()]).then_inc(cck)
            g.collective_compute("AllGather", ALU.bypass, replica_groups=groups2,
                                 ins=[vc_d.opt()], outs=[VG.opt()]).then_inc(ccv)
            esD.close()
            barrier()

            esE = contextlib.ExitStack()
            sbE = lambda n_, s_, d_: esE.enter_context(nc.sbuf_tensor(n_, s_, d_))
            attn_tok = sbE("attokE", [128, NT * 2048], BF16)
            negmb = sbE("negmbE", [128, 16], F32)
            es1E = contextlib.ExitStack()
            sb1E = lambda n_, s_, d_: es1E.enter_context(nc.sbuf_tensor(n_, s_, d_))
            k_sb = sb1E("ksbE", [128, 4 * T], BF16)
            v_sb = sb1E("vsbE", [128, 32 * 512], BF16)
            q_sb = sb1E("qsbE", [128, 16 * 1024], BF16)
            ksq = sb1E("ksqE", [128, 2 * T], BF16)
            P_sb = sb1E("PsbE", [128, 2 * 512], BF16)
            msk = sb1E("mskE", [128, 512], BF16)
            att_s = sb1E("attsE", [128, 512], BF16)
            l_sb = sb1E("lsbE", [1, 512], F32)
            rl_col = sb1E("rlcolE", [128, 4], F32)
            nm_row = sb1E("nmrowE", [1, 40], F32)
            kn = sb1E("knE", [1, 4], F32)
            qn = sb1E("qnE", [1, 16], F32)
            knr = sb1E("knrE", [1, 16], F32)
            nmr = sb1E("nmrE", [1, 16], F32)
            sid32 = sb1E("sid32E", [128, 128], F32)
            ones_cb = sb1E("onescbE", [128, 1], BF16)
            ps1E = contextlib.ExitStack()
            pst = [ps1E.enter_context(nc.psum_tensor(f"pstE{i}", [128, 512], F32)) for i in range(2)]
            ppv = ps1E.enter_context(nc.psum_tensor("ppvE", [128, 512], F32))
            pl = ps1E.enter_context(nc.psum_tensor("plE", [1, 512], F32))
            ptrE = ps1E.enter_context(nc.psum_tensor("ptrE", [128, 512], BF16))
            ptr32E = ps1E.enter_context(nc.psum_tensor("ptr32E", [128, 128], F32))
            pn = ps1E.enter_context(nc.psum_tensor("pnE", [1, 512], F32))
            SE['ld'].inc(sy.dma_start(out=q_sb[:], in_=q_d[:, :]), 16)
            sy.wait_ge(cck, 1)
            for gq in range(4):
                k_sb4 = k_sb[:, gq * T:(gq + 1) * T].rearrange(
                    "p (j k c) -> p j k c", k=4, c=128)
                for kk in range(4):
                    SE['ld'].inc(sy.dma_start(
                        out=k_sb4[:, :, kk, :],
                        in_=KG[kk * 128:(kk + 1) * 128,
                               gq * 1024:(gq + 1) * 1024].rearrange(
                            "p (j c) -> p j c", c=128)), 16)
            sy.wait_ge(ccv, 1)
            v_sb4 = v_sb[:].rearrange("p (j k d) -> p j k d", k=4, d=512)
            for kk in range(4):
                SE['ld'].inc(sy.dma_start(
                    out=v_sb4[:, :, kk, :],
                    in_=VG[kk * 1024:(kk + 1) * 1024, :].rearrange(
                        "(j p) d -> p j d", p=128)), 16)
            SE['ld'].inc(sy.dma_start(out=msk[:], in_=mask_in[:, :]), 16)
            SE['ld'].inc(sy.dma_start(out=sid32[:], in_=ident32_in[:, :]), 16)
            ld_allE = SE['ld'].n
            ocv = SV.inc(v.memset(ones_cb[:], 1.0))
            SV.wait(te, ocv)
            SE['ld'].wait(v, ld_allE)
            SE['ld'].wait(sc, ld_allE)
            SE['ld'].wait(te, ld_allE)
            for gi in range(4):
                if gi >= 2:
                    SV.wait(sc)
                isq = sc.activation(ksq[:, (gi % 2) * T:(gi % 2 + 1) * T],
                                    k_sb[:, gi * T:(gi + 1) * T],
                                    AF.Square, bias=0.0, scale=1.0)
                sqv = SS.inc(isq)
                SS.wait(te, sqv)
                for c in range(8):
                    imm = te.matmul(pn[:], ones_cb[:],
                                    ksq[:, (gi % 2) * T + c * 512:(gi % 2) * T + (c + 1) * 512],
                                    start=True, stop=True)
                    tv = STE.inc(imm)
                    STE.wait(v, tv)
                    ir = v.tensor_reduce(nm_row[:, 8 * (gi % 2) + c:8 * (gi % 2) + c + 1],
                                         pn[0:1, :], axis=AX.X, op=ALU.max)
                    SV.wait(v, SV.inc(ir))
                    SV.wait(te)
                ir = v.tensor_reduce(kn[:, gi:gi + 1], nm_row[:, 8 * (gi % 2):8 * (gi % 2) + 8],
                                     axis=AX.X, op=ALU.max)
                SV.wait(v, SV.inc(ir))
            for h in range(NHEAD):
                isq = sc.activation(ksq[:, (h % 4) * 1024:(h % 4) * 1024 + 1024],
                                    q_sb[:, h * 1024:(h + 1) * 1024], AF.Square,
                                    bias=0.0, scale=1.0)
                hv = SS.inc(isq)
                SS.wait(te, hv)
                for c in range(2):
                    imm = te.matmul(pn[:], ones_cb[:],
                                    ksq[:, (h % 4) * 1024 + c * 512:(h % 4) * 1024 + (c + 1) * 512],
                                    start=True, stop=True)
                    tv = STE.inc(imm)
                    STE.wait(v, tv)
                    ir = v.tensor_reduce(nm_row[:, 32 + c:33 + c], pn[0:1, :], axis=AX.X, op=ALU.max)
                    SV.wait(v, SV.inc(ir))
                    SV.wait(te)
                ir = v.tensor_reduce(qn[:, h:h + 1], nm_row[:, 32:34], axis=AX.X, op=ALU.max)
                SV.wait(v, SV.inc(ir))
                SV.wait(sc)
            vv = SV.n
            SV.wait(sc, vv)
            sc.activation(kn[:], kn[:], AF.Sqrt, bias=eps_tile[0:1, 0:1], scale=1.0)
            iq = sc.activation(qn[:], qn[:], AF.Sqrt, bias=eps_tile[0:1, 0:1], scale=1.0)
            sv = SS.inc(iq)
            SS.wait(v, sv)
            for h in range(NHEAD):
                v.tensor_copy(knr[:, h:h + 1], kn[:, h // 4:h // 4 + 1])
            iv = v.tensor_tensor(nmr[:], qn[:], knr[:], op=ALU.mult)
            SV.inc(iv)
            iv2 = v.tensor_scalar(nmr[:], nmr[:], -1.05, None, ALU.mult)
            nv = SV.inc(iv2)
            SV.wait(g, nv)
            gv = SE['gb'].inc(g.dma_start(out=nb_d[0:1, :], in_=nmr[:]), 16)
            SE['gb'].wait(g, gv)
            gv = SE['gb'].inc(g.dma_start(out=negmb[:],
                                          in_=nb_d[0:1, :].squeeze(0).partition_broadcast(128)), 16)
            SE['gb'].wait(sc, gv)
            SE['gb'].wait(v, gv)
            exp_done = {}
            pv_done = {}
            row_free = []
            prev_pv = None
            for h in range(NHEAD):
                gi = h // 4
                for qd in range(2):
                    nk = 16 * (qd + 1)
                    q_ap = q_sb[:, h * 1024 + qd * 512:h * 1024 + (qd + 1) * 512]
                    for ktc in range(nk):
                        stb = pst[ktc % 2]
                        if ktc >= 2:
                            SS.wait(te, exp_done[(h, qd, ktc - 2)])
                        if (h, qd) != (0, 0) and ktc == 0:
                            SV.wait(te, row_free[-1])
                        iqk = te.matmul(stb[:],
                                        k_sb[:, gi * T + ktc * 128:gi * T + (ktc + 1) * 128],
                                        q_ap, start=True, stop=True)
                        stv = STE.inc(iqk)
                        jls = ktc // 4 - qd * 4
                        slot = (ktc % 2) * 512
                        pfree = pv_done[(h, qd, ktc - 2)] if ktc >= 2 else prev_pv
                        STE.wait(sc, stv)
                        if pfree is not None:
                            STE.wait(sc, pfree)
                        iex = sc.activation(P_sb[:, slot:slot + 512], stb[:],
                                            AF.Exp, bias=negmb[:, h:h + 1], scale=1.0)
                        exp_done[(h, qd, ktc)] = SS.inc(iex)
                        pv_wait_v = None
                        if jls >= 0:
                            tq = ktc % 4
                            SS.wait(v, exp_done[(h, qd, ktc)])
                            if pfree is not None:
                                STE.wait(v, pfree)
                            imm_ = v.tensor_tensor(
                                P_sb[:, slot + jls * 128:slot + (jls + 1) * 128],
                                P_sb[:, slot + jls * 128:slot + (jls + 1) * 128],
                                msk[:, tq * 128:(tq + 1) * 128], op=ALU.mult)
                            pv_wait_v = SV.inc(imm_)
                            if jls > 0:
                                imz = v.memset(P_sb[:, slot:slot + jls * 128], 0.0)
                                pv_wait_v = SV.inc(imz)
                        SS.wait(te, exp_done[(h, qd, ktc)])
                        if pv_wait_v is not None:
                            SV.wait(te, pv_wait_v)
                        first = (ktc == 0)
                        last = (ktc == nk - 1)
                        te.matmul(ppv[:], v_sb[:, ktc * 512 + gi * 128:ktc * 512 + (gi + 1) * 128],
                                  P_sb[:, slot:slot + 512], start=first, stop=last)
                        ipl = te.matmul(pl[:], ones_cb[:], P_sb[:, slot:slot + 512],
                                        start=first, stop=last)
                        pv_done[(h, qd, ktc)] = STE.inc(ipl)
                    prev_pv = pv_done[(h, qd, nk - 1)]
                    rowv = prev_pv
                    STE.wait(v, rowv)
                    il = v.tensor_scalar(l_sb[:], pl[0:1, :], 1e-30, None, ALU.add)
                    SV.wait(v, SV.inc(il))
                    il2 = v.reciprocal(l_sb[:], l_sb[:])
                    rv = SV.inc(il2)
                    SV.wait(te, rv)
                    for jj in range(4):
                        itr = te.transpose(ptr32E[:, jj:jj + 1],
                                           l_sb[0:1, jj * 128:(jj + 1) * 128],
                                           sid32[0:1, 0:1])
                    t1 = STE.inc(itr)
                    STE.wait(v, t1)
                    ir1 = v.tensor_copy(rl_col[:, 0:4], ptr32E[:, 0:4])
                    ie1 = v.tensor_copy(att_s[:], ppv[:])
                    ev = SV.inc(ie1)
                    SV.wait(v, ev)
                    SV.wait(te, ev)
                    for jj in range(4):
                        itr2 = te.transpose(ptrE[:, jj * 128:(jj + 1) * 128],
                                            att_s[:, jj * 128:(jj + 1) * 128], sidb[:])
                    t2 = STE.inc(itr2)
                    STE.wait(v, t2)
                    for jj in range(4):
                        j = qd * 4 + jj
                        ifin = v.tensor_scalar(
                            attn_tok[:, j * 2048 + h * 128:j * 2048 + (h + 1) * 128],
                            ptrE[:, jj * 128:(jj + 1) * 128], rl_col[:, jj:jj + 1],
                            None, ALU.mult)
                    row_free.append(SV.inc(ifin))
            att_done = SV.n
            ps1E.close()
            es1E.close()
            attn_T = esE.enter_context(nc.sbuf_tensor("attTE", [128, 16 * 1024], BF16))
            c_wb = esE.enter_context(nc.sbuf_tensor("cwbE", [128, 16 * 2048], BF16))
            xoutE = esE.enter_context(nc.sbuf_tensor("xoutE", [128, 2048], F32))
            xin2E = esE.enter_context(nc.sbuf_tensor("xin2E", [128, 2048], F32))
            ptrb2 = [esE.enter_context(nc.psum_tensor(f"ptrb2E{i}", [128, 512], BF16)) for i in range(2)]
            pog = [esE.enter_context(nc.psum_tensor(f"pogE{i}", [128, 512], F32)) for i in range(4)]
            attn_T3 = attn_T[:].rearrange("p (a c) -> p a c", c=1024)
            SV.wait(sy, att_done)
            SE['ldc'].inc(sy.dma_start(out=c_wb[:, 0:9 * 2048],
                                       in_=GW[0:128, 80 * 512:116 * 512]), 16)
            SE['ldc'].inc(sy.dma_start(out=c_wb[:, 9 * 2048:16 * 2048],
                                       in_=GW[128:256, 0:28 * 512]), 16)
            ldc_all = SE['ldc'].n
            SV.wait(te, att_done)
            evac_doneE = []
            for t in range(NT):
                for q in range(4):
                    gidx = t * 4 + q
                    bank = ptrb2[gidx % 2]
                    if gidx >= 2:
                        SV.wait(te, evac_doneE[gidx - 2])
                    for i in range(4):
                        dc = q * 4 + i
                        ins = te.transpose(bank[:, i * 128:(i + 1) * 128],
                                           attn_tok[:, t * 2048 + dc * 128:t * 2048 + (dc + 1) * 128],
                                           sidb[:])
                    tv = STE.inc(ins)
                    STE.wait(v, tv)
                    dst = attn_T3[:, 4 * q:4 * q + 4, t * 128:(t + 1) * 128]
                    evac_doneE.append(SV.inc(v.tensor_copy(dst, bank[:])))
            aT_done = SV.n
            SV.wait(te, aT_done)
            SE['ldc'].wait(te, ldc_all)
            prev_apply = None
            for t in range(NT):
                if prev_apply is not None:
                    SV.wait(sy, prev_apply)
                SE['ldxr'].inc(sy.dma_start(out=xin2E[:], in_=x2_d[:, t * D:(t + 1) * D]), 16)
                if prev_apply is not None:
                    SV.wait(te, prev_apply)
                for fc in range(16):
                    for og in range(4):
                        imm = te.matmul(pog[og][:], attn_T3[:, fc, t * 128:(t + 1) * 128],
                                        c_wb[:, fc * 2048 + og * 512:fc * 2048 + (og + 1) * 512],
                                        start=(fc == 0), stop=(fc == 15))
                mmv = STE.inc(imm)
                STE.wait(v, mmv)
                STE.wait(sc, mmv)
                s0 = t * 24
                for og in range(4):
                    ist = v.bn_stats(stats[:, s0 + 6 * og:s0 + 6 * og + 6], pog[og][:])
                    SV.wait(v, SV.inc(ist))
                ia = v.bn_aggr(mvall[:, 2 * t:2 * t + 2], stats[:, s0:s0 + 24])
                SV.wait(v, SV.inc(ia))
                i1 = v.tensor_scalar(negm[:, t:t + 1], mvall[:, 2 * t:2 * t + 1], -1.0, None, ALU.mult)
                nv2 = SV.inc(i1)
                SV.wait(sc, nv2)
                i2 = sc.activation(sd_[:, t:t + 1], mvall[:, 2 * t + 1:2 * t + 2], AF.Sqrt,
                                   bias=eps_tile[:, 0:1], scale=1.0)
                sv2 = SS.inc(i2)
                SS.wait(v, sv2)
                i3 = v.reciprocal(rstd[:, t:t + 1], sd_[:, t:t + 1])
                SV.wait(v, SV.inc(i3))
                for og in range(4):
                    v.tensor_scalar(xoutE[:, og * 512:(og + 1) * 512], pog[og][:],
                                    negm[:, t:t + 1], rstd[:, t:t + 1], ALU.add, ALU.mult)
                SE['ldxr'].wait(v, 16 * (t + 1))
                iadd = v.tensor_tensor(xoutE[:], xoutE[:], xin2E[:], op=ALU.add)
                av = SV.inc(iadd)
                prev_apply = av
                SV.wait(g, av)
                SE['st'].inc(g.dma_start(out=x3_d[:, t * D:(t + 1) * D], in_=xoutE[:]), 16)
                SE['st'].wait(v)
            SE['st'].wait(g)
            esE.close()
            barrier()

            esF = contextlib.ExitStack()
            sbF = lambda n_, s_, d_: esF.enter_context(nc.sbuf_tensor(n_, s_, d_))
            xlnTF = sbF("xlnTF", [128, 16 * 1024], BF16)
            hs = sbF("hsF", [128, 64 * 1024], BF16)
            es0F = contextlib.ExitStack()
            xtF = es0F.enter_context(nc.sbuf_tensor("xtF", [128, 2 * D], F32))
            xlntF = es0F.enter_context(nc.sbuf_tensor("xlntF", [128, 2 * D], BF16))
            ptrbF = [es0F.enter_context(nc.psum_tensor(f"ptrbF{i}", [128, 512], BF16)) for i in range(2)]
            es1F = contextlib.ExitStack()
            xlnT3F = xlnTF[:].rearrange("p (a c) -> p a c", c=1024)
            emit_ln_transpose(nc, SF, v, sy, sc, te, x3_d, xtF, xlntF, xlnT3F, sidb,
                              stats, mvall, negm, sd_, rstd, eps_tile, ptrbF)
            xlnT_doneF = SV.n
            SV.wait(te, xlnT_doneF)
            es0F.close()
            sb1F = lambda n_, s_, d_: es1F.enter_context(nc.sbuf_tensor(n_, s_, d_))
            wbtF = sb1F("wbtF", [128, 2 * 512], BF16)
            gch = sb1F("gchF", [128, 8 * 512], F32)
            ht = sb1F("htF", [128, 8 * 512], BF16)
            at = sb1F("atF", [128, 8 * 512], BF16)
            cs_ = sb1F("csF", [128, 6 * 512], BF16)
            ts_ = sb1F("tsF", [128, 2 * 512], BF16)
            ppF = [esF.enter_context(nc.psum_tensor(f"ppF{i}", [128, 512], F32)) for i in range(8)]
            SV.wait(sy, xlnT_doneF)
            wcons = []
            ev_doneF = []
            sp_done = []
            for gq in range(16):
                for sweep, ubase in ((0, 160), (1, 416)):
                    for dc in range(16):
                        di = len(wcons)
                        if di >= 2:
                            STE.wait(sy, wcons[di - 2])
                        SF['ldw'].inc(sy.dma_start(
                            out=wbtF[:, (di % 2) * 512:(di % 2 + 1) * 512],
                            in_=gw_tile(ubase + gq * 16 + dc, 1)), 16)
                        SF['ldw'].wait(te, 16 * (di + 1))
                        if dc == 0 and ev_doneF:
                            SV.wait(te, ev_doneF[-1][0])
                            SS.wait(te, ev_doneF[-1][1])
                        for tb in range(2):
                            for sc_ in range(4):
                                imm = te.matmul(
                                    ppF[tb * 4 + sc_][:],
                                    wbtF[:, (di % 2) * 512 + sc_ * 128:(di % 2) * 512 + (sc_ + 1) * 128],
                                    xlnT3F[:, dc, tb * 512:(tb + 1) * 512],
                                    start=(dc == 0), stop=(dc == 15))
                        wcons.append(STE.inc(imm))
                    mmv = wcons[-1]
                    STE.wait(sc, mmv)
                    STE.wait(v, mmv)
                    if sweep == 0:
                        for i8 in range(8):
                            if gq > 0 and i8 == 0:
                                SV.wait(sc, sp_done[-1])
                            isl = sc.activation(gch[:, i8 * 512:(i8 + 1) * 512], ppF[i8][:],
                                                AF.Silu, bias=0.0, scale=1.0)
                        slv = SS.inc(isl)
                        ev_doneF.append((SV.n, slv))
                    else:
                        SS.wait(v)
                        for i8 in range(8):
                            iht = v.scalar_tensor_tensor(ht[:, i8 * 512:(i8 + 1) * 512],
                                                         ppF[i8][:], 1.0, gch[:, i8 * 512:(i8 + 1) * 512],
                                                         ALU.mult, ALU.mult)
                        hv = SV.inc(iht)
                        SV.wait(sc, hv)
                        for i8 in range(8):
                            iab = sc.activation(at[:, i8 * 512:(i8 + 1) * 512], ht[:, i8 * 512:(i8 + 1) * 512],
                                                AF.Abs, bias=0.0, scale=1.0)
                        abv = SS.inc(iab)
                        SS.wait(v, abv)
                        ev_doneF.append((SV.n, abv))
                        for tb in range(2):
                            a_ = [at[:, (tb * 4 + p) * 512:(tb * 4 + p + 1) * 512] for p in range(4)]
                            h_ = [ht[:, (tb * 4 + p) * 512:(tb * 4 + p + 1) * 512] for p in range(4)]
                            c_ = [cs_[:, i * 512:(i + 1) * 512] for i in range(6)]
                            t_ = [ts_[:, i * 512:(i + 1) * 512] for i in range(2)]
                            pairs = [(0, 1), (0, 2), (0, 3), (1, 2), (1, 3), (2, 3)]
                            for ci, (a, b) in enumerate(pairs):
                                v.tensor_tensor(c_[ci], a_[a], a_[b], op=ALU.is_ge)
                            def dstF(p):
                                return hs[:, ((p * 16 + gq) * 1024 + tb * 512):((p * 16 + gq) * 1024 + (tb + 1) * 512)]
                            v.tensor_tensor(t_[0], c_[0], c_[1], op=ALU.add)
                            v.tensor_tensor(t_[1], t_[0], c_[2], op=ALU.add)
                            v.scalar_tensor_tensor(dstF(0), t_[1], 2.0, h_[0], ALU.is_ge, ALU.mult)
                            v.tensor_tensor(t_[0], c_[3], c_[4], op=ALU.add)
                            v.tensor_tensor(t_[1], t_[0], c_[0], op=ALU.subtract)
                            v.scalar_tensor_tensor(dstF(1), t_[1], 1.0, h_[1], ALU.is_ge, ALU.mult)
                            v.tensor_tensor(t_[0], c_[5], c_[1], op=ALU.subtract)
                            v.tensor_tensor(t_[1], t_[0], c_[3], op=ALU.subtract)
                            v.scalar_tensor_tensor(dstF(2), t_[1], 0.0, h_[2], ALU.is_ge, ALU.mult)
                            v.tensor_tensor(t_[0], c_[2], c_[4], op=ALU.add)
                            v.tensor_tensor(t_[1], t_[0], c_[5], op=ALU.add)
                            isp = v.scalar_tensor_tensor(dstF(3), t_[1], 1.0, h_[3], ALU.is_le, ALU.mult)
                        sp_done.append(SV.inc(isp))
            hs_done = SV.n
            es1F.close()
            wdb = esF.enter_context(nc.sbuf_tensor("wdbF", [128, 2 * D], BF16))
            xoutF = esF.enter_context(nc.sbuf_tensor("xoutF", [128, D], F32))
            xin2F = esF.enter_context(nc.sbuf_tensor("xin2F", [128, D], F32))
            SV.wait(te, hs_done)
            SV.wait(sy, hs_done)
            pair_applyF = []
            wd_cons = []
            for pair in range(4):
                if pair >= 1:
                    SV.wait(sy, pair_applyF[-1])
                for ic in range(64):
                    di = pair * 64 + ic
                    if di >= 2:
                        STE.wait(sy, wd_cons[di - 2])
                    SF['ldwd'].inc(sy.dma_start(out=wdb[:, (di % 2) * D:(di % 2 + 1) * D],
                                                in_=gw_tile(672 + 4 * ic, 4)), 16)
                    SF['ldwd'].wait(te, 16 * (di + 1))
                    if ic == 0 and pair > 0:
                        SV.wait(te, pair_applyF[-1])
                    for tt in range(2):
                        t = pair * 2 + tt
                        for dg in range(4):
                            iw = te.matmul(ppF[tt * 4 + dg][:],
                                           hs[:, ic * 1024 + t * 128:ic * 1024 + (t + 1) * 128],
                                           wdb[:, (di % 2) * D + dg * 512:(di % 2) * D + (dg + 1) * 512],
                                           start=(ic == 0), stop=(ic == 63))
                    wd_cons.append(STE.inc(iw))
                mm_done = wd_cons[-1]
                STE.wait(v, mm_done)
                STE.wait(sc, mm_done)
                for tt in range(2):
                    t = pair * 2 + tt
                    SV.wait(sy)
                    SF['ldxr'].inc(sy.dma_start(out=xin2F[:], in_=x3_d[:, t * D:(t + 1) * D]), 16)
                    s0 = t * 24
                    for dg in range(4):
                        ist = v.bn_stats(stats[:, s0 + 6 * dg:s0 + 6 * dg + 6], ppF[tt * 4 + dg][:])
                        SV.wait(v, SV.inc(ist))
                    ia = v.bn_aggr(mvall[:, 2 * t:2 * t + 2], stats[:, s0:s0 + 24])
                    SV.wait(v, SV.inc(ia))
                    i1 = v.tensor_scalar(negm[:, t:t + 1], mvall[:, 2 * t:2 * t + 1], -1.0, None, ALU.mult)
                    nv = SV.inc(i1)
                    SV.wait(sc, nv)
                    i2 = sc.activation(sd_[:, t:t + 1], mvall[:, 2 * t + 1:2 * t + 2], AF.Sqrt,
                                       bias=eps_tile[:, 0:1], scale=1.0)
                    sv = SS.inc(i2)
                    SS.wait(v, sv)
                    i3 = v.reciprocal(rstd[:, t:t + 1], sd_[:, t:t + 1])
                    SV.wait(v, SV.inc(i3))
                    SF['st'].wait(v)
                    for dg in range(4):
                        v.tensor_scalar(xoutF[:, dg * 512:(dg + 1) * 512],
                                        ppF[tt * 4 + dg][:], negm[:, t:t + 1], rstd[:, t:t + 1],
                                        ALU.add, ALU.mult)
                    SF['ldxr'].wait(v)
                    iadd = v.tensor_tensor(xoutF[:], xoutF[:], xin2F[:], op=ALU.add)
                    av = SV.inc(iadd)
                    SV.wait(g, av)
                    SF['st'].inc(g.dma_start(out=xo_out[:, t * D:(t + 1) * D], in_=xoutF[:]), 16)
                pair_applyF.append(SV.n)
            SF['st'].wait(g)
            esF.close()
    return nc


_ROPE_PERM1 = np.concatenate([np.arange(0, 128, 2), np.arange(1, 128, 2)])

def _img_lhsT(wT, nout, dtype):
    out = np.empty((128, nout * 2048), dtype)
    w4 = np.asarray(wT).reshape(16, 128, nout, 128)
    for oc in range(nout):
        out[:, oc * 2048:(oc + 1) * 2048] = w4[:, :, oc, :].transpose(1, 0, 2).reshape(128, 2048)
    return np.ascontiguousarray(out)

def _prep_all(inputs):
    x = np.asarray(inputs["L0_attn_x"], np.float32).reshape(2 * T, D)
    Wkv = np.asarray(inputs["L0_attn_Wkv"], np.float32)
    bkv = np.asarray(inputs["L0_attn_bkv"], np.float32)
    vcols = np.concatenate([np.arange(g * 128 + 64, g * 128 + 128) for g in range(4)])
    wkvv = wimg(np.ascontiguousarray(Wkv[:, vcols]))
    bkvv = np.ascontiguousarray(bkv[vcols]).reshape(1, 256)
    identb = np.eye(128, dtype=BF)
    ident32 = np.eye(128, dtype=np.float32)
    w1T = np.asarray(inputs["L0_ffn_w1"], np.float32).T
    w2T = np.asarray(inputs["L0_ffn_w2"], np.float32).T
    w3T = np.asarray(inputs["L0_ffn_w3"], np.float32).T
    w1Tp = np.zeros((D, H0P), np.float32); w1Tp[:, :H0] = w1T
    w2Tp = np.zeros((D, H0P), np.float32); w2Tp[:, :H0] = w2T
    w3Tp = np.zeros((H0P, D), np.float32); w3Tp[:H0] = w3T
    w1cs, w2cs, w3rs = [], [], []
    for c in range(NCORES):
        w1cs.append(np.ascontiguousarray(
            w1Tp[:, c * 768:(c + 1) * 768].reshape(16, 128, 6, 128)
            .transpose(1, 2, 0, 3).reshape(128, 12288)))
        w2cs.append(np.ascontiguousarray(
            w2Tp[:, c * 768:(c + 1) * 768].reshape(16, 128, 6, 128)
            .transpose(1, 2, 0, 3).reshape(128, 12288)))
        w3rs.append(np.ascontiguousarray(
            w3Tp[c * 768:(c + 1) * 768].reshape(6, 128, D)
            .transpose(1, 0, 2).reshape(128, 12288)))
    q_w = np.asarray(inputs["L1_attn_q_w"], np.float32)
    kv_w = np.asarray(inputs["L1_attn_kv_w"], np.float32)
    qperm = np.concatenate([h * 128 + _ROPE_PERM1 for h in range(16)])
    kperm = np.concatenate([g * 128 + _ROPE_PERM1 for g in range(4)])
    qwT = np.ascontiguousarray(q_w[qperm].T)
    kwT = np.ascontiguousarray(kv_w[0:512][kperm].T)
    vwT = np.ascontiguousarray(kv_w[512:1024].T)
    cwT = np.ascontiguousarray(np.asarray(inputs["L1_attn_c_w"], np.float32).T)
    pairs = np.arange(64, dtype=np.float32)
    freqs = 1.0 / 10000.0 ** (2.0 * pairs / 128.0)
    pos = np.arange(T, dtype=np.float32)
    ang = pos[:, None] * freqs[None, :]
    cosf = np.cos(ang).astype(np.float32)
    sinf = np.sin(ang).astype(np.float32)
    jj = np.arange(HI)
    gq = jj // 512; rem = jj % 512; sc = rem // 128; cc = rem % 128
    perm_cols = 4 * (gq * 128 + cc) + sc
    ic = jj // 128; c2 = jj % 128
    perm_rows = 4 * ((ic % 16) * 128 + c2) + ic // 16
    wg_p = np.asarray(inputs["L1_ffn_w_gate"], np.float32)[:, perm_cols]
    wu_p = np.asarray(inputs["L1_ffn_w_up"], np.float32)[:, perm_cols]
    wd_p = np.ascontiguousarray(np.asarray(inputs["L1_ffn_w_down"], np.float32)[perm_rows, :])
    def wimg_f(wp):
        out = np.empty((256 * 128, 512), np.float32)
        for g in range(16):
            for dc in range(16):
                out[(g * 16 + dc) * 128:(g * 16 + dc + 1) * 128, :] = \
                    wp[dc * 128:(dc + 1) * 128, g * 512:(g + 1) * 512]
        return np.ascontiguousarray(out)
    wg_i = wimg_f(wg_p)
    wu_i = wimg_f(wu_p)
    def rows_img(mat, nrow, w):
        out = np.empty((128, nrow * w), np.float32)
        for t_ in range(nrow):
            out[:, t_ * w:(t_ + 1) * w] = mat[t_ * 128:(t_ + 1) * 128, :]
        return out
    LW = np.concatenate([
        _img_lhsT(qwT, 16, np.float32),
        _img_lhsT(kwT, 4, np.float32),
        rows_img(cwT, 16, 2048),
        rows_img(vwT, 16, 512),
        rows_img(wg_i, 256, 512),
        rows_img(wu_i, 256, 512),
        rows_img(wd_p, 64, 2048),
    ], axis=1)
    lw_sl = [np.ascontiguousarray(LW[:, c * 116 * 512:(c + 1) * 116 * 512])
             for c in range(NCORES)]
    prep = {
        "x": x, "wkvv": wkvv, "bkvv": bkvv, "identb": identb, "ident32": ident32,
        "w1cs": w1cs, "w2cs": w2cs, "w3rs": w3rs,
        "lw": lw_sl, "cosf": cosf, "sinf": sinf,
    }
    return prep

def _core_mask(c):
    kq = c % 4
    mask = np.ones((128, 512), np.float32)
    ii = np.arange(128)
    for tq in range(4):
        mm = mask[:, tq * 128:(tq + 1) * 128]
        if tq == kq:
            mm[ii[:, None] > ii[None, :]] = 0.0
        elif tq > kq:
            mm[:] = 0.0
    return mask.astype(BF)

_NC_CACHE = {}

def kernel(**inputs):
    P = _prep_all(inputs)
    maps = []
    for c in range(NCORES):
        rows = core_rows(c) % T
        maps.append({
            "x_img": img_tokens(P["x"], c),
            "wkvv": P["wkvv"], "bkvv": P["bkvv"],
            "identb": P["identb"], "ident32": P["ident32"],
            "w1cs": P["w1cs"][c], "w2cs": P["w2cs"][c], "w3rs": P["w3rs"][c],
            "lw": P["lw"][c],
            "cosd": np.ascontiguousarray(P["cosf"][rows].T),
            "sind": np.ascontiguousarray(P["sinf"][rows].T),
            "mask": _core_mask(c),
        })
    if "nc" not in _NC_CACHE:
        nc = bass.Bass("TRN2", target_bir_lowering=False)
        build_mega(nc)
        _NC_CACHE["nc"] = nc
    res = run_bass_kernel_spmd(_NC_CACHE["nc"], maps, core_ids=list(range(NCORES)))
    out = np.empty((2 * T, D), np.float32)
    for c in range(NCORES):
        out[core_rows(c)] = unimg_tokens(res.results[c]["xo"], D)
    return out.reshape(2, T, D)


# version 11
# speedup vs baseline: 1.1846x; 1.1846x over previous
import numpy as np
import ml_dtypes
import contextlib
import concourse.bass as bass
import concourse.mybir as mybir
from concourse.bass_utils import run_bass_kernel_spmd

F32 = mybir.dt.float32
BF16 = mybir.dt.bfloat16
I32 = mybir.dt.int32
AF = mybir.ActivationFunctionType
ALU = mybir.AluOpType
AX = mybir.AxisListType
BF = ml_dtypes.bfloat16

D = 2048
T = 4096
NCORES = 8
NT = 8
H0 = 5632
H0C = 44
H0P = 6144
HI = 8192
EPS_LN = 1e-5
DK = 128
NHEAD = 16
NKV = 4
QSCALE = 1.0 / np.sqrt(128.0)


def core_rows(c):
    s, k = divmod(c, 4)
    return np.concatenate([np.arange(s * T + (4 * j + k) * 128,
                                     s * T + (4 * j + k) * 128 + 128) for j in range(8)])

def img_tokens(x2d, c):
    r = core_rows(c)
    w = x2d.shape[1]
    out = np.empty((128, NT * w), x2d.dtype)
    for t in range(NT):
        out[:, t * w:(t + 1) * w] = x2d[r[t * 128:(t + 1) * 128]]
    return np.ascontiguousarray(out)

def unimg_tokens(img, w):
    out = np.empty((NT * 128, w), img.dtype)
    for t in range(NT):
        out[t * 128:(t + 1) * 128] = img[:, t * w:(t + 1) * w]
    return out

def wimg(wT):
    K, N = wT.shape
    nk = (K + 127) // 128
    out = np.zeros((128, nk * N), wT.dtype)
    for a in range(nk):
        rows = wT[a * 128:(a + 1) * 128]
        out[:rows.shape[0], a * N:(a + 1) * N] = rows
    return np.ascontiguousarray(out)


class Sem:
    def __init__(self, es, nc, name):
        self.h = es.enter_context(nc.semaphore(name))
        self.n = 0
    def inc(self, instr, k=1):
        instr.then_inc(self.h, k)
        self.n += k
        return self.n
    def wait(self, eng, val=None):
        eng.wait_ge(self.h, self.n if val is None else val)

def emit_ln(v, sc, S, tile_fn, n_tiles, width, stats, mvall, negm, sd, rstd, eps_tile):
    nchunk = (width + 511) // 512
    for t in range(n_tiles):
        xv = tile_fn(t)
        s0 = t * nchunk * 6
        for ch in range(nchunk):
            ist = v.bn_stats(stats[:, s0 + 6 * ch:s0 + 6 * ch + 6], xv[:, ch * 512:min(width, (ch + 1) * 512)])
            S['v'].wait(v, S['v'].inc(ist))
        ia = v.bn_aggr(mvall[:, 2 * t:2 * t + 2], stats[:, s0:s0 + 6 * nchunk])
        S['v'].wait(v, S['v'].inc(ia))
    for t in range(n_tiles):
        i1 = v.tensor_scalar(negm[:, t:t + 1], mvall[:, 2 * t:2 * t + 1], -1.0, None, ALU.mult)
    val = S['v'].inc(i1)
    S['v'].wait(sc, val)
    for t in range(n_tiles):
        i2 = sc.activation(sd[:, t:t + 1], mvall[:, 2 * t + 1:2 * t + 2], AF.Sqrt,
                           bias=eps_tile[:, 0:1], scale=1.0)
    sval = S['s'].inc(i2)
    S['s'].wait(v, sval)
    i3 = v.reciprocal(rstd[:, 0:n_tiles], sd[:, 0:n_tiles])
    rv = S['v'].inc(i3)
    S['v'].wait(v, rv)
    return rv

def emit_ln_transpose(nc, S, v, sy, sc, te, x_in, xt, xlnt, xlnT3, sidb,
                      stats, mvall, negm, sd, rstd, eps_tile, ptrb):
    ALUl = ALU
    apply_done = []
    evac_done = []
    base_ldx = S['ldx'].n
    for t in range(2):
        S['ldx'].inc(sy.dma_start(out=xt[:, (t % 2) * D:(t % 2 + 1) * D],
                                  in_=x_in[:, t * D:(t + 1) * D]), 16)
    for t in range(NT):
        tb = (t % 2) * D
        S['ldx'].wait(v, base_ldx + 16 * (t + 1))
        emit_ln(v, sc, S, lambda _t: xt[:, tb:tb + D], 1, D,
                stats[:, t * 24:], mvall[:, 2 * t:], negm[:, t:], sd[:, t:],
                rstd[:, t:], eps_tile)
        ia = v.tensor_scalar(xlnt[:, tb:tb + D], xt[:, tb:tb + D],
                             negm[:, t:t + 1], rstd[:, t:t + 1], ALUl.add, ALUl.mult)
        apply_done.append(S['v'].inc(ia))
        if t + 2 < NT:
            S['v'].wait(sy, apply_done[t])
            S['ldx'].inc(sy.dma_start(out=xt[:, ((t + 2) % 2) * D:((t + 2) % 2 + 1) * D],
                                      in_=x_in[:, (t + 2) * D:(t + 3) * D]), 16)
        for q in range(4):
            gi = t * 4 + q
            bank = ptrb[gi % 2]
            S['v'].wait(te, apply_done[-1])
            if gi >= 2:
                S['v'].wait(te, evac_done[gi - 2])
            for i in range(4):
                dc = q * 4 + i
                ins = te.transpose(bank[:, i * 128:(i + 1) * 128],
                                   xlnt[:, tb + dc * 128:tb + (dc + 1) * 128], sidb[:])
            tv = S['te'].inc(ins)
            S['te'].wait(v, tv)
            dst = xlnT3[:, 4 * q:4 * q + 4, t * 128:(t + 1) * 128]
            evac_done.append(S['v'].inc(v.tensor_copy(dst, bank[:])))
    return S['v'].n


def build_mega(nc):
    x_in = nc.dram_tensor("x_img", [128, NT * D], F32, kind="ExternalInput").ap()
    wkvv_in = nc.dram_tensor("wkvv", [128, 16 * 256], F32, kind="ExternalInput").ap()
    bkvv_in = nc.dram_tensor("bkvv", [1, 256], F32, kind="ExternalInput").ap()
    identb_in = nc.dram_tensor("identb", [128, 128], BF16, kind="ExternalInput").ap()
    ident32_in = nc.dram_tensor("ident32", [128, 128], F32, kind="ExternalInput").ap()
    w1cs_in = nc.dram_tensor("w1cs", [128, 12288], F32, kind="ExternalInput").ap()
    w2cs_in = nc.dram_tensor("w2cs", [128, 12288], F32, kind="ExternalInput").ap()
    w3rs_in = nc.dram_tensor("w3rs", [128, 12288], F32, kind="ExternalInput").ap()
    lw_in = nc.dram_tensor("lw", [128, 116 * 512], F32, kind="ExternalInput").ap()
    cos_in = nc.dram_tensor("cosd", [64, 1024], F32, kind="ExternalInput").ap()
    sin_in = nc.dram_tensor("sind", [64, 1024], F32, kind="ExternalInput").ap()
    mask_in = nc.dram_tensor("mask", [128, 512], BF16, kind="ExternalInput").ap()
    xo_out = nc.dram_tensor("xo", [128, NT * D], F32, kind="ExternalOutput").ap()
    pabs_d = nc.dram_tensor("pabs_d", [1, 8], F32, kind="Internal").ap()
    pall_d = nc.dram_tensor("pall_d", [1, 8], F32, kind="Internal",
                            addr_space="Shared").ap()
    gam_d = nc.dram_tensor("gam_d", [1, 4], F32, kind="Internal").ap()
    q1c_d = nc.dram_tensor("q1c_d", [128, 12288], BF16, kind="Internal").ap()
    q2c_d = nc.dram_tensor("q2c_d", [128, 12288], BF16, kind="Internal").ap()
    q3c_d = nc.dram_tensor("q3c_d", [128, 12288], BF16, kind="Internal").ap()
    G1 = nc.dram_tensor("G1", [1024, 12288], BF16, kind="Internal",
                        addr_space="Shared").ap()
    G2 = nc.dram_tensor("G2", [1024, 12288], BF16, kind="Internal",
                        addr_space="Shared").ap()
    G3 = nc.dram_tensor("G3", [1024, 12288], BF16, kind="Internal",
                        addr_space="Shared").ap()
    x1_d = nc.dram_tensor("x1_d", [128, NT * D], F32, kind="Internal").ap()
    x2_d = nc.dram_tensor("x2_d", [128, NT * D], F32, kind="Internal").ap()
    x3_d = nc.dram_tensor("x3_d", [128, NT * D], F32, kind="Internal").ap()
    q_d = nc.dram_tensor("q_d", [128, 16 * 1024], BF16, kind="Internal").ap()
    kc_d = nc.dram_tensor("kc_d", [128, 4096], BF16, kind="Internal").ap()
    vc_d = nc.dram_tensor("vc_d", [1024, 512], BF16, kind="Internal").ap()
    KG = nc.dram_tensor("KG", [512, 4096], BF16, kind="Internal").ap()
    VG = nc.dram_tensor("VG", [4096, 512], BF16, kind="Internal").ap()
    nb_d = nc.dram_tensor("nb_d", [1, 16], F32, kind="Internal").ap()
    lwc_d = nc.dram_tensor("lwc_d", [128, 116 * 512], BF16, kind="Internal").ap()
    GW = nc.dram_tensor("GW", [1024, 116 * 512], BF16, kind="Internal",
                        addr_space="Shared").ap()

    with contextlib.ExitStack() as es:
        sb = lambda n_, s_, d_: es.enter_context(nc.sbuf_tensor(n_, s_, d_))
        sidb = sb("sidb", [128, 128], BF16)
        eps_tile = sb("eps_tile", [128, 1], F32)
        bar_t = sb("bar_t", [1, 1], F32)
        stats = sb("stats", [128, NT * 24], F32)
        mvall = sb("mvall", [128, 16], F32)
        negm = sb("negm", [128, 8], F32)
        sd_ = sb("sd_", [128, 8], F32)
        rstd = sb("rstd", [128, 8], F32)
        SV = Sem(es, nc, "sem_v")
        SS = Sem(es, nc, "sem_s")
        STE = Sem(es, nc, "sem_te")
        SGE = Sem(es, nc, "sem_g")
        ENG = {'v': SV, 's': SS, 'te': STE, 'g': SGE}
        mk = lambda pref, keys: {**{k: Sem(es, nc, f"sem_{pref}_{k}") for k in keys}, **ENG}
        SP = mk("p", ['ld'])
        SW = mk("w", ['ld', 'st'])
        SQ = mk("q", ['ld', 'st'])
        SA = mk("a", ['ld', 'st'])
        SC_ = mk("c", ['ldx', 'ldw', 'ldw3', 'ldxr', 'st'])
        SD = mk("d", ['ldx', 'ld', 'ldw', 'st'])
        SE = mk("e", ['ld', 'ldc', 'ldxr', 'st', 'gb'])
        SF = mk("f", ['ldx', 'ldw', 'ldwd', 'ldxr', 'st'])
        ALL_SEMS = []
        for dct in (SP, SW, SQ, SA, SC_, SD, SE, SF):
            for k, s_ in dct.items():
                if s_ not in ALL_SEMS:
                    ALL_SEMS.append(s_)
        ccg = es.enter_context(nc.semaphore("ccg"))
        ccw1 = es.enter_context(nc.semaphore("ccw1"))
        ccw2 = es.enter_context(nc.semaphore("ccw2"))
        ccw3 = es.enter_context(nc.semaphore("ccw3"))
        cck = es.enter_context(nc.semaphore("cck"))
        ccv = es.enter_context(nc.semaphore("ccv"))
        ccL = es.enter_context(nc.semaphore("ccL"))

        block = es.enter_context(nc.Block())

        @block.vector
        def _(v):
            sy, sc, te, g = nc.sync, nc.scalar, nc.tensor, nc.gpsimd

            def barrier():
                for s_ in ALL_SEMS:
                    if s_ is not SV:
                        s_.wait(v)
                SV.wait(v)
                bv = SV.inc(v.memset(bar_t[:], 0.0))
                for eng in (sy, sc, te, g):
                    SV.wait(eng, bv)

            SP['ld'].inc(sy.dma_start(out=sidb[:], in_=identb_in[:, :]), 16)
            pl_all = SP['ld'].n
            pv = SV.inc(v.memset(eps_tile[:], EPS_LN))
            for eng in (te, sc, g, v):
                SP['ld'].wait(eng, pl_all)
                SV.wait(eng, pv)

            def gw_tile(u, nu):
                r, o = divmod(u, 116)
                return GW[r * 128:(r + 1) * 128, o * 512:(o + nu) * 512]

            esW = contextlib.ExitStack()
            PCW = 29 * 512
            lwf = esW.enter_context(nc.sbuf_tensor("lwfW", [128, 2 * PCW], F32))
            lwb = esW.enter_context(nc.sbuf_tensor("lwbW", [128, PCW], BF16))
            cvW = []
            for piece in range(4):
                if piece >= 2:
                    SS.wait(sy, cvW[piece - 2])
                lv = SW['ld'].inc(sy.dma_start(
                    out=lwf[:, (piece % 2) * PCW:(piece % 2 + 1) * PCW],
                    in_=lw_in[:, piece * PCW:(piece + 1) * PCW]), 16)
                SW['ld'].wait(sc, lv)
                if piece >= 1:
                    SW['st'].wait(sc, 16 * piece)
                ic_ = sc.activation(lwb[:], lwf[:, (piece % 2) * PCW:(piece % 2 + 1) * PCW],
                                    AF.Copy, bias=0.0, scale=1.0)
                cv = SS.inc(ic_)
                cvW.append(cv)
                SS.wait(g, cv)
                SW['st'].inc(g.dma_start(out=lwc_d[:, piece * PCW:(piece + 1) * PCW],
                                         in_=lwb[:]), 16)
            esQ1 = contextlib.ExitStack()
            sbQ1 = lambda n_, s_, d_: esQ1.enter_context(nc.sbuf_tensor(n_, s_, d_))
            ws = sbQ1("wsQ", [128, 6144], F32)
            absacc = sbQ1("absaccQ", [128, 8], F32)
            mcol = sbQ1("mcolQ", [128, 4], F32)
            pabs_sb = sbQ1("pabsQ", [1, 8], F32)
            pieces = [(w1cs_in, 0, 0), (w1cs_in, 6144, 1),
                      (w2cs_in, 0, 2), (w2cs_in, 6144, 3),
                      (w3rs_in, 0, 4), (w3rs_in, 6144, 5)]
            red_done = []
            for pi, (srcap, c0, col) in enumerate(pieces):
                if pi > 0:
                    SV.wait(sy, red_done[-1])
                lv = SQ['ld'].inc(sy.dma_start(out=ws[:], in_=srcap[:, c0:c0 + 6144]), 16)
                SQ['ld'].wait(v, lv)
                ins = v.tensor_reduce(absacc[:, col:col + 1], ws[:],
                                      axis=AX.X, op=ALU.add, apply_absolute_value=True)
                red_done.append(SV.inc(ins))
                SV.wait(v, red_done[-1])
            for m in range(3):
                ins = v.tensor_tensor(mcol[:, m:m + 1], absacc[:, 2 * m:2 * m + 1],
                                      absacc[:, 2 * m + 1:2 * m + 2], op=ALU.add)
            SV.inc(ins)
            mz = SV.inc(v.memset(pabs_sb[:], 0.0))
            SV.wait(g, SV.n)
            gv = SGE.inc(g.tensor_reduce(pabs_sb[0:1, 0:3], mcol[:, 0:3],
                                         axis=AX.C, op=ALU.add))
            SGE.wait(g, gv)
            sv_ = SQ['st'].inc(g.dma_start(out=pabs_d[0:1, :], in_=pabs_sb[:]), 16)
            SQ['st'].wait(g, sv_)
            g.collective_compute("AllReduce", ALU.add,
                                 replica_groups=[list(range(NCORES))],
                                 ins=[pabs_d.opt()], outs=[pall_d.opt()]).then_inc(ccg)
            SW['st'].wait(g)
            g.collective_compute("AllGather", ALU.bypass,
                                 replica_groups=[list(range(NCORES))],
                                 ins=[lwc_d.opt()], outs=[GW.opt()]).then_inc(ccL)
            esQ1.close()
            esW.close()
            barrier()

            esA = contextlib.ExitStack()
            sbA = lambda n_, s_, d_: esA.enter_context(nc.sbuf_tensor(n_, s_, d_))
            xs = sbA("xsA", [128, NT * D], F32)
            xln = sbA("xlnA", [128, NT * D], BF16)
            xlnT = sbA("xlnTA", [128, 16 * 1024], BF16)
            wkv32 = sbA("wkv32A", [128, 4096], F32)
            wkvb = sbA("wkvbA", [128, 4096], BF16)
            bkv32 = sbA("bkv32A", [1, 256], F32)
            bkvb = sbA("bkvbA", [1, 256], BF16)
            vfm = sbA("vfmA", [128, 2048], F32)
            vT = sbA("vTA", [128, NT * 256], F32)
            lnv = sbA("lnvA", [128, 2 * 256], F32)
            sid32 = sbA("sid32A", [128, 128], F32)
            ones_row = sbA("onesrA", [1, 512], BF16)
            ptrbA = [esA.enter_context(nc.psum_tensor("ptrbA0", [128, 512], BF16)),
                     esA.enter_context(nc.psum_tensor("ptrbA1", [128, 512], BF16))]
            pvb = [esA.enter_context(nc.psum_tensor("pvbA0", [128, 512], F32)),
                   esA.enter_context(nc.psum_tensor("pvbA1", [128, 512], F32))]
            ptr32A = [esA.enter_context(nc.psum_tensor("ptr32A0", [128, 512], F32)),
                      esA.enter_context(nc.psum_tensor("ptr32A1", [128, 512], F32))]
            xlnT3 = xlnT[:].rearrange("p (a c) -> p a c", c=1024)
            vT3 = vT[:].rearrange("p (t c) -> p t c", c=256)
            SA['ld'].inc(sy.dma_start(out=xs[:], in_=x_in[:, :]), 16)
            SA['ld'].inc(sy.dma_start(out=wkv32[:], in_=wkvv_in[:, :]), 16)
            SA['ld'].inc(sy.dma_start(out=bkv32[:], in_=bkvv_in[:, :]), 16)
            SA['ld'].inc(sy.dma_start(out=sid32[:], in_=ident32_in[:, :]), 16)
            ld_all = SA['ld'].n
            orv = SV.inc(v.memset(ones_row[:], 1.0))
            SV.wait(te, orv)
            SA['ld'].wait(sc, ld_all)
            sc.activation(wkvb[:], wkv32[:], AF.Copy, bias=0.0, scale=1.0)
            cast_done = SS.inc(sc.activation(bkvb[:], bkv32[:], AF.Copy, bias=0.0, scale=1.0))
            SA['ld'].wait(v, 16)
            emit_ln(v, sc, {'v': SV, 's': SS}, lambda t: xs[:, t * D:(t + 1) * D], NT, D,
                    stats, mvall, negm, sd_, rstd, eps_tile)
            for t in range(NT):
                i = v.tensor_scalar(xln[:, t * D:(t + 1) * D], xs[:, t * D:(t + 1) * D],
                                    negm[:, t:t + 1], rstd[:, t:t + 1], ALU.add, ALU.mult)
            xln_done = SV.inc(i)
            SA['ld'].wait(te, ld_all)
            SV.wait(te, xln_done)
            evac_vals = []
            gi = 0
            for t in range(NT):
                for q in range(4):
                    bank = ptrbA[gi % 2]
                    if gi >= 2:
                        SV.wait(te, evac_vals[gi - 2])
                    for i in range(4):
                        dc = q * 4 + i
                        ins = te.transpose(bank[:, i * 128:(i + 1) * 128],
                                           xln[:, t * D + dc * 128:t * D + (dc + 1) * 128],
                                           sidb[:])
                    tv = STE.inc(ins)
                    STE.wait(v, tv)
                    dst = xlnT3[:, 4 * q:4 * q + 4, t * 128:(t + 1) * 128]
                    evac_vals.append(SV.inc(v.tensor_copy(dst, bank[:])))
                    gi += 1
            xlnT_done = SV.n
            SS.wait(te, cast_done)
            SV.wait(te, xlnT_done)
            vp_vals = []
            for idx, (vc, tb) in enumerate([(a, b) for a in range(2) for b in range(2)]):
                bank = pvb[idx % 2]
                if idx >= 2:
                    SV.wait(te, vp_vals[idx - 2])
                for dc in range(16):
                    te.matmul(bank[:], wkvb[:, dc * 256 + vc * 128:dc * 256 + (vc + 1) * 128],
                              xlnT3[:, dc, tb * 512:(tb + 1) * 512],
                              start=(dc == 0), stop=False)
                ins = te.matmul(bank[:], bkvb[:, vc * 128:(vc + 1) * 128], ones_row[:],
                                start=False, stop=True)
                tv = STE.inc(ins)
                STE.wait(v, tv)
                vp_vals.append(SV.inc(v.tensor_copy(
                    vfm[:, vc * 1024 + tb * 512:vc * 1024 + (tb + 1) * 512], bank[:])))
            SV.wait(te)
            vt_vals = []
            for grp in range(4):
                vc, th = divmod(grp, 2)
                bank = ptr32A[grp % 2]
                if grp >= 2:
                    SV.wait(te, vt_vals[grp - 2])
                for i in range(4):
                    t = th * 4 + i
                    ins = te.transpose(bank[:, i * 128:(i + 1) * 128],
                                       vfm[:, vc * 1024 + t * 128:vc * 1024 + (t + 1) * 128],
                                       sid32[:])
                tv = STE.inc(ins)
                STE.wait(v, tv)
                dst = vT3[:, th * 4:th * 4 + 4, vc * 128:(vc + 1) * 128]
                vt_vals.append(SV.inc(v.tensor_copy(dst, bank[:])))
            emit_ln(v, sc, {'v': SV, 's': SS}, lambda t: vT[:, t * 256:(t + 1) * 256], NT, 256,
                    stats, mvall, negm, sd_, rstd, eps_tile)
            for t in range(NT):
                lb = (t % 2) * 256
                v.tensor_scalar(lnv[:, lb:lb + 256], vT[:, t * 256:(t + 1) * 256],
                                negm[:, t:t + 1], rstd[:, t:t + 1], ALU.add, ALU.mult)
                for gq in range(4):
                    for h in range(8):
                        c0 = t * D + gq * 512 + h * 64
                        i = v.tensor_tensor(xs[:, c0:c0 + 64], xs[:, c0:c0 + 64],
                                            lnv[:, lb + gq * 64:lb + gq * 64 + 64], op=ALU.add)
            x_done = SV.inc(i)
            SV.wait(g, x_done)
            SA['st'].inc(g.dma_start(out=x1_d[:, :], in_=xs[:]), 16)
            SA['st'].wait(g)
            esA.close()
            barrier()

            esQ2 = contextlib.ExitStack()
            sbQ2 = lambda n_, s_, d_: esQ2.enter_context(nc.sbuf_tensor(n_, s_, d_))
            ws2 = sbQ2("ws2Q", [128, 2 * 6144], F32)
            wi = sbQ2("wiQ", [128, 2 * 6144], F32)
            wqb = sbQ2("wqbQ", [128, 2 * 6144], BF16)
            pall_sb = sbQ2("pallQ", [1, 8], F32)
            gamr = sbQ2("gamrQ", [1, 4], F32)
            rcol = sbQ2("rcolQ", [128, 4], F32)
            sy.wait_ge(ccg, 1)
            lv = SQ['ld'].inc(sy.dma_start(out=pall_sb[:], in_=pall_d[0:1, :]), 16)
            SQ['ld'].wait(v, lv)
            i = v.tensor_scalar(gamr[:, 0:3], pall_sb[:, 0:3], 1.0 / (H0 * D), 1e-8,
                                ALU.mult, ALU.add)
            SV.wait(v, SV.inc(i))
            i = v.reciprocal(gamr[:, 0:3], gamr[:, 0:3])
            iv = SV.inc(i)
            SV.wait(g, iv)
            gv2 = SQ['st'].inc(g.dma_start(out=gam_d[0:1, 0:3], in_=gamr[:, 0:3]), 16)
            SQ['st'].wait(g, gv2)
            gv2 = SQ['st'].inc(g.dma_start(
                out=rcol[:, 0:3],
                in_=gam_d[0:1, 0:3].squeeze(0).partition_broadcast(128)), 16)
            SQ['st'].wait(v, gv2)
            SQL = [Sem(es, nc, "sem_q2ld0"), Sem(es, nc, "sem_q2ld1")]
            SQS = [Sem(es, nc, "sem_q2st0"), Sem(es, nc, "sem_q2st1")]
            QSRC = [(w1cs_in, q1c_d, G1, ccw1), (w2cs_in, q2c_d, G2, ccw2),
                    (w3rs_in, q3c_d, G3, ccw3)]
            qv_l = []
            for it in range(6):
                m, half = divmod(it, 2)
                src, dst, gout, ch = QSRC[m]
                b = it % 2
                sl = b * 6144
                if it >= 2:
                    SV.wait(sy, qv_l[it - 2])
                lv = SQL[b].inc(sy.dma_start(out=ws2[:, sl:sl + 6144],
                                             in_=src[:, half * 6144:(half + 1) * 6144]), 16)
                SQL[b].wait(v, lv)
                if it >= 2:
                    SQS[b].wait(v)
                v.tensor_scalar(wi[:, sl:sl + 6144], ws2[:, sl:sl + 6144],
                                rcol[:, m:m + 1], 0.5, ALU.mult, ALU.is_ge)
                v.tensor_scalar(ws2[:, sl:sl + 6144], ws2[:, sl:sl + 6144],
                                rcol[:, m:m + 1], -0.5, ALU.mult, ALU.is_le)
                i = v.tensor_tensor(wqb[:, sl:sl + 6144], wi[:, sl:sl + 6144],
                                    ws2[:, sl:sl + 6144], op=ALU.subtract)
                qv_l.append(SV.inc(i))
                SV.wait(g, qv_l[-1])
                SQS[b].inc(g.dma_start(out=dst[:, half * 6144:(half + 1) * 6144],
                                       in_=wqb[:, sl:sl + 6144]), 16)
                if half == 1:
                    SQS[0].wait(g)
                    SQS[1].wait(g)
                    g.collective_compute("AllGather", ALU.bypass,
                                         replica_groups=[list(range(NCORES))],
                                         ins=[dst.opt()], outs=[gout.opt()]).then_inc(ch)
            esQ2.close()
            barrier()

            esC = contextlib.ExitStack()
            sbC = lambda n_, s_, d_: esC.enter_context(nc.sbuf_tensor(n_, s_, d_))
            xtC = sbC("xtC", [128, 2 * D], F32)
            xlntC = sbC("xlntC", [128, 2 * D], BF16)
            xlnTC = sbC("xlnTC", [128, 16 * 1024], BF16)
            hidT = sbC("hidTC", [128, H0C * 1024], BF16)
            w1t = sbC("w1tC", [128, 2 * 2048], BF16)
            w2t = sbC("w2tC", [128, 2 * 2048], BF16)
            w3t = sbC("w3tC", [128, 2 * 2048], BF16)
            sg = sbC("sgC", [128, 2 * 512], F32)
            xoutC = sbC("xoutC", [128, 2 * D], F32)
            xin2C = sbC("xin2C", [128, 2 * D], F32)
            psC = contextlib.ExitStack()
            ptrbC = [psC.enter_context(nc.psum_tensor("ptrbC0", [128, 512], BF16)),
                     psC.enter_context(nc.psum_tensor("ptrbC1", [128, 512], BF16))]
            pf = [psC.enter_context(nc.psum_tensor(f"pfC{i}", [128, 512], F32)) for i in range(4)]
            xlnT3C = xlnTC[:].rearrange("p (a c) -> p a c", c=1024)
            sy.wait_ge(ccw1, 1)
            sy.wait_ge(ccw2, 1)
            sy.wait_ge(ccw3, 1)
            emit_ln_transpose(nc, SC_, v, sy, sc, te, x1_d, xtC, xlntC, xlnT3C, sidb,
                              stats, mvall, negm, sd_, rstd, eps_tile, ptrbC)
            xlnT_doneC = SV.n
            SV.wait(te, xlnT_doneC)
            def g1tile(hc):
                r, hcl = divmod(hc, 6)
                return (G1[r * 128:(r + 1) * 128, hcl * 2048:(hcl + 1) * 2048],
                        G2[r * 128:(r + 1) * 128, hcl * 2048:(hcl + 1) * 2048])
            grp_sg = []
            grp_hid = []
            grp_mm = []
            for gi2 in range(2 * H0C):
                hc, tbk = divmod(gi2, 2)
                if tbk == 0:
                    if hc >= 2:
                        STE.wait(sy, grp_mm[(hc - 2) * 2 + 1][1])
                    s1, s2 = g1tile(hc)
                    SC_['ldw'].inc(sy.dma_start(out=w1t[:, (hc % 2) * 2048:(hc % 2 + 1) * 2048],
                                                in_=s1), 16)
                    SC_['ldw'].inc(sy.dma_start(out=w2t[:, (hc % 2) * 2048:(hc % 2 + 1) * 2048],
                                                in_=s2), 16)
                b1, b2 = pf[(gi2 % 2) * 2], pf[(gi2 % 2) * 2 + 1]
                wof = (hc % 2) * 2048
                SC_['ldw'].wait(te, 32 * (hc + 1))
                if gi2 >= 2:
                    SV.wait(te, grp_hid[gi2 - 2])
                for dc in range(16):
                    i1 = te.matmul(b1[:], w1t[:, wof + dc * 128:wof + (dc + 1) * 128],
                                   xlnT3C[:, dc, tbk * 512:(tbk + 1) * 512],
                                   start=(dc == 0), stop=(dc == 15))
                p1v = STE.inc(i1)
                for dc in range(16):
                    i2 = te.matmul(b2[:], w2t[:, wof + dc * 128:wof + (dc + 1) * 128],
                                   xlnT3C[:, dc, tbk * 512:(tbk + 1) * 512],
                                   start=(dc == 0), stop=(dc == 15))
                p2v = STE.inc(i2)
                grp_mm.append((p1v, p2v))
                STE.wait(sc, p1v)
                if gi2 >= 2:
                    SV.wait(sc, grp_hid[gi2 - 2])
                isg = sc.activation(sg[:, (gi2 % 2) * 512:(gi2 % 2 + 1) * 512], b1[:],
                                    AF.Sigmoid, bias=0.0, scale=1.0)
                grp_sg.append(SS.inc(isg))
                STE.wait(v, p2v)
                SS.wait(v, grp_sg[-1])
                ih = v.scalar_tensor_tensor(
                    hidT[:, hc * 1024 + tbk * 512:hc * 1024 + (tbk + 1) * 512],
                    b2[:], 1.0, sg[:, (gi2 % 2) * 512:(gi2 % 2 + 1) * 512],
                    ALU.mult, ALU.mult)
                grp_hid.append(SV.inc(ih))
            hid_done = SV.n
            psC.close()
            px = [esC.enter_context(nc.psum_tensor(f"pxC{i}", [128, 512], F32)) for i in range(8)]
            SV.wait(te, hid_done)
            pair_apply = []
            w3_cons = []
            for pair in range(4):
                if pair >= 1:
                    SV.wait(sy, pair_apply[pair - 1])
                for i in range(2):
                    t = pair * 2 + i
                    SC_['ldxr'].inc(sy.dma_start(
                        out=xin2C[:, i * D:(i + 1) * D],
                        in_=x1_d[:, t * D:(t + 1) * D]), 16)
                for hc in range(H0C):
                    di = pair * H0C + hc
                    if di >= 2:
                        STE.wait(sy, w3_cons[di - 2])
                    r3, a3 = divmod(hc, 6)
                    SC_['ldw3'].inc(sy.dma_start(out=w3t[:, (di % 2) * 2048:(di % 2 + 1) * 2048],
                                                 in_=G3[r3 * 128:(r3 + 1) * 128,
                                                        a3 * 2048:(a3 + 1) * 2048]), 16)
                    SC_['ldw3'].wait(te, 16 * (di + 1))
                    if hc == 0 and pair > 0:
                        SV.wait(te, pair_apply[pair - 1])
                    for tt in range(2):
                        t = pair * 2 + tt
                        for dg in range(4):
                            iw = te.matmul(px[tt * 4 + dg][:],
                                           hidT[:, hc * 1024 + t * 128:hc * 1024 + (t + 1) * 128],
                                           w3t[:, (di % 2) * 2048 + dg * 512:(di % 2) * 2048 + (dg + 1) * 512],
                                           start=(hc == 0), stop=(hc == H0C - 1))
                    w3_cons.append(STE.inc(iw))
                mm_done = w3_cons[-1]
                STE.wait(v, mm_done)
                STE.wait(sc, mm_done)
                for tt in range(2):
                    t = pair * 2 + tt
                    s0 = t * 24
                    for dg in range(4):
                        ist = v.bn_stats(stats[:, s0 + 6 * dg:s0 + 6 * dg + 6], px[tt * 4 + dg][:])
                        SV.wait(v, SV.inc(ist))
                    ia = v.bn_aggr(mvall[:, 2 * t:2 * t + 2], stats[:, s0:s0 + 24])
                    SV.wait(v, SV.inc(ia))
                    i1 = v.tensor_scalar(negm[:, t:t + 1], mvall[:, 2 * t:2 * t + 1], -1.0, None, ALU.mult)
                    nv = SV.inc(i1)
                    SV.wait(sc, nv)
                    i2 = sc.activation(sd_[:, t:t + 1], mvall[:, 2 * t + 1:2 * t + 2], AF.Sqrt,
                                       bias=eps_tile[:, 0:1], scale=1.0)
                    sv = SS.inc(i2)
                    SS.wait(v, sv)
                    i3 = v.reciprocal(rstd[:, t:t + 1], sd_[:, t:t + 1])
                    SV.wait(v, SV.inc(i3))
                    for dg in range(4):
                        v.tensor_scalar(xoutC[:, tt * D + dg * 512:tt * D + (dg + 1) * 512],
                                        px[tt * 4 + dg][:], negm[:, t:t + 1], rstd[:, t:t + 1],
                                        ALU.add, ALU.mult)
                    SC_['ldxr'].wait(v, 16 * (pair * 2 + tt + 1))
                    iadd = v.tensor_tensor(xoutC[:, tt * D:(tt + 1) * D],
                                           xoutC[:, tt * D:(tt + 1) * D],
                                           xin2C[:, tt * D:(tt + 1) * D],
                                           op=ALU.add)
                    av = SV.inc(iadd)
                    SV.wait(g, av)
                    SC_['st'].inc(g.dma_start(out=x2_d[:, t * D:(t + 1) * D],
                                              in_=xoutC[:, tt * D:(tt + 1) * D]), 16)
                pair_apply.append(SV.n)
                SC_['st'].wait(v)
            SC_['st'].wait(g)
            esC.close()
            barrier()

            esD = contextlib.ExitStack()
            sbD = lambda n_, s_, d_: esD.enter_context(nc.sbuf_tensor(n_, s_, d_))
            xtD = sbD("xtD", [128, 2 * D], F32)
            xlntD = sbD("xlntD", [128, 2 * D], BF16)
            xlnTD = sbD("xlnTD", [128, 16 * 1024], BF16)
            wsb = sbD("wsbD", [128, 2 * 2048], BF16)
            qf = sbD("qfD", [128, 2 * 1024], F32)
            ct = sbD("ctD", [128, 1024], F32)
            st_ = sbD("stD", [128, 1024], F32)
            tt1 = sbD("tt1D", [128, 2 * 1024], F32)
            tt2 = sbD("tt2D", [128, 2 * 1024], F32)
            qsw = sbD("qswD", [128, 2 * 1024], F32)
            sgn = sbD("sgnD", [128, 1], F32)
            qro = sbD("qroD", [128, 16 * 1024], BF16)
            kro = sbD("kroD", [128, 4 * 1024], BF16)
            vwb = sbD("vwbD", [128, 16 * 512], BF16)
            vt = sbD("vtD", [128, 8 * 512], BF16)
            ptrbD = [esD.enter_context(nc.psum_tensor("ptrbD0", [128, 512], BF16)),
                     esD.enter_context(nc.psum_tensor("ptrbD1", [128, 512], BF16))]
            pp = [esD.enter_context(nc.psum_tensor(f"ppD{i}", [128, 512], F32)) for i in range(4)]
            xlnT3D = xlnTD[:].rearrange("p (a c) -> p a c", c=1024)
            v.memset(sgn[0:64, :], -1.0)
            v.memset(sgn[64:128, :], 1.0)
            SD['ld'].inc(sy.dma_start(out=ct[0:64, :], in_=cos_in[:, :]), 16)
            SD['ld'].inc(sy.dma_start(out=ct[64:128, :], in_=cos_in[:, :]), 16)
            SD['ld'].inc(sy.dma_start(out=st_[0:64, :], in_=sin_in[:, :]), 16)
            SD['ld'].inc(sy.dma_start(out=st_[64:128, :], in_=sin_in[:, :]), 16)
            ld_allD = SD['ld'].n
            emit_ln_transpose(nc, SD, v, sy, sc, te, x2_d, xtD, xlntD, xlnT3D, sidb,
                              stats, mvall, negm, sd_, rstd, eps_tile, ptrbD)
            xlnT_doneD = SV.n
            SV.wait(te, xlnT_doneD)
            SD['ld'].wait(v, ld_allD)
            sy.wait_ge(ccL, 1)
            rope_done = []
            evq = []
            mmq = []
            for ci in range(20):
                is_q = ci < 16
                u0 = 4 * ci if is_q else 64 + 4 * (ci - 16)
                dstf = qro if is_q else kro
                dof = (ci if is_q else ci - 16) * 1024
                wb = (ci % 2)
                if ci >= 2:
                    STE.wait(sy, mmq[2 * (ci - 2) + 1])
                SD['ldw'].inc(sy.dma_start(out=wsb[:, wb * 2048:(wb + 1) * 2048],
                                           in_=gw_tile(u0, 4)), 16)
                for tb in range(2):
                    pb = pp[(2 * ci + tb) % 4]
                    if 2 * ci + tb >= 4:
                        SS.wait(te, evq[2 * ci + tb - 4])
                    SD['ldw'].wait(te, 16 * (ci + 1))
                    for dc in range(16):
                        im = te.matmul(pb[:], wsb[:, wb * 2048 + dc * 128:wb * 2048 + (dc + 1) * 128],
                                       xlnT3D[:, dc, tb * 512:(tb + 1) * 512],
                                       start=(dc == 0), stop=(dc == 15))
                    mmq.append(STE.inc(im))
                    STE.wait(sc, mmq[-1])
                    if ci >= 2 and tb == 0:
                        SV.wait(sc, rope_done[ci - 2])
                    qb = (ci % 2) * 1024
                    iev = sc.activation(qf[:, qb + tb * 512:qb + (tb + 1) * 512], pb[:],
                                        AF.Copy, bias=0.0, scale=(QSCALE if is_q else 1.0))
                    evq.append(SS.inc(iev))
                SS.wait(v, evq[-1])
                qb = (ci % 2) * 1024
                b = (ci % 2) * 1024
                v.tensor_copy(qsw[0:64, b:b + 1024], qf[64:128, qb:qb + 1024])
                v.tensor_copy(qsw[64:128, b:b + 1024], qf[0:64, qb:qb + 1024])
                v.tensor_tensor(tt1[:, b:b + 1024], qf[:, qb:qb + 1024], ct[:], op=ALU.mult)
                v.tensor_tensor(tt2[:, b:b + 1024], qsw[:, b:b + 1024], st_[:], op=ALU.mult)
                ir = v.scalar_tensor_tensor(dstf[:, dof:dof + 1024], tt2[:, b:b + 1024],
                                            sgn[:, 0:1], tt1[:, b:b + 1024],
                                            ALU.mult, ALU.add)
                rope_done.append(SV.inc(ir))
            SD['ldw'].inc(sy.dma_start(out=vwb[:], in_=gw_tile(144, 16)), 16)
            ldw_allD = SD['ldw'].n
            SV.wait(te)
            SS.wait(te)
            SD['ldw'].wait(te, ldw_allD)
            evv = []
            for t in range(NT):
                pb = pp[t % 4]
                if t >= 4:
                    SV.wait(te, evv[t - 4])
                for dc in range(16):
                    im = te.matmul(pb[:], xlnT3D[:, dc, t * 128:(t + 1) * 128],
                                   vwb[:, dc * 512:(dc + 1) * 512],
                                   start=(dc == 0), stop=(dc == 15))
                mv_ = STE.inc(im)
                STE.wait(v, mv_)
                iev = v.tensor_copy(vt[:, t * 512:(t + 1) * 512], pb[:])
                evv.append(SV.inc(iev))
            SV.wait(g)
            SD['st'].inc(g.dma_start(out=q_d[:, :], in_=qro[:]), 16)
            SD['st'].inc(g.dma_start(out=kc_d[:, :], in_=kro[:]), 16)
            SD['st'].inc(g.dma_start(out=vc_d.rearrange("(t p) d -> p t d", p=128),
                                     in_=vt[:].rearrange("p (t d) -> p t d", d=512)), 16)
            SD['st'].wait(g)
            groups2 = [[0, 1, 2, 3], [4, 5, 6, 7]]
            g.collective_compute("AllGather", ALU.bypass, replica_groups=groups2,
                                 ins=[kc_d.opt()], outs=[KG.opt()]).then_inc(cck)
            g.collective_compute("AllGather", ALU.bypass, replica_groups=groups2,
                                 ins=[vc_d.opt()], outs=[VG.opt()]).then_inc(ccv)
            esD.close()
            barrier()

            esE = contextlib.ExitStack()
            sbE = lambda n_, s_, d_: esE.enter_context(nc.sbuf_tensor(n_, s_, d_))
            attn_tok = sbE("attokE", [128, NT * 2048], BF16)
            negmb = sbE("negmbE", [128, 16], F32)
            es1E = contextlib.ExitStack()
            sb1E = lambda n_, s_, d_: es1E.enter_context(nc.sbuf_tensor(n_, s_, d_))
            k_sb = sb1E("ksbE", [128, 4 * T], BF16)
            v_sb = sb1E("vsbE", [128, 32 * 512], BF16)
            q_sb = sb1E("qsbE", [128, 16 * 1024], BF16)
            ksq = sb1E("ksqE", [128, 2 * T], BF16)
            P_sb = sb1E("PsbE", [128, 2 * 512], BF16)
            msk = sb1E("mskE", [128, 512], BF16)
            att_s = sb1E("attsE", [128, 512], BF16)
            l_sb = sb1E("lsbE", [1, 512], F32)
            rl_col = sb1E("rlcolE", [128, 4], F32)
            nm_row = sb1E("nmrowE", [1, 40], F32)
            kn = sb1E("knE", [1, 4], F32)
            qn = sb1E("qnE", [1, 16], F32)
            knr = sb1E("knrE", [1, 16], F32)
            nmr = sb1E("nmrE", [1, 16], F32)
            sid32 = sb1E("sid32E", [128, 128], F32)
            ones_cb = sb1E("onescbE", [128, 1], BF16)
            ps1E = contextlib.ExitStack()
            pst = [ps1E.enter_context(nc.psum_tensor(f"pstE{i}", [128, 512], F32)) for i in range(2)]
            ppv = ps1E.enter_context(nc.psum_tensor("ppvE", [128, 512], F32))
            pl = ps1E.enter_context(nc.psum_tensor("plE", [1, 512], F32))
            ptrE = ps1E.enter_context(nc.psum_tensor("ptrE", [128, 512], BF16))
            ptr32E = ps1E.enter_context(nc.psum_tensor("ptr32E", [128, 128], F32))
            pn = ps1E.enter_context(nc.psum_tensor("pnE", [1, 512], F32))
            SE['ld'].inc(sy.dma_start(out=q_sb[:], in_=q_d[:, :]), 16)
            sy.wait_ge(cck, 1)
            for gq in range(4):
                k_sb4 = k_sb[:, gq * T:(gq + 1) * T].rearrange(
                    "p (j k c) -> p j k c", k=4, c=128)
                for kk in range(4):
                    SE['ld'].inc(sy.dma_start(
                        out=k_sb4[:, :, kk, :],
                        in_=KG[kk * 128:(kk + 1) * 128,
                               gq * 1024:(gq + 1) * 1024].rearrange(
                            "p (j c) -> p j c", c=128)), 16)
            sy.wait_ge(ccv, 1)
            v_sb4 = v_sb[:].rearrange("p (j k d) -> p j k d", k=4, d=512)
            for kk in range(4):
                SE['ld'].inc(sy.dma_start(
                    out=v_sb4[:, :, kk, :],
                    in_=VG[kk * 1024:(kk + 1) * 1024, :].rearrange(
                        "(j p) d -> p j d", p=128)), 16)
            SE['ld'].inc(sy.dma_start(out=msk[:], in_=mask_in[:, :]), 16)
            SE['ld'].inc(sy.dma_start(out=sid32[:], in_=ident32_in[:, :]), 16)
            ld_allE = SE['ld'].n
            ocv = SV.inc(v.memset(ones_cb[:], 1.0))
            SV.wait(te, ocv)
            SE['ld'].wait(v, ld_allE)
            SE['ld'].wait(sc, ld_allE)
            SE['ld'].wait(te, ld_allE)
            for gi in range(4):
                if gi >= 2:
                    SV.wait(sc)
                isq = sc.activation(ksq[:, (gi % 2) * T:(gi % 2 + 1) * T],
                                    k_sb[:, gi * T:(gi + 1) * T],
                                    AF.Square, bias=0.0, scale=1.0)
                sqv = SS.inc(isq)
                SS.wait(te, sqv)
                for c in range(8):
                    imm = te.matmul(pn[:], ones_cb[:],
                                    ksq[:, (gi % 2) * T + c * 512:(gi % 2) * T + (c + 1) * 512],
                                    start=True, stop=True)
                    tv = STE.inc(imm)
                    STE.wait(v, tv)
                    ir = v.tensor_reduce(nm_row[:, 8 * (gi % 2) + c:8 * (gi % 2) + c + 1],
                                         pn[0:1, :], axis=AX.X, op=ALU.max)
                    SV.wait(v, SV.inc(ir))
                    SV.wait(te)
                ir = v.tensor_reduce(kn[:, gi:gi + 1], nm_row[:, 8 * (gi % 2):8 * (gi % 2) + 8],
                                     axis=AX.X, op=ALU.max)
                SV.wait(v, SV.inc(ir))
            for h in range(NHEAD):
                isq = sc.activation(ksq[:, (h % 4) * 1024:(h % 4) * 1024 + 1024],
                                    q_sb[:, h * 1024:(h + 1) * 1024], AF.Square,
                                    bias=0.0, scale=1.0)
                hv = SS.inc(isq)
                SS.wait(te, hv)
                for c in range(2):
                    imm = te.matmul(pn[:], ones_cb[:],
                                    ksq[:, (h % 4) * 1024 + c * 512:(h % 4) * 1024 + (c + 1) * 512],
                                    start=True, stop=True)
                    tv = STE.inc(imm)
                    STE.wait(v, tv)
                    ir = v.tensor_reduce(nm_row[:, 32 + c:33 + c], pn[0:1, :], axis=AX.X, op=ALU.max)
                    SV.wait(v, SV.inc(ir))
                    SV.wait(te)
                ir = v.tensor_reduce(qn[:, h:h + 1], nm_row[:, 32:34], axis=AX.X, op=ALU.max)
                SV.wait(v, SV.inc(ir))
                SV.wait(sc)
            vv = SV.n
            SV.wait(sc, vv)
            sc.activation(kn[:], kn[:], AF.Sqrt, bias=eps_tile[0:1, 0:1], scale=1.0)
            iq = sc.activation(qn[:], qn[:], AF.Sqrt, bias=eps_tile[0:1, 0:1], scale=1.0)
            sv = SS.inc(iq)
            SS.wait(v, sv)
            for h in range(NHEAD):
                v.tensor_copy(knr[:, h:h + 1], kn[:, h // 4:h // 4 + 1])
            iv = v.tensor_tensor(nmr[:], qn[:], knr[:], op=ALU.mult)
            SV.inc(iv)
            iv2 = v.tensor_scalar(nmr[:], nmr[:], -1.05, None, ALU.mult)
            nv = SV.inc(iv2)
            SV.wait(g, nv)
            gv = SE['gb'].inc(g.dma_start(out=nb_d[0:1, :], in_=nmr[:]), 16)
            SE['gb'].wait(g, gv)
            gv = SE['gb'].inc(g.dma_start(out=negmb[:],
                                          in_=nb_d[0:1, :].squeeze(0).partition_broadcast(128)), 16)
            SE['gb'].wait(sc, gv)
            SE['gb'].wait(v, gv)
            exp_done = {}
            pv_done = {}
            row_free = []
            prev_pv = None
            for h in range(NHEAD):
                gi = h // 4
                for qd in range(2):
                    nk = 16 * (qd + 1)
                    q_ap = q_sb[:, h * 1024 + qd * 512:h * 1024 + (qd + 1) * 512]
                    qk_inc = {}
                    for ktc in range(-1, nk):
                        kn_ = ktc + 1
                        if kn_ < nk:
                            stb_n = pst[kn_ % 2]
                            if kn_ >= 2:
                                SS.wait(te, exp_done[(h, qd, kn_ - 2)])
                            if (h, qd) != (0, 0) and kn_ == 0:
                                SV.wait(te, row_free[-1])
                            iqk = te.matmul(
                                stb_n[:],
                                k_sb[:, gi * T + kn_ * 128:gi * T + (kn_ + 1) * 128],
                                q_ap, start=True, stop=True)
                            qk_inc[kn_] = STE.inc(iqk)
                        if ktc < 0:
                            continue
                        jls = ktc // 4 - qd * 4
                        slot = (ktc % 2) * 512
                        stb = pst[ktc % 2]
                        pfree = pv_done[(h, qd, ktc - 2)] if ktc >= 2 else prev_pv
                        STE.wait(sc, qk_inc[ktc])
                        if pfree is not None:
                            STE.wait(sc, pfree)
                        iex = sc.activation(P_sb[:, slot:slot + 512], stb[:],
                                            AF.Exp, bias=negmb[:, h:h + 1], scale=1.0)
                        exp_done[(h, qd, ktc)] = SS.inc(iex)
                        pv_wait_v = None
                        if jls >= 0:
                            tq = ktc % 4
                            SS.wait(v, exp_done[(h, qd, ktc)])
                            if pfree is not None:
                                STE.wait(v, pfree)
                            imm_ = v.tensor_tensor(
                                P_sb[:, slot + jls * 128:slot + (jls + 1) * 128],
                                P_sb[:, slot + jls * 128:slot + (jls + 1) * 128],
                                msk[:, tq * 128:(tq + 1) * 128], op=ALU.mult)
                            pv_wait_v = SV.inc(imm_)
                            if jls > 0:
                                imz = v.memset(P_sb[:, slot:slot + jls * 128], 0.0)
                                pv_wait_v = SV.inc(imz)
                        SS.wait(te, exp_done[(h, qd, ktc)])
                        if pv_wait_v is not None:
                            SV.wait(te, pv_wait_v)
                        first = (ktc == 0)
                        last = (ktc == nk - 1)
                        te.matmul(ppv[:], v_sb[:, ktc * 512 + gi * 128:ktc * 512 + (gi + 1) * 128],
                                  P_sb[:, slot:slot + 512], start=first, stop=last)
                        ipl = te.matmul(pl[:], ones_cb[:], P_sb[:, slot:slot + 512],
                                        start=first, stop=last)
                        pv_done[(h, qd, ktc)] = STE.inc(ipl)
                    prev_pv = pv_done[(h, qd, nk - 1)]
                    rowv = prev_pv
                    STE.wait(v, rowv)
                    il = v.tensor_scalar(l_sb[:], pl[0:1, :], 1e-30, None, ALU.add)
                    SV.wait(v, SV.inc(il))
                    il2 = v.reciprocal(l_sb[:], l_sb[:])
                    rv = SV.inc(il2)
                    SV.wait(te, rv)
                    for jj in range(4):
                        itr = te.transpose(ptr32E[:, jj:jj + 1],
                                           l_sb[0:1, jj * 128:(jj + 1) * 128],
                                           sid32[0:1, 0:1])
                    t1 = STE.inc(itr)
                    STE.wait(v, t1)
                    ir1 = v.tensor_copy(rl_col[:, 0:4], ptr32E[:, 0:4])
                    ie1 = v.tensor_copy(att_s[:], ppv[:])
                    ev = SV.inc(ie1)
                    SV.wait(v, ev)
                    SV.wait(te, ev)
                    for jj in range(4):
                        itr2 = te.transpose(ptrE[:, jj * 128:(jj + 1) * 128],
                                            att_s[:, jj * 128:(jj + 1) * 128], sidb[:])
                    t2 = STE.inc(itr2)
                    STE.wait(v, t2)
                    for jj in range(4):
                        j = qd * 4 + jj
                        ifin = v.tensor_scalar(
                            attn_tok[:, j * 2048 + h * 128:j * 2048 + (h + 1) * 128],
                            ptrE[:, jj * 128:(jj + 1) * 128], rl_col[:, jj:jj + 1],
                            None, ALU.mult)
                    row_free.append(SV.inc(ifin))
            att_done = SV.n
            ps1E.close()
            es1E.close()
            attn_T = esE.enter_context(nc.sbuf_tensor("attTE", [128, 16 * 1024], BF16))
            c_wb = esE.enter_context(nc.sbuf_tensor("cwbE", [128, 16 * 2048], BF16))
            xoutE = esE.enter_context(nc.sbuf_tensor("xoutE", [128, 2048], F32))
            xin2E = esE.enter_context(nc.sbuf_tensor("xin2E", [128, 2048], F32))
            ptrb2 = [esE.enter_context(nc.psum_tensor(f"ptrb2E{i}", [128, 512], BF16)) for i in range(2)]
            pog = [esE.enter_context(nc.psum_tensor(f"pogE{i}", [128, 512], F32)) for i in range(4)]
            attn_T3 = attn_T[:].rearrange("p (a c) -> p a c", c=1024)
            SV.wait(sy, att_done)
            SE['ldc'].inc(sy.dma_start(out=c_wb[:, 0:9 * 2048],
                                       in_=GW[0:128, 80 * 512:116 * 512]), 16)
            SE['ldc'].inc(sy.dma_start(out=c_wb[:, 9 * 2048:16 * 2048],
                                       in_=GW[128:256, 0:28 * 512]), 16)
            ldc_all = SE['ldc'].n
            SV.wait(te, att_done)
            evac_doneE = []
            for t in range(NT):
                for q in range(4):
                    gidx = t * 4 + q
                    bank = ptrb2[gidx % 2]
                    if gidx >= 2:
                        SV.wait(te, evac_doneE[gidx - 2])
                    for i in range(4):
                        dc = q * 4 + i
                        ins = te.transpose(bank[:, i * 128:(i + 1) * 128],
                                           attn_tok[:, t * 2048 + dc * 128:t * 2048 + (dc + 1) * 128],
                                           sidb[:])
                    tv = STE.inc(ins)
                    STE.wait(v, tv)
                    dst = attn_T3[:, 4 * q:4 * q + 4, t * 128:(t + 1) * 128]
                    evac_doneE.append(SV.inc(v.tensor_copy(dst, bank[:])))
            aT_done = SV.n
            SV.wait(te, aT_done)
            SE['ldc'].wait(te, ldc_all)
            prev_apply = None
            for t in range(NT):
                if prev_apply is not None:
                    SV.wait(sy, prev_apply)
                SE['ldxr'].inc(sy.dma_start(out=xin2E[:], in_=x2_d[:, t * D:(t + 1) * D]), 16)
                if prev_apply is not None:
                    SV.wait(te, prev_apply)
                for fc in range(16):
                    for og in range(4):
                        imm = te.matmul(pog[og][:], attn_T3[:, fc, t * 128:(t + 1) * 128],
                                        c_wb[:, fc * 2048 + og * 512:fc * 2048 + (og + 1) * 512],
                                        start=(fc == 0), stop=(fc == 15))
                mmv = STE.inc(imm)
                STE.wait(v, mmv)
                STE.wait(sc, mmv)
                s0 = t * 24
                for og in range(4):
                    ist = v.bn_stats(stats[:, s0 + 6 * og:s0 + 6 * og + 6], pog[og][:])
                    SV.wait(v, SV.inc(ist))
                ia = v.bn_aggr(mvall[:, 2 * t:2 * t + 2], stats[:, s0:s0 + 24])
                SV.wait(v, SV.inc(ia))
                i1 = v.tensor_scalar(negm[:, t:t + 1], mvall[:, 2 * t:2 * t + 1], -1.0, None, ALU.mult)
                nv2 = SV.inc(i1)
                SV.wait(sc, nv2)
                i2 = sc.activation(sd_[:, t:t + 1], mvall[:, 2 * t + 1:2 * t + 2], AF.Sqrt,
                                   bias=eps_tile[:, 0:1], scale=1.0)
                sv2 = SS.inc(i2)
                SS.wait(v, sv2)
                i3 = v.reciprocal(rstd[:, t:t + 1], sd_[:, t:t + 1])
                SV.wait(v, SV.inc(i3))
                for og in range(4):
                    v.tensor_scalar(xoutE[:, og * 512:(og + 1) * 512], pog[og][:],
                                    negm[:, t:t + 1], rstd[:, t:t + 1], ALU.add, ALU.mult)
                SE['ldxr'].wait(v, 16 * (t + 1))
                iadd = v.tensor_tensor(xoutE[:], xoutE[:], xin2E[:], op=ALU.add)
                av = SV.inc(iadd)
                prev_apply = av
                SV.wait(g, av)
                SE['st'].inc(g.dma_start(out=x3_d[:, t * D:(t + 1) * D], in_=xoutE[:]), 16)
                SE['st'].wait(v)
            SE['st'].wait(g)
            esE.close()
            barrier()

            esF = contextlib.ExitStack()
            sbF = lambda n_, s_, d_: esF.enter_context(nc.sbuf_tensor(n_, s_, d_))
            xlnTF = sbF("xlnTF", [128, 16 * 1024], BF16)
            hs = sbF("hsF", [128, 64 * 1024], BF16)
            es0F = contextlib.ExitStack()
            xtF = es0F.enter_context(nc.sbuf_tensor("xtF", [128, 2 * D], F32))
            xlntF = es0F.enter_context(nc.sbuf_tensor("xlntF", [128, 2 * D], BF16))
            ptrbF = [es0F.enter_context(nc.psum_tensor(f"ptrbF{i}", [128, 512], BF16)) for i in range(2)]
            es1F = contextlib.ExitStack()
            xlnT3F = xlnTF[:].rearrange("p (a c) -> p a c", c=1024)
            emit_ln_transpose(nc, SF, v, sy, sc, te, x3_d, xtF, xlntF, xlnT3F, sidb,
                              stats, mvall, negm, sd_, rstd, eps_tile, ptrbF)
            xlnT_doneF = SV.n
            SV.wait(te, xlnT_doneF)
            es0F.close()
            sb1F = lambda n_, s_, d_: es1F.enter_context(nc.sbuf_tensor(n_, s_, d_))
            wbtF = sb1F("wbtF", [128, 2 * 512], BF16)
            gch = sb1F("gchF", [128, 8 * 512], F32)
            ht = sb1F("htF", [128, 8 * 512], BF16)
            at = sb1F("atF", [128, 8 * 512], BF16)
            cs_ = sb1F("csF", [128, 6 * 512], BF16)
            ts_ = sb1F("tsF", [128, 2 * 512], BF16)
            ppF = [esF.enter_context(nc.psum_tensor(f"ppF{i}", [128, 512], F32)) for i in range(8)]
            SV.wait(sy, xlnT_doneF)
            wcons = []
            ev_doneF = []
            sp_done = []
            for gq in range(16):
                for sweep, ubase in ((0, 160), (1, 416)):
                    for dc in range(16):
                        di = len(wcons)
                        if di >= 2:
                            STE.wait(sy, wcons[di - 2])
                        SF['ldw'].inc(sy.dma_start(
                            out=wbtF[:, (di % 2) * 512:(di % 2 + 1) * 512],
                            in_=gw_tile(ubase + gq * 16 + dc, 1)), 16)
                        SF['ldw'].wait(te, 16 * (di + 1))
                        if dc == 0 and ev_doneF:
                            SV.wait(te, ev_doneF[-1][0])
                            SS.wait(te, ev_doneF[-1][1])
                        for tb in range(2):
                            for sc_ in range(4):
                                imm = te.matmul(
                                    ppF[tb * 4 + sc_][:],
                                    wbtF[:, (di % 2) * 512 + sc_ * 128:(di % 2) * 512 + (sc_ + 1) * 128],
                                    xlnT3F[:, dc, tb * 512:(tb + 1) * 512],
                                    start=(dc == 0), stop=(dc == 15))
                        wcons.append(STE.inc(imm))
                    mmv = wcons[-1]
                    STE.wait(sc, mmv)
                    STE.wait(v, mmv)
                    if sweep == 0:
                        for i8 in range(8):
                            if gq > 0 and i8 == 0:
                                SV.wait(sc, sp_done[-1])
                            isl = sc.activation(gch[:, i8 * 512:(i8 + 1) * 512], ppF[i8][:],
                                                AF.Silu, bias=0.0, scale=1.0)
                        slv = SS.inc(isl)
                        ev_doneF.append((SV.n, slv))
                    else:
                        SS.wait(v)
                        for i8 in range(8):
                            iht = v.scalar_tensor_tensor(ht[:, i8 * 512:(i8 + 1) * 512],
                                                         ppF[i8][:], 1.0, gch[:, i8 * 512:(i8 + 1) * 512],
                                                         ALU.mult, ALU.mult)
                        hv = SV.inc(iht)
                        SV.wait(sc, hv)
                        for i8 in range(8):
                            iab = sc.activation(at[:, i8 * 512:(i8 + 1) * 512], ht[:, i8 * 512:(i8 + 1) * 512],
                                                AF.Abs, bias=0.0, scale=1.0)
                        abv = SS.inc(iab)
                        SS.wait(v, abv)
                        ev_doneF.append((SV.n, abv))
                        for tb in range(2):
                            a_ = [at[:, (tb * 4 + p) * 512:(tb * 4 + p + 1) * 512] for p in range(4)]
                            h_ = [ht[:, (tb * 4 + p) * 512:(tb * 4 + p + 1) * 512] for p in range(4)]
                            c_ = [cs_[:, i * 512:(i + 1) * 512] for i in range(6)]
                            t_ = [ts_[:, i * 512:(i + 1) * 512] for i in range(2)]
                            pairs = [(0, 1), (0, 2), (0, 3), (1, 2), (1, 3), (2, 3)]
                            for ci, (a, b) in enumerate(pairs):
                                v.tensor_tensor(c_[ci], a_[a], a_[b], op=ALU.is_ge)
                            def dstF(p):
                                return hs[:, ((p * 16 + gq) * 1024 + tb * 512):((p * 16 + gq) * 1024 + (tb + 1) * 512)]
                            v.tensor_tensor(t_[0], c_[0], c_[1], op=ALU.add)
                            v.tensor_tensor(t_[1], t_[0], c_[2], op=ALU.add)
                            v.scalar_tensor_tensor(dstF(0), t_[1], 2.0, h_[0], ALU.is_ge, ALU.mult)
                            v.tensor_tensor(t_[0], c_[3], c_[4], op=ALU.add)
                            v.tensor_tensor(t_[1], t_[0], c_[0], op=ALU.subtract)
                            v.scalar_tensor_tensor(dstF(1), t_[1], 1.0, h_[1], ALU.is_ge, ALU.mult)
                            v.tensor_tensor(t_[0], c_[5], c_[1], op=ALU.subtract)
                            v.tensor_tensor(t_[1], t_[0], c_[3], op=ALU.subtract)
                            v.scalar_tensor_tensor(dstF(2), t_[1], 0.0, h_[2], ALU.is_ge, ALU.mult)
                            v.tensor_tensor(t_[0], c_[2], c_[4], op=ALU.add)
                            v.tensor_tensor(t_[1], t_[0], c_[5], op=ALU.add)
                            isp = v.scalar_tensor_tensor(dstF(3), t_[1], 1.0, h_[3], ALU.is_le, ALU.mult)
                        sp_done.append(SV.inc(isp))
            hs_done = SV.n
            es1F.close()
            wdb = esF.enter_context(nc.sbuf_tensor("wdbF", [128, 2 * D], BF16))
            xoutF = esF.enter_context(nc.sbuf_tensor("xoutF", [128, D], F32))
            xin2F = esF.enter_context(nc.sbuf_tensor("xin2F", [128, D], F32))
            SV.wait(te, hs_done)
            SV.wait(sy, hs_done)
            pair_applyF = []
            wd_cons = []
            for pair in range(4):
                if pair >= 1:
                    SV.wait(sy, pair_applyF[-1])
                for ic in range(64):
                    di = pair * 64 + ic
                    if di >= 2:
                        STE.wait(sy, wd_cons[di - 2])
                    SF['ldwd'].inc(sy.dma_start(out=wdb[:, (di % 2) * D:(di % 2 + 1) * D],
                                                in_=gw_tile(672 + 4 * ic, 4)), 16)
                    SF['ldwd'].wait(te, 16 * (di + 1))
                    if ic == 0 and pair > 0:
                        SV.wait(te, pair_applyF[-1])
                    for tt in range(2):
                        t = pair * 2 + tt
                        for dg in range(4):
                            iw = te.matmul(ppF[tt * 4 + dg][:],
                                           hs[:, ic * 1024 + t * 128:ic * 1024 + (t + 1) * 128],
                                           wdb[:, (di % 2) * D + dg * 512:(di % 2) * D + (dg + 1) * 512],
                                           start=(ic == 0), stop=(ic == 63))
                    wd_cons.append(STE.inc(iw))
                mm_done = wd_cons[-1]
                STE.wait(v, mm_done)
                STE.wait(sc, mm_done)
                for tt in range(2):
                    t = pair * 2 + tt
                    SV.wait(sy)
                    SF['ldxr'].inc(sy.dma_start(out=xin2F[:], in_=x3_d[:, t * D:(t + 1) * D]), 16)
                    s0 = t * 24
                    for dg in range(4):
                        ist = v.bn_stats(stats[:, s0 + 6 * dg:s0 + 6 * dg + 6], ppF[tt * 4 + dg][:])
                        SV.wait(v, SV.inc(ist))
                    ia = v.bn_aggr(mvall[:, 2 * t:2 * t + 2], stats[:, s0:s0 + 24])
                    SV.wait(v, SV.inc(ia))
                    i1 = v.tensor_scalar(negm[:, t:t + 1], mvall[:, 2 * t:2 * t + 1], -1.0, None, ALU.mult)
                    nv = SV.inc(i1)
                    SV.wait(sc, nv)
                    i2 = sc.activation(sd_[:, t:t + 1], mvall[:, 2 * t + 1:2 * t + 2], AF.Sqrt,
                                       bias=eps_tile[:, 0:1], scale=1.0)
                    sv = SS.inc(i2)
                    SS.wait(v, sv)
                    i3 = v.reciprocal(rstd[:, t:t + 1], sd_[:, t:t + 1])
                    SV.wait(v, SV.inc(i3))
                    SF['st'].wait(v)
                    for dg in range(4):
                        v.tensor_scalar(xoutF[:, dg * 512:(dg + 1) * 512],
                                        ppF[tt * 4 + dg][:], negm[:, t:t + 1], rstd[:, t:t + 1],
                                        ALU.add, ALU.mult)
                    SF['ldxr'].wait(v)
                    iadd = v.tensor_tensor(xoutF[:], xoutF[:], xin2F[:], op=ALU.add)
                    av = SV.inc(iadd)
                    SV.wait(g, av)
                    SF['st'].inc(g.dma_start(out=xo_out[:, t * D:(t + 1) * D], in_=xoutF[:]), 16)
                pair_applyF.append(SV.n)
            SF['st'].wait(g)
            esF.close()
    return nc


_ROPE_PERM1 = np.concatenate([np.arange(0, 128, 2), np.arange(1, 128, 2)])

def _img_lhsT(wT, nout, dtype):
    out = np.empty((128, nout * 2048), dtype)
    w4 = np.asarray(wT).reshape(16, 128, nout, 128)
    for oc in range(nout):
        out[:, oc * 2048:(oc + 1) * 2048] = w4[:, :, oc, :].transpose(1, 0, 2).reshape(128, 2048)
    return np.ascontiguousarray(out)

def _prep_all(inputs):
    x = np.asarray(inputs["L0_attn_x"], np.float32).reshape(2 * T, D)
    Wkv = np.asarray(inputs["L0_attn_Wkv"], np.float32)
    bkv = np.asarray(inputs["L0_attn_bkv"], np.float32)
    vcols = np.concatenate([np.arange(g * 128 + 64, g * 128 + 128) for g in range(4)])
    wkvv = wimg(np.ascontiguousarray(Wkv[:, vcols]))
    bkvv = np.ascontiguousarray(bkv[vcols]).reshape(1, 256)
    identb = np.eye(128, dtype=BF)
    ident32 = np.eye(128, dtype=np.float32)
    w1T = np.asarray(inputs["L0_ffn_w1"], np.float32).T
    w2T = np.asarray(inputs["L0_ffn_w2"], np.float32).T
    w3T = np.asarray(inputs["L0_ffn_w3"], np.float32).T
    w1Tp = np.zeros((D, H0P), np.float32); w1Tp[:, :H0] = w1T
    w2Tp = np.zeros((D, H0P), np.float32); w2Tp[:, :H0] = w2T
    w3Tp = np.zeros((H0P, D), np.float32); w3Tp[:H0] = w3T
    w1cs, w2cs, w3rs = [], [], []
    for c in range(NCORES):
        w1cs.append(np.ascontiguousarray(
            w1Tp[:, c * 768:(c + 1) * 768].reshape(16, 128, 6, 128)
            .transpose(1, 2, 0, 3).reshape(128, 12288)))
        w2cs.append(np.ascontiguousarray(
            w2Tp[:, c * 768:(c + 1) * 768].reshape(16, 128, 6, 128)
            .transpose(1, 2, 0, 3).reshape(128, 12288)))
        w3rs.append(np.ascontiguousarray(
            w3Tp[c * 768:(c + 1) * 768].reshape(6, 128, D)
            .transpose(1, 0, 2).reshape(128, 12288)))
    q_w = np.asarray(inputs["L1_attn_q_w"], np.float32)
    kv_w = np.asarray(inputs["L1_attn_kv_w"], np.float32)
    qperm = np.concatenate([h * 128 + _ROPE_PERM1 for h in range(16)])
    kperm = np.concatenate([g * 128 + _ROPE_PERM1 for g in range(4)])
    qwT = np.ascontiguousarray(q_w[qperm].T)
    kwT = np.ascontiguousarray(kv_w[0:512][kperm].T)
    vwT = np.ascontiguousarray(kv_w[512:1024].T)
    cwT = np.ascontiguousarray(np.asarray(inputs["L1_attn_c_w"], np.float32).T)
    pairs = np.arange(64, dtype=np.float32)
    freqs = 1.0 / 10000.0 ** (2.0 * pairs / 128.0)
    pos = np.arange(T, dtype=np.float32)
    ang = pos[:, None] * freqs[None, :]
    cosf = np.cos(ang).astype(np.float32)
    sinf = np.sin(ang).astype(np.float32)
    jj = np.arange(HI)
    gq = jj // 512; rem = jj % 512; sc = rem // 128; cc = rem % 128
    perm_cols = 4 * (gq * 128 + cc) + sc
    ic = jj // 128; c2 = jj % 128
    perm_rows = 4 * ((ic % 16) * 128 + c2) + ic // 16
    wg_p = np.asarray(inputs["L1_ffn_w_gate"], np.float32)[:, perm_cols]
    wu_p = np.asarray(inputs["L1_ffn_w_up"], np.float32)[:, perm_cols]
    wd_p = np.ascontiguousarray(np.asarray(inputs["L1_ffn_w_down"], np.float32)[perm_rows, :])
    def wimg_f(wp):
        out = np.empty((256 * 128, 512), np.float32)
        for g in range(16):
            for dc in range(16):
                out[(g * 16 + dc) * 128:(g * 16 + dc + 1) * 128, :] = \
                    wp[dc * 128:(dc + 1) * 128, g * 512:(g + 1) * 512]
        return np.ascontiguousarray(out)
    wg_i = wimg_f(wg_p)
    wu_i = wimg_f(wu_p)
    def rows_img(mat, nrow, w):
        out = np.empty((128, nrow * w), np.float32)
        for t_ in range(nrow):
            out[:, t_ * w:(t_ + 1) * w] = mat[t_ * 128:(t_ + 1) * 128, :]
        return out
    LW = np.concatenate([
        _img_lhsT(qwT, 16, np.float32),
        _img_lhsT(kwT, 4, np.float32),
        rows_img(cwT, 16, 2048),
        rows_img(vwT, 16, 512),
        rows_img(wg_i, 256, 512),
        rows_img(wu_i, 256, 512),
        rows_img(wd_p, 64, 2048),
    ], axis=1)
    lw_sl = [np.ascontiguousarray(LW[:, c * 116 * 512:(c + 1) * 116 * 512])
             for c in range(NCORES)]
    prep = {
        "x": x, "wkvv": wkvv, "bkvv": bkvv, "identb": identb, "ident32": ident32,
        "w1cs": w1cs, "w2cs": w2cs, "w3rs": w3rs,
        "lw": lw_sl, "cosf": cosf, "sinf": sinf,
    }
    return prep

def _core_mask(c):
    kq = c % 4
    mask = np.ones((128, 512), np.float32)
    ii = np.arange(128)
    for tq in range(4):
        mm = mask[:, tq * 128:(tq + 1) * 128]
        if tq == kq:
            mm[ii[:, None] > ii[None, :]] = 0.0
        elif tq > kq:
            mm[:] = 0.0
    return mask.astype(BF)

_NC_CACHE = {}

def kernel(**inputs):
    P = _prep_all(inputs)
    maps = []
    for c in range(NCORES):
        rows = core_rows(c) % T
        maps.append({
            "x_img": img_tokens(P["x"], c),
            "wkvv": P["wkvv"], "bkvv": P["bkvv"],
            "identb": P["identb"], "ident32": P["ident32"],
            "w1cs": P["w1cs"][c], "w2cs": P["w2cs"][c], "w3rs": P["w3rs"][c],
            "lw": P["lw"][c],
            "cosd": np.ascontiguousarray(P["cosf"][rows].T),
            "sind": np.ascontiguousarray(P["sinf"][rows].T),
            "mask": _core_mask(c),
        })
    if "nc" not in _NC_CACHE:
        nc = bass.Bass("TRN2", target_bir_lowering=False)
        build_mega(nc)
        _NC_CACHE["nc"] = nc
    res = run_bass_kernel_spmd(_NC_CACHE["nc"], maps, core_ids=list(range(NCORES)))
    out = np.empty((2 * T, D), np.float32)
    for c in range(NCORES):
        out[core_rows(c)] = unimg_tokens(res.results[c]["xo"], D)
    return out.reshape(2, T, D)


# version 12
# speedup vs baseline: 1.2026x; 1.0152x over previous
import numpy as np
import ml_dtypes
import contextlib
import concourse.bass as bass
import concourse.mybir as mybir
from concourse.bass_utils import run_bass_kernel_spmd

F32 = mybir.dt.float32
BF16 = mybir.dt.bfloat16
I32 = mybir.dt.int32
AF = mybir.ActivationFunctionType
ALU = mybir.AluOpType
AX = mybir.AxisListType
BF = ml_dtypes.bfloat16

D = 2048
T = 4096
NCORES = 8
NT = 8
H0 = 5632
H0C = 44
H0P = 6144
HI = 8192
EPS_LN = 1e-5
DK = 128
NHEAD = 16
NKV = 4
QSCALE = 1.0 / np.sqrt(128.0)


def core_rows(c):
    s, k = divmod(c, 4)
    return np.concatenate([np.arange(s * T + (4 * j + k) * 128,
                                     s * T + (4 * j + k) * 128 + 128) for j in range(8)])

def img_tokens(x2d, c):
    r = core_rows(c)
    w = x2d.shape[1]
    out = np.empty((128, NT * w), x2d.dtype)
    for t in range(NT):
        out[:, t * w:(t + 1) * w] = x2d[r[t * 128:(t + 1) * 128]]
    return np.ascontiguousarray(out)

def unimg_tokens(img, w):
    out = np.empty((NT * 128, w), img.dtype)
    for t in range(NT):
        out[t * 128:(t + 1) * 128] = img[:, t * w:(t + 1) * w]
    return out

def wimg(wT):
    K, N = wT.shape
    nk = (K + 127) // 128
    out = np.zeros((128, nk * N), wT.dtype)
    for a in range(nk):
        rows = wT[a * 128:(a + 1) * 128]
        out[:rows.shape[0], a * N:(a + 1) * N] = rows
    return np.ascontiguousarray(out)


class Sem:
    def __init__(self, es, nc, name):
        self.h = es.enter_context(nc.semaphore(name))
        self.n = 0
    def inc(self, instr, k=1):
        instr.then_inc(self.h, k)
        self.n += k
        return self.n
    def wait(self, eng, val=None):
        eng.wait_ge(self.h, self.n if val is None else val)

def emit_ln(v, sc, S, tile_fn, n_tiles, width, stats, mvall, negm, sd, rstd, eps_tile):
    nchunk = (width + 511) // 512
    for t in range(n_tiles):
        xv = tile_fn(t)
        s0 = t * nchunk * 6
        for ch in range(nchunk):
            ist = v.bn_stats(stats[:, s0 + 6 * ch:s0 + 6 * ch + 6], xv[:, ch * 512:min(width, (ch + 1) * 512)])
            S['v'].wait(v, S['v'].inc(ist))
        ia = v.bn_aggr(mvall[:, 2 * t:2 * t + 2], stats[:, s0:s0 + 6 * nchunk])
        S['v'].wait(v, S['v'].inc(ia))
    for t in range(n_tiles):
        i1 = v.tensor_scalar(negm[:, t:t + 1], mvall[:, 2 * t:2 * t + 1], -1.0, None, ALU.mult)
    val = S['v'].inc(i1)
    S['v'].wait(sc, val)
    for t in range(n_tiles):
        i2 = sc.activation(sd[:, t:t + 1], mvall[:, 2 * t + 1:2 * t + 2], AF.Sqrt,
                           bias=eps_tile[:, 0:1], scale=1.0)
    sval = S['s'].inc(i2)
    S['s'].wait(v, sval)
    i3 = v.reciprocal(rstd[:, 0:n_tiles], sd[:, 0:n_tiles])
    rv = S['v'].inc(i3)
    S['v'].wait(v, rv)
    return rv

def emit_ln_transpose(nc, S, v, sy, sc, te, x_in, xt, xlnt, xlnT3, sidb,
                      stats, mvall, negm, sd, rstd, eps_tile, ptrb):
    ALUl = ALU
    apply_done = []
    evac_done = []
    base_ldx = S['ldx'].n
    for t in range(2):
        S['ldx'].inc(sy.dma_start(out=xt[:, (t % 2) * D:(t % 2 + 1) * D],
                                  in_=x_in[:, t * D:(t + 1) * D]), 16)
    for t in range(NT):
        tb = (t % 2) * D
        S['ldx'].wait(v, base_ldx + 16 * (t + 1))
        emit_ln(v, sc, S, lambda _t: xt[:, tb:tb + D], 1, D,
                stats[:, t * 24:], mvall[:, 2 * t:], negm[:, t:], sd[:, t:],
                rstd[:, t:], eps_tile)
        ia = v.tensor_scalar(xlnt[:, tb:tb + D], xt[:, tb:tb + D],
                             negm[:, t:t + 1], rstd[:, t:t + 1], ALUl.add, ALUl.mult)
        apply_done.append(S['v'].inc(ia))
        if t + 2 < NT:
            S['v'].wait(sy, apply_done[t])
            S['ldx'].inc(sy.dma_start(out=xt[:, ((t + 2) % 2) * D:((t + 2) % 2 + 1) * D],
                                      in_=x_in[:, (t + 2) * D:(t + 3) * D]), 16)
        for q in range(4):
            gi = t * 4 + q
            bank = ptrb[gi % 2]
            S['v'].wait(te, apply_done[-1])
            if gi >= 2:
                S['v'].wait(te, evac_done[gi - 2])
            for i in range(4):
                dc = q * 4 + i
                ins = te.transpose(bank[:, i * 128:(i + 1) * 128],
                                   xlnt[:, tb + dc * 128:tb + (dc + 1) * 128], sidb[:])
            tv = S['te'].inc(ins)
            S['te'].wait(v, tv)
            dst = xlnT3[:, 4 * q:4 * q + 4, t * 128:(t + 1) * 128]
            evac_done.append(S['v'].inc(v.tensor_copy(dst, bank[:])))
    return S['v'].n


def build_mega(nc):
    x_in = nc.dram_tensor("x_img", [128, NT * D], F32, kind="ExternalInput").ap()
    wkvv_in = nc.dram_tensor("wkvv", [128, 16 * 256], F32, kind="ExternalInput").ap()
    bkvv_in = nc.dram_tensor("bkvv", [1, 256], F32, kind="ExternalInput").ap()
    identb_in = nc.dram_tensor("identb", [128, 128], BF16, kind="ExternalInput").ap()
    ident32_in = nc.dram_tensor("ident32", [128, 128], F32, kind="ExternalInput").ap()
    w1cs_in = nc.dram_tensor("w1cs", [128, 12288], F32, kind="ExternalInput").ap()
    w2cs_in = nc.dram_tensor("w2cs", [128, 12288], F32, kind="ExternalInput").ap()
    w3rs_in = nc.dram_tensor("w3rs", [128, 12288], F32, kind="ExternalInput").ap()
    lw_in = nc.dram_tensor("lw", [128, 116 * 512], F32, kind="ExternalInput").ap()
    cos_in = nc.dram_tensor("cosd", [64, 1024], F32, kind="ExternalInput").ap()
    sin_in = nc.dram_tensor("sind", [64, 1024], F32, kind="ExternalInput").ap()
    mask_in = nc.dram_tensor("mask", [128, 512], BF16, kind="ExternalInput").ap()
    xo_out = nc.dram_tensor("xo", [128, NT * D], F32, kind="ExternalOutput").ap()
    pabs_d = nc.dram_tensor("pabs_d", [1, 8], F32, kind="Internal").ap()
    pall_d = nc.dram_tensor("pall_d", [1, 8], F32, kind="Internal",
                            addr_space="Shared").ap()
    gam_d = nc.dram_tensor("gam_d", [1, 4], F32, kind="Internal").ap()
    q1c_d = nc.dram_tensor("q1c_d", [128, 12288], BF16, kind="Internal").ap()
    q2c_d = nc.dram_tensor("q2c_d", [128, 12288], BF16, kind="Internal").ap()
    q3c_d = nc.dram_tensor("q3c_d", [128, 12288], BF16, kind="Internal").ap()
    G1 = nc.dram_tensor("G1", [1024, 12288], BF16, kind="Internal",
                        addr_space="Shared").ap()
    G2 = nc.dram_tensor("G2", [1024, 12288], BF16, kind="Internal",
                        addr_space="Shared").ap()
    G3 = nc.dram_tensor("G3", [1024, 12288], BF16, kind="Internal",
                        addr_space="Shared").ap()
    x1_d = nc.dram_tensor("x1_d", [128, NT * D], F32, kind="Internal").ap()
    x2_d = nc.dram_tensor("x2_d", [128, NT * D], F32, kind="Internal").ap()
    x3_d = nc.dram_tensor("x3_d", [128, NT * D], F32, kind="Internal").ap()
    q_d = nc.dram_tensor("q_d", [128, 16 * 1024], BF16, kind="Internal").ap()
    kc_d = nc.dram_tensor("kc_d", [128, 4096], BF16, kind="Internal").ap()
    vc_d = nc.dram_tensor("vc_d", [1024, 512], BF16, kind="Internal").ap()
    KG = nc.dram_tensor("KG", [512, 4096], BF16, kind="Internal").ap()
    VG = nc.dram_tensor("VG", [4096, 512], BF16, kind="Internal").ap()
    nb_d = nc.dram_tensor("nb_d", [1, 16], F32, kind="Internal").ap()
    lwc_d = nc.dram_tensor("lwc_d", [128, 116 * 512], BF16, kind="Internal").ap()
    GW = nc.dram_tensor("GW", [1024, 116 * 512], BF16, kind="Internal",
                        addr_space="Shared").ap()

    with contextlib.ExitStack() as es:
        sb = lambda n_, s_, d_: es.enter_context(nc.sbuf_tensor(n_, s_, d_))
        sidb = sb("sidb", [128, 128], BF16)
        eps_tile = sb("eps_tile", [128, 1], F32)
        bar_t = sb("bar_t", [1, 1], F32)
        stats = sb("stats", [128, NT * 24], F32)
        mvall = sb("mvall", [128, 16], F32)
        negm = sb("negm", [128, 8], F32)
        sd_ = sb("sd_", [128, 8], F32)
        rstd = sb("rstd", [128, 8], F32)
        SV = Sem(es, nc, "sem_v")
        SS = Sem(es, nc, "sem_s")
        STE = Sem(es, nc, "sem_te")
        SGE = Sem(es, nc, "sem_g")
        ENG = {'v': SV, 's': SS, 'te': STE, 'g': SGE}
        mk = lambda pref, keys: {**{k: Sem(es, nc, f"sem_{pref}_{k}") for k in keys}, **ENG}
        SP = mk("p", ['ld'])
        SW = mk("w", ['ld', 'st'])
        SQ = mk("q", ['ld', 'st'])
        SA = mk("a", ['ld', 'st'])
        SC_ = mk("c", ['ldx', 'ldw', 'ldw3', 'ldxr', 'st'])
        SD = mk("d", ['ldx', 'ld', 'ldw', 'st'])
        SE = mk("e", ['ld', 'ldc', 'ldxr', 'st', 'gb'])
        SF = mk("f", ['ldx', 'ldw', 'ldwd', 'ldxr', 'st'])
        ALL_SEMS = []
        for dct in (SP, SW, SQ, SA, SC_, SD, SE, SF):
            for k, s_ in dct.items():
                if s_ not in ALL_SEMS:
                    ALL_SEMS.append(s_)
        ccg = es.enter_context(nc.semaphore("ccg"))
        ccw1 = es.enter_context(nc.semaphore("ccw1"))
        ccw2 = es.enter_context(nc.semaphore("ccw2"))
        ccw3 = es.enter_context(nc.semaphore("ccw3"))
        cck = es.enter_context(nc.semaphore("cck"))
        ccv = es.enter_context(nc.semaphore("ccv"))
        ccL = es.enter_context(nc.semaphore("ccL"))

        block = es.enter_context(nc.Block())

        @block.vector
        def _(v):
            sy, sc, te, g = nc.sync, nc.scalar, nc.tensor, nc.gpsimd

            def barrier():
                for s_ in ALL_SEMS:
                    if s_ is not SV:
                        s_.wait(v)
                SV.wait(v)
                bv = SV.inc(v.memset(bar_t[:], 0.0))
                for eng in (sy, sc, te, g):
                    SV.wait(eng, bv)

            SP['ld'].inc(sy.dma_start(out=sidb[:], in_=identb_in[:, :]), 16)
            pl_all = SP['ld'].n
            pv = SV.inc(v.memset(eps_tile[:], EPS_LN))
            for eng in (te, sc, g, v):
                SP['ld'].wait(eng, pl_all)
                SV.wait(eng, pv)

            def gw_tile(u, nu):
                r, o = divmod(u, 116)
                return GW[r * 128:(r + 1) * 128, o * 512:(o + nu) * 512]

            esW = contextlib.ExitStack()
            PCW = 29 * 512
            lwf = esW.enter_context(nc.sbuf_tensor("lwfW", [128, 2 * PCW], F32))
            lwb = esW.enter_context(nc.sbuf_tensor("lwbW", [128, PCW], BF16))
            cvW = []
            for piece in range(4):
                if piece >= 2:
                    SS.wait(sy, cvW[piece - 2])
                lv = SW['ld'].inc(sy.dma_start(
                    out=lwf[:, (piece % 2) * PCW:(piece % 2 + 1) * PCW],
                    in_=lw_in[:, piece * PCW:(piece + 1) * PCW]), 16)
                SW['ld'].wait(sc, lv)
                if piece >= 1:
                    SW['st'].wait(sc, 16 * piece)
                ic_ = sc.activation(lwb[:], lwf[:, (piece % 2) * PCW:(piece % 2 + 1) * PCW],
                                    AF.Copy, bias=0.0, scale=1.0)
                cv = SS.inc(ic_)
                cvW.append(cv)
                SS.wait(g, cv)
                SW['st'].inc(g.dma_start(out=lwc_d[:, piece * PCW:(piece + 1) * PCW],
                                         in_=lwb[:]), 16)
            esQ1 = contextlib.ExitStack()
            sbQ1 = lambda n_, s_, d_: esQ1.enter_context(nc.sbuf_tensor(n_, s_, d_))
            ws = sbQ1("wsQ", [128, 6144], F32)
            absacc = sbQ1("absaccQ", [128, 8], F32)
            mcol = sbQ1("mcolQ", [128, 4], F32)
            pabs_sb = sbQ1("pabsQ", [1, 8], F32)
            pieces = [(w1cs_in, 0, 0), (w1cs_in, 6144, 1),
                      (w2cs_in, 0, 2), (w2cs_in, 6144, 3),
                      (w3rs_in, 0, 4), (w3rs_in, 6144, 5)]
            red_done = []
            for pi, (srcap, c0, col) in enumerate(pieces):
                if pi > 0:
                    SV.wait(sy, red_done[-1])
                lv = SQ['ld'].inc(sy.dma_start(out=ws[:], in_=srcap[:, c0:c0 + 6144]), 16)
                SQ['ld'].wait(v, lv)
                ins = v.tensor_reduce(absacc[:, col:col + 1], ws[:],
                                      axis=AX.X, op=ALU.add, apply_absolute_value=True)
                red_done.append(SV.inc(ins))
                SV.wait(v, red_done[-1])
            for m in range(3):
                ins = v.tensor_tensor(mcol[:, m:m + 1], absacc[:, 2 * m:2 * m + 1],
                                      absacc[:, 2 * m + 1:2 * m + 2], op=ALU.add)
            SV.inc(ins)
            mz = SV.inc(v.memset(pabs_sb[:], 0.0))
            SV.wait(g, SV.n)
            gv = SGE.inc(g.tensor_reduce(pabs_sb[0:1, 0:3], mcol[:, 0:3],
                                         axis=AX.C, op=ALU.add))
            SGE.wait(g, gv)
            sv_ = SQ['st'].inc(g.dma_start(out=pabs_d[0:1, :], in_=pabs_sb[:]), 16)
            SQ['st'].wait(g, sv_)
            g.collective_compute("AllReduce", ALU.add,
                                 replica_groups=[list(range(NCORES))],
                                 ins=[pabs_d.opt()], outs=[pall_d.opt()]).then_inc(ccg)
            SW['st'].wait(g)
            g.collective_compute("AllGather", ALU.bypass,
                                 replica_groups=[list(range(NCORES))],
                                 ins=[lwc_d.opt()], outs=[GW.opt()]).then_inc(ccL)
            esQ1.close()
            esW.close()
            barrier()

            esA = contextlib.ExitStack()
            sbA = lambda n_, s_, d_: esA.enter_context(nc.sbuf_tensor(n_, s_, d_))
            xs = sbA("xsA", [128, NT * D], F32)
            xln = sbA("xlnA", [128, NT * D], BF16)
            xlnT = sbA("xlnTA", [128, 16 * 1024], BF16)
            wkv32 = sbA("wkv32A", [128, 4096], F32)
            wkvb = sbA("wkvbA", [128, 4096], BF16)
            bkv32 = sbA("bkv32A", [1, 256], F32)
            bkvb = sbA("bkvbA", [1, 256], BF16)
            vfm = sbA("vfmA", [128, 2048], F32)
            vT = sbA("vTA", [128, NT * 256], F32)
            lnv = sbA("lnvA", [128, 2 * 256], F32)
            sid32 = sbA("sid32A", [128, 128], F32)
            ones_row = sbA("onesrA", [1, 512], BF16)
            ptrbA = [esA.enter_context(nc.psum_tensor("ptrbA0", [128, 512], BF16)),
                     esA.enter_context(nc.psum_tensor("ptrbA1", [128, 512], BF16))]
            pvb = [esA.enter_context(nc.psum_tensor("pvbA0", [128, 512], F32)),
                   esA.enter_context(nc.psum_tensor("pvbA1", [128, 512], F32))]
            ptr32A = [esA.enter_context(nc.psum_tensor("ptr32A0", [128, 512], F32)),
                      esA.enter_context(nc.psum_tensor("ptr32A1", [128, 512], F32))]
            xlnT3 = xlnT[:].rearrange("p (a c) -> p a c", c=1024)
            vT3 = vT[:].rearrange("p (t c) -> p t c", c=256)
            SA['ld'].inc(sy.dma_start(out=xs[:], in_=x_in[:, :]), 16)
            SA['ld'].inc(sy.dma_start(out=wkv32[:], in_=wkvv_in[:, :]), 16)
            SA['ld'].inc(sy.dma_start(out=bkv32[:], in_=bkvv_in[:, :]), 16)
            SA['ld'].inc(sy.dma_start(out=sid32[:], in_=ident32_in[:, :]), 16)
            ld_all = SA['ld'].n
            orv = SV.inc(v.memset(ones_row[:], 1.0))
            SV.wait(te, orv)
            SA['ld'].wait(sc, ld_all)
            sc.activation(wkvb[:], wkv32[:], AF.Copy, bias=0.0, scale=1.0)
            cast_done = SS.inc(sc.activation(bkvb[:], bkv32[:], AF.Copy, bias=0.0, scale=1.0))
            SA['ld'].wait(v, 16)
            emit_ln(v, sc, {'v': SV, 's': SS}, lambda t: xs[:, t * D:(t + 1) * D], NT, D,
                    stats, mvall, negm, sd_, rstd, eps_tile)
            for t in range(NT):
                i = v.tensor_scalar(xln[:, t * D:(t + 1) * D], xs[:, t * D:(t + 1) * D],
                                    negm[:, t:t + 1], rstd[:, t:t + 1], ALU.add, ALU.mult)
            xln_done = SV.inc(i)
            SA['ld'].wait(te, ld_all)
            SV.wait(te, xln_done)
            evac_vals = []
            gi = 0
            for t in range(NT):
                for q in range(4):
                    bank = ptrbA[gi % 2]
                    if gi >= 2:
                        SV.wait(te, evac_vals[gi - 2])
                    for i in range(4):
                        dc = q * 4 + i
                        ins = te.transpose(bank[:, i * 128:(i + 1) * 128],
                                           xln[:, t * D + dc * 128:t * D + (dc + 1) * 128],
                                           sidb[:])
                    tv = STE.inc(ins)
                    STE.wait(v, tv)
                    dst = xlnT3[:, 4 * q:4 * q + 4, t * 128:(t + 1) * 128]
                    evac_vals.append(SV.inc(v.tensor_copy(dst, bank[:])))
                    gi += 1
            xlnT_done = SV.n
            SS.wait(te, cast_done)
            SV.wait(te, xlnT_done)
            vp_vals = []
            for idx, (vc, tb) in enumerate([(a, b) for a in range(2) for b in range(2)]):
                bank = pvb[idx % 2]
                if idx >= 2:
                    SV.wait(te, vp_vals[idx - 2])
                for dc in range(16):
                    te.matmul(bank[:], wkvb[:, dc * 256 + vc * 128:dc * 256 + (vc + 1) * 128],
                              xlnT3[:, dc, tb * 512:(tb + 1) * 512],
                              start=(dc == 0), stop=False)
                ins = te.matmul(bank[:], bkvb[:, vc * 128:(vc + 1) * 128], ones_row[:],
                                start=False, stop=True)
                tv = STE.inc(ins)
                STE.wait(v, tv)
                vp_vals.append(SV.inc(v.tensor_copy(
                    vfm[:, vc * 1024 + tb * 512:vc * 1024 + (tb + 1) * 512], bank[:])))
            SV.wait(te)
            vt_vals = []
            for grp in range(4):
                vc, th = divmod(grp, 2)
                bank = ptr32A[grp % 2]
                if grp >= 2:
                    SV.wait(te, vt_vals[grp - 2])
                for i in range(4):
                    t = th * 4 + i
                    ins = te.transpose(bank[:, i * 128:(i + 1) * 128],
                                       vfm[:, vc * 1024 + t * 128:vc * 1024 + (t + 1) * 128],
                                       sid32[:])
                tv = STE.inc(ins)
                STE.wait(v, tv)
                dst = vT3[:, th * 4:th * 4 + 4, vc * 128:(vc + 1) * 128]
                vt_vals.append(SV.inc(v.tensor_copy(dst, bank[:])))
            emit_ln(v, sc, {'v': SV, 's': SS}, lambda t: vT[:, t * 256:(t + 1) * 256], NT, 256,
                    stats, mvall, negm, sd_, rstd, eps_tile)
            for t in range(NT):
                lb = (t % 2) * 256
                v.tensor_scalar(lnv[:, lb:lb + 256], vT[:, t * 256:(t + 1) * 256],
                                negm[:, t:t + 1], rstd[:, t:t + 1], ALU.add, ALU.mult)
                for gq in range(4):
                    for h in range(8):
                        c0 = t * D + gq * 512 + h * 64
                        i = v.tensor_tensor(xs[:, c0:c0 + 64], xs[:, c0:c0 + 64],
                                            lnv[:, lb + gq * 64:lb + gq * 64 + 64], op=ALU.add)
            x_done = SV.inc(i)
            SV.wait(g, x_done)
            SA['st'].inc(g.dma_start(out=x1_d[:, :], in_=xs[:]), 16)
            SA['st'].wait(g)
            esA.close()
            barrier()

            esQ2 = contextlib.ExitStack()
            sbQ2 = lambda n_, s_, d_: esQ2.enter_context(nc.sbuf_tensor(n_, s_, d_))
            ws2 = sbQ2("ws2Q", [128, 2 * 6144], F32)
            wi = sbQ2("wiQ", [128, 2 * 6144], F32)
            wqb = sbQ2("wqbQ", [128, 2 * 6144], BF16)
            pall_sb = sbQ2("pallQ", [1, 8], F32)
            gamr = sbQ2("gamrQ", [1, 4], F32)
            rcol = sbQ2("rcolQ", [128, 4], F32)
            sy.wait_ge(ccg, 1)
            lv = SQ['ld'].inc(sy.dma_start(out=pall_sb[:], in_=pall_d[0:1, :]), 16)
            SQ['ld'].wait(v, lv)
            i = v.tensor_scalar(gamr[:, 0:3], pall_sb[:, 0:3], 1.0 / (H0 * D), 1e-8,
                                ALU.mult, ALU.add)
            SV.wait(v, SV.inc(i))
            i = v.reciprocal(gamr[:, 0:3], gamr[:, 0:3])
            iv = SV.inc(i)
            SV.wait(g, iv)
            gv2 = SQ['st'].inc(g.dma_start(out=gam_d[0:1, 0:3], in_=gamr[:, 0:3]), 16)
            SQ['st'].wait(g, gv2)
            gv2 = SQ['st'].inc(g.dma_start(
                out=rcol[:, 0:3],
                in_=gam_d[0:1, 0:3].squeeze(0).partition_broadcast(128)), 16)
            SQ['st'].wait(v, gv2)
            SQL = [Sem(es, nc, "sem_q2ld0"), Sem(es, nc, "sem_q2ld1")]
            SQS = [Sem(es, nc, "sem_q2st0"), Sem(es, nc, "sem_q2st1")]
            QSRC = [(w1cs_in, q1c_d), (w2cs_in, q2c_d), (w3rs_in, q3c_d)]
            qv_l = []
            for it in range(6):
                m, half = divmod(it, 2)
                src, dst = QSRC[m]
                b = it % 2
                sl = b * 6144
                if it >= 2:
                    SV.wait(sy, qv_l[it - 2])
                lv = SQL[b].inc(sy.dma_start(out=ws2[:, sl:sl + 6144],
                                             in_=src[:, half * 6144:(half + 1) * 6144]), 16)
                SQL[b].wait(v, lv)
                if it >= 2:
                    SQS[b].wait(v)
                v.tensor_scalar(wi[:, sl:sl + 6144], ws2[:, sl:sl + 6144],
                                rcol[:, m:m + 1], 0.5, ALU.mult, ALU.is_ge)
                v.tensor_scalar(ws2[:, sl:sl + 6144], ws2[:, sl:sl + 6144],
                                rcol[:, m:m + 1], -0.5, ALU.mult, ALU.is_le)
                i = v.tensor_tensor(wqb[:, sl:sl + 6144], wi[:, sl:sl + 6144],
                                    ws2[:, sl:sl + 6144], op=ALU.subtract)
                qv_l.append(SV.inc(i))
                SV.wait(g, qv_l[-1])
                SQS[b].inc(g.dma_start(out=dst[:, half * 6144:(half + 1) * 6144],
                                       in_=wqb[:, sl:sl + 6144]), 16)
            SQS[0].wait(g)
            SQS[1].wait(g)
            for (cin, cout, ch) in [(q1c_d, G1, ccw1), (q2c_d, G2, ccw2), (q3c_d, G3, ccw3)]:
                g.collective_compute("AllGather", ALU.bypass,
                                     replica_groups=[list(range(NCORES))],
                                     ins=[cin.opt()], outs=[cout.opt()]).then_inc(ch)
            esQ2.close()
            barrier()

            esC = contextlib.ExitStack()
            sbC = lambda n_, s_, d_: esC.enter_context(nc.sbuf_tensor(n_, s_, d_))
            xtC = sbC("xtC", [128, 2 * D], F32)
            xlntC = sbC("xlntC", [128, 2 * D], BF16)
            xlnTC = sbC("xlnTC", [128, 16 * 1024], BF16)
            hidT = sbC("hidTC", [128, H0C * 1024], BF16)
            w1t = sbC("w1tC", [128, 2 * 2048], BF16)
            w2t = sbC("w2tC", [128, 2 * 2048], BF16)
            w3t = sbC("w3tC", [128, 2 * 2048], BF16)
            sg = sbC("sgC", [128, 2 * 512], F32)
            xoutC = sbC("xoutC", [128, 2 * D], F32)
            xin2C = sbC("xin2C", [128, 2 * D], F32)
            psC = contextlib.ExitStack()
            ptrbC = [psC.enter_context(nc.psum_tensor("ptrbC0", [128, 512], BF16)),
                     psC.enter_context(nc.psum_tensor("ptrbC1", [128, 512], BF16))]
            pf = [psC.enter_context(nc.psum_tensor(f"pfC{i}", [128, 512], F32)) for i in range(4)]
            xlnT3C = xlnTC[:].rearrange("p (a c) -> p a c", c=1024)
            sy.wait_ge(ccw1, 1)
            sy.wait_ge(ccw2, 1)
            sy.wait_ge(ccw3, 1)
            emit_ln_transpose(nc, SC_, v, sy, sc, te, x1_d, xtC, xlntC, xlnT3C, sidb,
                              stats, mvall, negm, sd_, rstd, eps_tile, ptrbC)
            xlnT_doneC = SV.n
            SV.wait(te, xlnT_doneC)
            def g1tile(hc):
                r, hcl = divmod(hc, 6)
                return (G1[r * 128:(r + 1) * 128, hcl * 2048:(hcl + 1) * 2048],
                        G2[r * 128:(r + 1) * 128, hcl * 2048:(hcl + 1) * 2048])
            grp_sg = []
            grp_hid = []
            grp_mm = []
            for gi2 in range(2 * H0C):
                hc, tbk = divmod(gi2, 2)
                if tbk == 0:
                    if hc >= 2:
                        STE.wait(sy, grp_mm[(hc - 2) * 2 + 1][1])
                    s1, s2 = g1tile(hc)
                    SC_['ldw'].inc(sy.dma_start(out=w1t[:, (hc % 2) * 2048:(hc % 2 + 1) * 2048],
                                                in_=s1), 16)
                    SC_['ldw'].inc(sy.dma_start(out=w2t[:, (hc % 2) * 2048:(hc % 2 + 1) * 2048],
                                                in_=s2), 16)
                b1, b2 = pf[(gi2 % 2) * 2], pf[(gi2 % 2) * 2 + 1]
                wof = (hc % 2) * 2048
                SC_['ldw'].wait(te, 32 * (hc + 1))
                if gi2 >= 2:
                    SV.wait(te, grp_hid[gi2 - 2])
                for dc in range(16):
                    i1 = te.matmul(b1[:], w1t[:, wof + dc * 128:wof + (dc + 1) * 128],
                                   xlnT3C[:, dc, tbk * 512:(tbk + 1) * 512],
                                   start=(dc == 0), stop=(dc == 15))
                p1v = STE.inc(i1)
                for dc in range(16):
                    i2 = te.matmul(b2[:], w2t[:, wof + dc * 128:wof + (dc + 1) * 128],
                                   xlnT3C[:, dc, tbk * 512:(tbk + 1) * 512],
                                   start=(dc == 0), stop=(dc == 15))
                p2v = STE.inc(i2)
                grp_mm.append((p1v, p2v))
                STE.wait(sc, p1v)
                if gi2 >= 2:
                    SV.wait(sc, grp_hid[gi2 - 2])
                isg = sc.activation(sg[:, (gi2 % 2) * 512:(gi2 % 2 + 1) * 512], b1[:],
                                    AF.Sigmoid, bias=0.0, scale=1.0)
                grp_sg.append(SS.inc(isg))
                STE.wait(v, p2v)
                SS.wait(v, grp_sg[-1])
                ih = v.scalar_tensor_tensor(
                    hidT[:, hc * 1024 + tbk * 512:hc * 1024 + (tbk + 1) * 512],
                    b2[:], 1.0, sg[:, (gi2 % 2) * 512:(gi2 % 2 + 1) * 512],
                    ALU.mult, ALU.mult)
                grp_hid.append(SV.inc(ih))
            hid_done = SV.n
            psC.close()
            px = [esC.enter_context(nc.psum_tensor(f"pxC{i}", [128, 512], F32)) for i in range(8)]
            SV.wait(te, hid_done)
            pair_apply = []
            w3_cons = []
            for pair in range(4):
                if pair >= 1:
                    SV.wait(sy, pair_apply[pair - 1])
                for i in range(2):
                    t = pair * 2 + i
                    SC_['ldxr'].inc(sy.dma_start(
                        out=xin2C[:, i * D:(i + 1) * D],
                        in_=x1_d[:, t * D:(t + 1) * D]), 16)
                for hc in range(H0C):
                    di = pair * H0C + hc
                    if di >= 2:
                        STE.wait(sy, w3_cons[di - 2])
                    r3, a3 = divmod(hc, 6)
                    SC_['ldw3'].inc(sy.dma_start(out=w3t[:, (di % 2) * 2048:(di % 2 + 1) * 2048],
                                                 in_=G3[r3 * 128:(r3 + 1) * 128,
                                                        a3 * 2048:(a3 + 1) * 2048]), 16)
                    SC_['ldw3'].wait(te, 16 * (di + 1))
                    if hc == 0 and pair > 0:
                        SV.wait(te, pair_apply[pair - 1])
                    for tt in range(2):
                        t = pair * 2 + tt
                        for dg in range(4):
                            iw = te.matmul(px[tt * 4 + dg][:],
                                           hidT[:, hc * 1024 + t * 128:hc * 1024 + (t + 1) * 128],
                                           w3t[:, (di % 2) * 2048 + dg * 512:(di % 2) * 2048 + (dg + 1) * 512],
                                           start=(hc == 0), stop=(hc == H0C - 1))
                    w3_cons.append(STE.inc(iw))
                mm_done = w3_cons[-1]
                STE.wait(v, mm_done)
                STE.wait(sc, mm_done)
                for tt in range(2):
                    t = pair * 2 + tt
                    s0 = t * 24
                    for dg in range(4):
                        ist = v.bn_stats(stats[:, s0 + 6 * dg:s0 + 6 * dg + 6], px[tt * 4 + dg][:])
                        SV.wait(v, SV.inc(ist))
                    ia = v.bn_aggr(mvall[:, 2 * t:2 * t + 2], stats[:, s0:s0 + 24])
                    SV.wait(v, SV.inc(ia))
                    i1 = v.tensor_scalar(negm[:, t:t + 1], mvall[:, 2 * t:2 * t + 1], -1.0, None, ALU.mult)
                    nv = SV.inc(i1)
                    SV.wait(sc, nv)
                    i2 = sc.activation(sd_[:, t:t + 1], mvall[:, 2 * t + 1:2 * t + 2], AF.Sqrt,
                                       bias=eps_tile[:, 0:1], scale=1.0)
                    sv = SS.inc(i2)
                    SS.wait(v, sv)
                    i3 = v.reciprocal(rstd[:, t:t + 1], sd_[:, t:t + 1])
                    SV.wait(v, SV.inc(i3))
                    for dg in range(4):
                        v.tensor_scalar(xoutC[:, tt * D + dg * 512:tt * D + (dg + 1) * 512],
                                        px[tt * 4 + dg][:], negm[:, t:t + 1], rstd[:, t:t + 1],
                                        ALU.add, ALU.mult)
                    SC_['ldxr'].wait(v, 16 * (pair * 2 + tt + 1))
                    iadd = v.tensor_tensor(xoutC[:, tt * D:(tt + 1) * D],
                                           xoutC[:, tt * D:(tt + 1) * D],
                                           xin2C[:, tt * D:(tt + 1) * D],
                                           op=ALU.add)
                    av = SV.inc(iadd)
                    SV.wait(g, av)
                    SC_['st'].inc(g.dma_start(out=x2_d[:, t * D:(t + 1) * D],
                                              in_=xoutC[:, tt * D:(tt + 1) * D]), 16)
                pair_apply.append(SV.n)
                SC_['st'].wait(v)
            SC_['st'].wait(g)
            esC.close()
            barrier()

            esD = contextlib.ExitStack()
            sbD = lambda n_, s_, d_: esD.enter_context(nc.sbuf_tensor(n_, s_, d_))
            xtD = sbD("xtD", [128, 2 * D], F32)
            xlntD = sbD("xlntD", [128, 2 * D], BF16)
            xlnTD = sbD("xlnTD", [128, 16 * 1024], BF16)
            wsb = sbD("wsbD", [128, 2 * 2048], BF16)
            qf = sbD("qfD", [128, 2 * 1024], F32)
            ct = sbD("ctD", [128, 1024], F32)
            st_ = sbD("stD", [128, 1024], F32)
            tt1 = sbD("tt1D", [128, 2 * 1024], F32)
            tt2 = sbD("tt2D", [128, 2 * 1024], F32)
            qsw = sbD("qswD", [128, 2 * 1024], F32)
            sgn = sbD("sgnD", [128, 1], F32)
            qro = sbD("qroD", [128, 16 * 1024], BF16)
            kro = sbD("kroD", [128, 4 * 1024], BF16)
            vwb = sbD("vwbD", [128, 16 * 512], BF16)
            vt = sbD("vtD", [128, 8 * 512], BF16)
            ptrbD = [esD.enter_context(nc.psum_tensor("ptrbD0", [128, 512], BF16)),
                     esD.enter_context(nc.psum_tensor("ptrbD1", [128, 512], BF16))]
            pp = [esD.enter_context(nc.psum_tensor(f"ppD{i}", [128, 512], F32)) for i in range(4)]
            xlnT3D = xlnTD[:].rearrange("p (a c) -> p a c", c=1024)
            v.memset(sgn[0:64, :], -1.0)
            v.memset(sgn[64:128, :], 1.0)
            SD['ld'].inc(sy.dma_start(out=ct[0:64, :], in_=cos_in[:, :]), 16)
            SD['ld'].inc(sy.dma_start(out=ct[64:128, :], in_=cos_in[:, :]), 16)
            SD['ld'].inc(sy.dma_start(out=st_[0:64, :], in_=sin_in[:, :]), 16)
            SD['ld'].inc(sy.dma_start(out=st_[64:128, :], in_=sin_in[:, :]), 16)
            ld_allD = SD['ld'].n
            emit_ln_transpose(nc, SD, v, sy, sc, te, x2_d, xtD, xlntD, xlnT3D, sidb,
                              stats, mvall, negm, sd_, rstd, eps_tile, ptrbD)
            xlnT_doneD = SV.n
            SV.wait(te, xlnT_doneD)
            SD['ld'].wait(v, ld_allD)
            sy.wait_ge(ccL, 1)
            rope_done = []
            evq = []
            mmq = []
            for ci in range(20):
                is_q = ci < 16
                u0 = 4 * ci if is_q else 64 + 4 * (ci - 16)
                dstf = qro if is_q else kro
                dof = (ci if is_q else ci - 16) * 1024
                wb = (ci % 2)
                if ci >= 2:
                    STE.wait(sy, mmq[2 * (ci - 2) + 1])
                SD['ldw'].inc(sy.dma_start(out=wsb[:, wb * 2048:(wb + 1) * 2048],
                                           in_=gw_tile(u0, 4)), 16)
                for tb in range(2):
                    pb = pp[(2 * ci + tb) % 4]
                    if 2 * ci + tb >= 4:
                        SS.wait(te, evq[2 * ci + tb - 4])
                    SD['ldw'].wait(te, 16 * (ci + 1))
                    for dc in range(16):
                        im = te.matmul(pb[:], wsb[:, wb * 2048 + dc * 128:wb * 2048 + (dc + 1) * 128],
                                       xlnT3D[:, dc, tb * 512:(tb + 1) * 512],
                                       start=(dc == 0), stop=(dc == 15))
                    mmq.append(STE.inc(im))
                    STE.wait(sc, mmq[-1])
                    if ci >= 2 and tb == 0:
                        SV.wait(sc, rope_done[ci - 2])
                    qb = (ci % 2) * 1024
                    iev = sc.activation(qf[:, qb + tb * 512:qb + (tb + 1) * 512], pb[:],
                                        AF.Copy, bias=0.0, scale=(QSCALE if is_q else 1.0))
                    evq.append(SS.inc(iev))
                SS.wait(v, evq[-1])
                qb = (ci % 2) * 1024
                b = (ci % 2) * 1024
                v.tensor_copy(qsw[0:64, b:b + 1024], qf[64:128, qb:qb + 1024])
                v.tensor_copy(qsw[64:128, b:b + 1024], qf[0:64, qb:qb + 1024])
                v.tensor_tensor(tt1[:, b:b + 1024], qf[:, qb:qb + 1024], ct[:], op=ALU.mult)
                v.tensor_tensor(tt2[:, b:b + 1024], qsw[:, b:b + 1024], st_[:], op=ALU.mult)
                ir = v.scalar_tensor_tensor(dstf[:, dof:dof + 1024], tt2[:, b:b + 1024],
                                            sgn[:, 0:1], tt1[:, b:b + 1024],
                                            ALU.mult, ALU.add)
                rope_done.append(SV.inc(ir))
            SD['ldw'].inc(sy.dma_start(out=vwb[:], in_=gw_tile(144, 16)), 16)
            ldw_allD = SD['ldw'].n
            SV.wait(te)
            SS.wait(te)
            SD['ldw'].wait(te, ldw_allD)
            evv = []
            for t in range(NT):
                pb = pp[t % 4]
                if t >= 4:
                    SV.wait(te, evv[t - 4])
                for dc in range(16):
                    im = te.matmul(pb[:], xlnT3D[:, dc, t * 128:(t + 1) * 128],
                                   vwb[:, dc * 512:(dc + 1) * 512],
                                   start=(dc == 0), stop=(dc == 15))
                mv_ = STE.inc(im)
                STE.wait(v, mv_)
                iev = v.tensor_copy(vt[:, t * 512:(t + 1) * 512], pb[:])
                evv.append(SV.inc(iev))
            SV.wait(g)
            SD['st'].inc(g.dma_start(out=q_d[:, :], in_=qro[:]), 16)
            SD['st'].inc(g.dma_start(out=kc_d[:, :], in_=kro[:]), 16)
            SD['st'].inc(g.dma_start(out=vc_d.rearrange("(t p) d -> p t d", p=128),
                                     in_=vt[:].rearrange("p (t d) -> p t d", d=512)), 16)
            SD['st'].wait(g)
            groups2 = [[0, 1, 2, 3], [4, 5, 6, 7]]
            g.collective_compute("AllGather", ALU.bypass, replica_groups=groups2,
                                 ins=[kc_d.opt()], outs=[KG.opt()]).then_inc(cck)
            g.collective_compute("AllGather", ALU.bypass, replica_groups=groups2,
                                 ins=[vc_d.opt()], outs=[VG.opt()]).then_inc(ccv)
            esD.close()
            barrier()

            esE = contextlib.ExitStack()
            sbE = lambda n_, s_, d_: esE.enter_context(nc.sbuf_tensor(n_, s_, d_))
            attn_tok = sbE("attokE", [128, NT * 2048], BF16)
            negmb = sbE("negmbE", [128, 16], F32)
            es1E = contextlib.ExitStack()
            sb1E = lambda n_, s_, d_: es1E.enter_context(nc.sbuf_tensor(n_, s_, d_))
            k_sb = sb1E("ksbE", [128, 4 * T], BF16)
            v_sb = sb1E("vsbE", [128, 32 * 512], BF16)
            q_sb = sb1E("qsbE", [128, 16 * 1024], BF16)
            ksq = sb1E("ksqE", [128, 2 * T], BF16)
            P_sb = sb1E("PsbE", [128, 2 * 512], BF16)
            msk = sb1E("mskE", [128, 512], BF16)
            att_s = sb1E("attsE", [128, 512], BF16)
            l_sb = sb1E("lsbE", [1, 512], F32)
            rl_col = sb1E("rlcolE", [128, 4], F32)
            nm_row = sb1E("nmrowE", [1, 40], F32)
            kn = sb1E("knE", [1, 4], F32)
            qn = sb1E("qnE", [1, 16], F32)
            knr = sb1E("knrE", [1, 16], F32)
            nmr = sb1E("nmrE", [1, 16], F32)
            sid32 = sb1E("sid32E", [128, 128], F32)
            ones_cb = sb1E("onescbE", [128, 1], BF16)
            ps1E = contextlib.ExitStack()
            pst = [ps1E.enter_context(nc.psum_tensor(f"pstE{i}", [128, 512], F32)) for i in range(2)]
            ppv = ps1E.enter_context(nc.psum_tensor("ppvE", [128, 512], F32))
            pl = ps1E.enter_context(nc.psum_tensor("plE", [1, 512], F32))
            ptrE = ps1E.enter_context(nc.psum_tensor("ptrE", [128, 512], BF16))
            ptr32E = ps1E.enter_context(nc.psum_tensor("ptr32E", [128, 128], F32))
            pn = ps1E.enter_context(nc.psum_tensor("pnE", [1, 512], F32))
            SE['ld'].inc(sy.dma_start(out=q_sb[:], in_=q_d[:, :]), 16)
            sy.wait_ge(cck, 1)
            for gq in range(4):
                k_sb4 = k_sb[:, gq * T:(gq + 1) * T].rearrange(
                    "p (j k c) -> p j k c", k=4, c=128)
                for kk in range(4):
                    SE['ld'].inc(sy.dma_start(
                        out=k_sb4[:, :, kk, :],
                        in_=KG[kk * 128:(kk + 1) * 128,
                               gq * 1024:(gq + 1) * 1024].rearrange(
                            "p (j c) -> p j c", c=128)), 16)
            sy.wait_ge(ccv, 1)
            v_sb4 = v_sb[:].rearrange("p (j k d) -> p j k d", k=4, d=512)
            for kk in range(4):
                SE['ld'].inc(sy.dma_start(
                    out=v_sb4[:, :, kk, :],
                    in_=VG[kk * 1024:(kk + 1) * 1024, :].rearrange(
                        "(j p) d -> p j d", p=128)), 16)
            SE['ld'].inc(sy.dma_start(out=msk[:], in_=mask_in[:, :]), 16)
            SE['ld'].inc(sy.dma_start(out=sid32[:], in_=ident32_in[:, :]), 16)
            ld_allE = SE['ld'].n
            ocv = SV.inc(v.memset(ones_cb[:], 1.0))
            SV.wait(te, ocv)
            SE['ld'].wait(v, ld_allE)
            SE['ld'].wait(sc, ld_allE)
            SE['ld'].wait(te, ld_allE)
            for gi in range(4):
                if gi >= 2:
                    SV.wait(sc)
                isq = sc.activation(ksq[:, (gi % 2) * T:(gi % 2 + 1) * T],
                                    k_sb[:, gi * T:(gi + 1) * T],
                                    AF.Square, bias=0.0, scale=1.0)
                sqv = SS.inc(isq)
                SS.wait(te, sqv)
                for c in range(8):
                    imm = te.matmul(pn[:], ones_cb[:],
                                    ksq[:, (gi % 2) * T + c * 512:(gi % 2) * T + (c + 1) * 512],
                                    start=True, stop=True)
                    tv = STE.inc(imm)
                    STE.wait(v, tv)
                    ir = v.tensor_reduce(nm_row[:, 8 * (gi % 2) + c:8 * (gi % 2) + c + 1],
                                         pn[0:1, :], axis=AX.X, op=ALU.max)
                    SV.wait(v, SV.inc(ir))
                    SV.wait(te)
                ir = v.tensor_reduce(kn[:, gi:gi + 1], nm_row[:, 8 * (gi % 2):8 * (gi % 2) + 8],
                                     axis=AX.X, op=ALU.max)
                SV.wait(v, SV.inc(ir))
            for h in range(NHEAD):
                isq = sc.activation(ksq[:, (h % 4) * 1024:(h % 4) * 1024 + 1024],
                                    q_sb[:, h * 1024:(h + 1) * 1024], AF.Square,
                                    bias=0.0, scale=1.0)
                hv = SS.inc(isq)
                SS.wait(te, hv)
                for c in range(2):
                    imm = te.matmul(pn[:], ones_cb[:],
                                    ksq[:, (h % 4) * 1024 + c * 512:(h % 4) * 1024 + (c + 1) * 512],
                                    start=True, stop=True)
                    tv = STE.inc(imm)
                    STE.wait(v, tv)
                    ir = v.tensor_reduce(nm_row[:, 32 + c:33 + c], pn[0:1, :], axis=AX.X, op=ALU.max)
                    SV.wait(v, SV.inc(ir))
                    SV.wait(te)
                ir = v.tensor_reduce(qn[:, h:h + 1], nm_row[:, 32:34], axis=AX.X, op=ALU.max)
                SV.wait(v, SV.inc(ir))
                SV.wait(sc)
            vv = SV.n
            SV.wait(sc, vv)
            sc.activation(kn[:], kn[:], AF.Sqrt, bias=eps_tile[0:1, 0:1], scale=1.0)
            iq = sc.activation(qn[:], qn[:], AF.Sqrt, bias=eps_tile[0:1, 0:1], scale=1.0)
            sv = SS.inc(iq)
            SS.wait(v, sv)
            for h in range(NHEAD):
                v.tensor_copy(knr[:, h:h + 1], kn[:, h // 4:h // 4 + 1])
            iv = v.tensor_tensor(nmr[:], qn[:], knr[:], op=ALU.mult)
            SV.inc(iv)
            iv2 = v.tensor_scalar(nmr[:], nmr[:], -1.05, None, ALU.mult)
            nv = SV.inc(iv2)
            SV.wait(g, nv)
            gv = SE['gb'].inc(g.dma_start(out=nb_d[0:1, :], in_=nmr[:]), 16)
            SE['gb'].wait(g, gv)
            gv = SE['gb'].inc(g.dma_start(out=negmb[:],
                                          in_=nb_d[0:1, :].squeeze(0).partition_broadcast(128)), 16)
            SE['gb'].wait(sc, gv)
            SE['gb'].wait(v, gv)
            exp_done = {}
            pv_done = {}
            row_free = []
            prev_pv = None
            for h in range(NHEAD):
                gi = h // 4
                for qd in range(2):
                    nk = 16 * (qd + 1)
                    q_ap = q_sb[:, h * 1024 + qd * 512:h * 1024 + (qd + 1) * 512]
                    qk_inc = {}
                    for ktc in range(-1, nk):
                        kn_ = ktc + 1
                        if kn_ < nk:
                            stb_n = pst[kn_ % 2]
                            if kn_ >= 2:
                                SS.wait(te, exp_done[(h, qd, kn_ - 2)])
                            if (h, qd) != (0, 0) and kn_ == 0:
                                SV.wait(te, row_free[-1])
                            iqk = te.matmul(
                                stb_n[:],
                                k_sb[:, gi * T + kn_ * 128:gi * T + (kn_ + 1) * 128],
                                q_ap, start=True, stop=True)
                            qk_inc[kn_] = STE.inc(iqk)
                        if ktc < 0:
                            continue
                        jls = ktc // 4 - qd * 4
                        slot = (ktc % 2) * 512
                        stb = pst[ktc % 2]
                        pfree = pv_done[(h, qd, ktc - 2)] if ktc >= 2 else prev_pv
                        STE.wait(sc, qk_inc[ktc])
                        if pfree is not None:
                            STE.wait(sc, pfree)
                        iex = sc.activation(P_sb[:, slot:slot + 512], stb[:],
                                            AF.Exp, bias=negmb[:, h:h + 1], scale=1.0)
                        exp_done[(h, qd, ktc)] = SS.inc(iex)
                        pv_wait_v = None
                        if jls >= 0:
                            tq = ktc % 4
                            SS.wait(v, exp_done[(h, qd, ktc)])
                            if pfree is not None:
                                STE.wait(v, pfree)
                            imm_ = v.tensor_tensor(
                                P_sb[:, slot + jls * 128:slot + (jls + 1) * 128],
                                P_sb[:, slot + jls * 128:slot + (jls + 1) * 128],
                                msk[:, tq * 128:(tq + 1) * 128], op=ALU.mult)
                            pv_wait_v = SV.inc(imm_)
                            if jls > 0:
                                imz = v.memset(P_sb[:, slot:slot + jls * 128], 0.0)
                                pv_wait_v = SV.inc(imz)
                        SS.wait(te, exp_done[(h, qd, ktc)])
                        if pv_wait_v is not None:
                            SV.wait(te, pv_wait_v)
                        first = (ktc == 0)
                        last = (ktc == nk - 1)
                        te.matmul(ppv[:], v_sb[:, ktc * 512 + gi * 128:ktc * 512 + (gi + 1) * 128],
                                  P_sb[:, slot:slot + 512], start=first, stop=last)
                        ipl = te.matmul(pl[:], ones_cb[:], P_sb[:, slot:slot + 512],
                                        start=first, stop=last)
                        pv_done[(h, qd, ktc)] = STE.inc(ipl)
                    prev_pv = pv_done[(h, qd, nk - 1)]
                    rowv = prev_pv
                    STE.wait(v, rowv)
                    il = v.tensor_scalar(l_sb[:], pl[0:1, :], 1e-30, None, ALU.add)
                    SV.wait(v, SV.inc(il))
                    il2 = v.reciprocal(l_sb[:], l_sb[:])
                    rv = SV.inc(il2)
                    SV.wait(te, rv)
                    for jj in range(4):
                        itr = te.transpose(ptr32E[:, jj:jj + 1],
                                           l_sb[0:1, jj * 128:(jj + 1) * 128],
                                           sid32[0:1, 0:1])
                    t1 = STE.inc(itr)
                    STE.wait(v, t1)
                    ir1 = v.tensor_copy(rl_col[:, 0:4], ptr32E[:, 0:4])
                    ie1 = v.tensor_copy(att_s[:], ppv[:])
                    ev = SV.inc(ie1)
                    SV.wait(v, ev)
                    SV.wait(te, ev)
                    for jj in range(4):
                        itr2 = te.transpose(ptrE[:, jj * 128:(jj + 1) * 128],
                                            att_s[:, jj * 128:(jj + 1) * 128], sidb[:])
                    t2 = STE.inc(itr2)
                    STE.wait(v, t2)
                    for jj in range(4):
                        j = qd * 4 + jj
                        ifin = v.tensor_scalar(
                            attn_tok[:, j * 2048 + h * 128:j * 2048 + (h + 1) * 128],
                            ptrE[:, jj * 128:(jj + 1) * 128], rl_col[:, jj:jj + 1],
                            None, ALU.mult)
                    row_free.append(SV.inc(ifin))
            att_done = SV.n
            ps1E.close()
            es1E.close()
            attn_T = esE.enter_context(nc.sbuf_tensor("attTE", [128, 16 * 1024], BF16))
            c_wb = esE.enter_context(nc.sbuf_tensor("cwbE", [128, 16 * 2048], BF16))
            xoutE = esE.enter_context(nc.sbuf_tensor("xoutE", [128, 2048], F32))
            xin2E = esE.enter_context(nc.sbuf_tensor("xin2E", [128, 2048], F32))
            ptrb2 = [esE.enter_context(nc.psum_tensor(f"ptrb2E{i}", [128, 512], BF16)) for i in range(2)]
            pog = [esE.enter_context(nc.psum_tensor(f"pogE{i}", [128, 512], F32)) for i in range(4)]
            attn_T3 = attn_T[:].rearrange("p (a c) -> p a c", c=1024)
            SV.wait(sy, att_done)
            SE['ldc'].inc(sy.dma_start(out=c_wb[:, 0:9 * 2048],
                                       in_=GW[0:128, 80 * 512:116 * 512]), 16)
            SE['ldc'].inc(sy.dma_start(out=c_wb[:, 9 * 2048:16 * 2048],
                                       in_=GW[128:256, 0:28 * 512]), 16)
            ldc_all = SE['ldc'].n
            SV.wait(te, att_done)
            evac_doneE = []
            for t in range(NT):
                for q in range(4):
                    gidx = t * 4 + q
                    bank = ptrb2[gidx % 2]
                    if gidx >= 2:
                        SV.wait(te, evac_doneE[gidx - 2])
                    for i in range(4):
                        dc = q * 4 + i
                        ins = te.transpose(bank[:, i * 128:(i + 1) * 128],
                                           attn_tok[:, t * 2048 + dc * 128:t * 2048 + (dc + 1) * 128],
                                           sidb[:])
                    tv = STE.inc(ins)
                    STE.wait(v, tv)
                    dst = attn_T3[:, 4 * q:4 * q + 4, t * 128:(t + 1) * 128]
                    evac_doneE.append(SV.inc(v.tensor_copy(dst, bank[:])))
            aT_done = SV.n
            SV.wait(te, aT_done)
            SE['ldc'].wait(te, ldc_all)
            prev_apply = None
            for t in range(NT):
                if prev_apply is not None:
                    SV.wait(sy, prev_apply)
                SE['ldxr'].inc(sy.dma_start(out=xin2E[:], in_=x2_d[:, t * D:(t + 1) * D]), 16)
                if prev_apply is not None:
                    SV.wait(te, prev_apply)
                for fc in range(16):
                    for og in range(4):
                        imm = te.matmul(pog[og][:], attn_T3[:, fc, t * 128:(t + 1) * 128],
                                        c_wb[:, fc * 2048 + og * 512:fc * 2048 + (og + 1) * 512],
                                        start=(fc == 0), stop=(fc == 15))
                mmv = STE.inc(imm)
                STE.wait(v, mmv)
                STE.wait(sc, mmv)
                s0 = t * 24
                for og in range(4):
                    ist = v.bn_stats(stats[:, s0 + 6 * og:s0 + 6 * og + 6], pog[og][:])
                    SV.wait(v, SV.inc(ist))
                ia = v.bn_aggr(mvall[:, 2 * t:2 * t + 2], stats[:, s0:s0 + 24])
                SV.wait(v, SV.inc(ia))
                i1 = v.tensor_scalar(negm[:, t:t + 1], mvall[:, 2 * t:2 * t + 1], -1.0, None, ALU.mult)
                nv2 = SV.inc(i1)
                SV.wait(sc, nv2)
                i2 = sc.activation(sd_[:, t:t + 1], mvall[:, 2 * t + 1:2 * t + 2], AF.Sqrt,
                                   bias=eps_tile[:, 0:1], scale=1.0)
                sv2 = SS.inc(i2)
                SS.wait(v, sv2)
                i3 = v.reciprocal(rstd[:, t:t + 1], sd_[:, t:t + 1])
                SV.wait(v, SV.inc(i3))
                for og in range(4):
                    v.tensor_scalar(xoutE[:, og * 512:(og + 1) * 512], pog[og][:],
                                    negm[:, t:t + 1], rstd[:, t:t + 1], ALU.add, ALU.mult)
                SE['ldxr'].wait(v, 16 * (t + 1))
                iadd = v.tensor_tensor(xoutE[:], xoutE[:], xin2E[:], op=ALU.add)
                av = SV.inc(iadd)
                prev_apply = av
                SV.wait(g, av)
                SE['st'].inc(g.dma_start(out=x3_d[:, t * D:(t + 1) * D], in_=xoutE[:]), 16)
                SE['st'].wait(v)
            SE['st'].wait(g)
            esE.close()
            barrier()

            esF = contextlib.ExitStack()
            sbF = lambda n_, s_, d_: esF.enter_context(nc.sbuf_tensor(n_, s_, d_))
            xlnTF = sbF("xlnTF", [128, 16 * 1024], BF16)
            hs = sbF("hsF", [128, 64 * 1024], BF16)
            es0F = contextlib.ExitStack()
            xtF = es0F.enter_context(nc.sbuf_tensor("xtF", [128, 2 * D], F32))
            xlntF = es0F.enter_context(nc.sbuf_tensor("xlntF", [128, 2 * D], BF16))
            ptrbF = [es0F.enter_context(nc.psum_tensor(f"ptrbF{i}", [128, 512], BF16)) for i in range(2)]
            es1F = contextlib.ExitStack()
            xlnT3F = xlnTF[:].rearrange("p (a c) -> p a c", c=1024)
            emit_ln_transpose(nc, SF, v, sy, sc, te, x3_d, xtF, xlntF, xlnT3F, sidb,
                              stats, mvall, negm, sd_, rstd, eps_tile, ptrbF)
            xlnT_doneF = SV.n
            SV.wait(te, xlnT_doneF)
            es0F.close()
            sb1F = lambda n_, s_, d_: es1F.enter_context(nc.sbuf_tensor(n_, s_, d_))
            wbtF = sb1F("wbtF", [128, 2 * 512], BF16)
            gch = sb1F("gchF", [128, 8 * 512], F32)
            ht = sb1F("htF", [128, 8 * 512], BF16)
            at = sb1F("atF", [128, 8 * 512], BF16)
            cs_ = sb1F("csF", [128, 6 * 512], BF16)
            ts_ = sb1F("tsF", [128, 2 * 512], BF16)
            ppF = [esF.enter_context(nc.psum_tensor(f"ppF{i}", [128, 512], F32)) for i in range(8)]
            SV.wait(sy, xlnT_doneF)
            wcons = []
            ev_doneF = []
            sp_done = []
            for gq in range(16):
                for sweep, ubase in ((0, 160), (1, 416)):
                    for dc in range(16):
                        di = len(wcons)
                        if di >= 2:
                            STE.wait(sy, wcons[di - 2])
                        SF['ldw'].inc(sy.dma_start(
                            out=wbtF[:, (di % 2) * 512:(di % 2 + 1) * 512],
                            in_=gw_tile(ubase + gq * 16 + dc, 1)), 16)
                        SF['ldw'].wait(te, 16 * (di + 1))
                        if dc == 0 and ev_doneF:
                            SV.wait(te, ev_doneF[-1][0])
                            SS.wait(te, ev_doneF[-1][1])
                        for tb in range(2):
                            for sc_ in range(4):
                                imm = te.matmul(
                                    ppF[tb * 4 + sc_][:],
                                    wbtF[:, (di % 2) * 512 + sc_ * 128:(di % 2) * 512 + (sc_ + 1) * 128],
                                    xlnT3F[:, dc, tb * 512:(tb + 1) * 512],
                                    start=(dc == 0), stop=(dc == 15))
                        wcons.append(STE.inc(imm))
                    mmv = wcons[-1]
                    STE.wait(sc, mmv)
                    STE.wait(v, mmv)
                    if sweep == 0:
                        for i8 in range(8):
                            if gq > 0 and i8 == 0:
                                SV.wait(sc, sp_done[-1])
                            isl = sc.activation(gch[:, i8 * 512:(i8 + 1) * 512], ppF[i8][:],
                                                AF.Silu, bias=0.0, scale=1.0)
                        slv = SS.inc(isl)
                        ev_doneF.append((SV.n, slv))
                    else:
                        SS.wait(v)
                        for i8 in range(8):
                            iht = v.scalar_tensor_tensor(ht[:, i8 * 512:(i8 + 1) * 512],
                                                         ppF[i8][:], 1.0, gch[:, i8 * 512:(i8 + 1) * 512],
                                                         ALU.mult, ALU.mult)
                        hv = SV.inc(iht)
                        SV.wait(sc, hv)
                        for i8 in range(8):
                            iab = sc.activation(at[:, i8 * 512:(i8 + 1) * 512], ht[:, i8 * 512:(i8 + 1) * 512],
                                                AF.Abs, bias=0.0, scale=1.0)
                        abv = SS.inc(iab)
                        SS.wait(v, abv)
                        ev_doneF.append((SV.n, abv))
                        for tb in range(2):
                            a_ = [at[:, (tb * 4 + p) * 512:(tb * 4 + p + 1) * 512] for p in range(4)]
                            h_ = [ht[:, (tb * 4 + p) * 512:(tb * 4 + p + 1) * 512] for p in range(4)]
                            c_ = [cs_[:, i * 512:(i + 1) * 512] for i in range(6)]
                            t_ = [ts_[:, i * 512:(i + 1) * 512] for i in range(2)]
                            pairs = [(0, 1), (0, 2), (0, 3), (1, 2), (1, 3), (2, 3)]
                            for ci, (a, b) in enumerate(pairs):
                                v.tensor_tensor(c_[ci], a_[a], a_[b], op=ALU.is_ge)
                            def dstF(p):
                                return hs[:, ((p * 16 + gq) * 1024 + tb * 512):((p * 16 + gq) * 1024 + (tb + 1) * 512)]
                            v.tensor_tensor(t_[0], c_[0], c_[1], op=ALU.add)
                            v.tensor_tensor(t_[1], t_[0], c_[2], op=ALU.add)
                            v.scalar_tensor_tensor(dstF(0), t_[1], 2.0, h_[0], ALU.is_ge, ALU.mult)
                            v.tensor_tensor(t_[0], c_[3], c_[4], op=ALU.add)
                            v.tensor_tensor(t_[1], t_[0], c_[0], op=ALU.subtract)
                            v.scalar_tensor_tensor(dstF(1), t_[1], 1.0, h_[1], ALU.is_ge, ALU.mult)
                            v.tensor_tensor(t_[0], c_[5], c_[1], op=ALU.subtract)
                            v.tensor_tensor(t_[1], t_[0], c_[3], op=ALU.subtract)
                            v.scalar_tensor_tensor(dstF(2), t_[1], 0.0, h_[2], ALU.is_ge, ALU.mult)
                            v.tensor_tensor(t_[0], c_[2], c_[4], op=ALU.add)
                            v.tensor_tensor(t_[1], t_[0], c_[5], op=ALU.add)
                            isp = v.scalar_tensor_tensor(dstF(3), t_[1], 1.0, h_[3], ALU.is_le, ALU.mult)
                        sp_done.append(SV.inc(isp))
            hs_done = SV.n
            es1F.close()
            wdb = esF.enter_context(nc.sbuf_tensor("wdbF", [128, 2 * D], BF16))
            xoutF = esF.enter_context(nc.sbuf_tensor("xoutF", [128, D], F32))
            xin2F = esF.enter_context(nc.sbuf_tensor("xin2F", [128, D], F32))
            SV.wait(te, hs_done)
            SV.wait(sy, hs_done)
            pair_applyF = []
            wd_cons = []
            for pair in range(4):
                if pair >= 1:
                    SV.wait(sy, pair_applyF[-1])
                for ic in range(64):
                    di = pair * 64 + ic
                    if di >= 2:
                        STE.wait(sy, wd_cons[di - 2])
                    SF['ldwd'].inc(sy.dma_start(out=wdb[:, (di % 2) * D:(di % 2 + 1) * D],
                                                in_=gw_tile(672 + 4 * ic, 4)), 16)
                    SF['ldwd'].wait(te, 16 * (di + 1))
                    if ic == 0 and pair > 0:
                        SV.wait(te, pair_applyF[-1])
                    for tt in range(2):
                        t = pair * 2 + tt
                        for dg in range(4):
                            iw = te.matmul(ppF[tt * 4 + dg][:],
                                           hs[:, ic * 1024 + t * 128:ic * 1024 + (t + 1) * 128],
                                           wdb[:, (di % 2) * D + dg * 512:(di % 2) * D + (dg + 1) * 512],
                                           start=(ic == 0), stop=(ic == 63))
                    wd_cons.append(STE.inc(iw))
                mm_done = wd_cons[-1]
                STE.wait(v, mm_done)
                STE.wait(sc, mm_done)
                for tt in range(2):
                    t = pair * 2 + tt
                    SV.wait(sy)
                    SF['ldxr'].inc(sy.dma_start(out=xin2F[:], in_=x3_d[:, t * D:(t + 1) * D]), 16)
                    s0 = t * 24
                    for dg in range(4):
                        ist = v.bn_stats(stats[:, s0 + 6 * dg:s0 + 6 * dg + 6], ppF[tt * 4 + dg][:])
                        SV.wait(v, SV.inc(ist))
                    ia = v.bn_aggr(mvall[:, 2 * t:2 * t + 2], stats[:, s0:s0 + 24])
                    SV.wait(v, SV.inc(ia))
                    i1 = v.tensor_scalar(negm[:, t:t + 1], mvall[:, 2 * t:2 * t + 1], -1.0, None, ALU.mult)
                    nv = SV.inc(i1)
                    SV.wait(sc, nv)
                    i2 = sc.activation(sd_[:, t:t + 1], mvall[:, 2 * t + 1:2 * t + 2], AF.Sqrt,
                                       bias=eps_tile[:, 0:1], scale=1.0)
                    sv = SS.inc(i2)
                    SS.wait(v, sv)
                    i3 = v.reciprocal(rstd[:, t:t + 1], sd_[:, t:t + 1])
                    SV.wait(v, SV.inc(i3))
                    SF['st'].wait(v)
                    for dg in range(4):
                        v.tensor_scalar(xoutF[:, dg * 512:(dg + 1) * 512],
                                        ppF[tt * 4 + dg][:], negm[:, t:t + 1], rstd[:, t:t + 1],
                                        ALU.add, ALU.mult)
                    SF['ldxr'].wait(v)
                    iadd = v.tensor_tensor(xoutF[:], xoutF[:], xin2F[:], op=ALU.add)
                    av = SV.inc(iadd)
                    SV.wait(g, av)
                    SF['st'].inc(g.dma_start(out=xo_out[:, t * D:(t + 1) * D], in_=xoutF[:]), 16)
                pair_applyF.append(SV.n)
            SF['st'].wait(g)
            esF.close()
    return nc


_ROPE_PERM1 = np.concatenate([np.arange(0, 128, 2), np.arange(1, 128, 2)])

def _img_lhsT(wT, nout, dtype):
    out = np.empty((128, nout * 2048), dtype)
    w4 = np.asarray(wT).reshape(16, 128, nout, 128)
    for oc in range(nout):
        out[:, oc * 2048:(oc + 1) * 2048] = w4[:, :, oc, :].transpose(1, 0, 2).reshape(128, 2048)
    return np.ascontiguousarray(out)

def _prep_all(inputs):
    x = np.asarray(inputs["L0_attn_x"], np.float32).reshape(2 * T, D)
    Wkv = np.asarray(inputs["L0_attn_Wkv"], np.float32)
    bkv = np.asarray(inputs["L0_attn_bkv"], np.float32)
    vcols = np.concatenate([np.arange(g * 128 + 64, g * 128 + 128) for g in range(4)])
    wkvv = wimg(np.ascontiguousarray(Wkv[:, vcols]))
    bkvv = np.ascontiguousarray(bkv[vcols]).reshape(1, 256)
    identb = np.eye(128, dtype=BF)
    ident32 = np.eye(128, dtype=np.float32)
    w1T = np.asarray(inputs["L0_ffn_w1"], np.float32).T
    w2T = np.asarray(inputs["L0_ffn_w2"], np.float32).T
    w3T = np.asarray(inputs["L0_ffn_w3"], np.float32).T
    w1Tp = np.zeros((D, H0P), np.float32); w1Tp[:, :H0] = w1T
    w2Tp = np.zeros((D, H0P), np.float32); w2Tp[:, :H0] = w2T
    w3Tp = np.zeros((H0P, D), np.float32); w3Tp[:H0] = w3T
    w1cs, w2cs, w3rs = [], [], []
    for c in range(NCORES):
        w1cs.append(np.ascontiguousarray(
            w1Tp[:, c * 768:(c + 1) * 768].reshape(16, 128, 6, 128)
            .transpose(1, 2, 0, 3).reshape(128, 12288)))
        w2cs.append(np.ascontiguousarray(
            w2Tp[:, c * 768:(c + 1) * 768].reshape(16, 128, 6, 128)
            .transpose(1, 2, 0, 3).reshape(128, 12288)))
        w3rs.append(np.ascontiguousarray(
            w3Tp[c * 768:(c + 1) * 768].reshape(6, 128, D)
            .transpose(1, 0, 2).reshape(128, 12288)))
    q_w = np.asarray(inputs["L1_attn_q_w"], np.float32)
    kv_w = np.asarray(inputs["L1_attn_kv_w"], np.float32)
    qperm = np.concatenate([h * 128 + _ROPE_PERM1 for h in range(16)])
    kperm = np.concatenate([g * 128 + _ROPE_PERM1 for g in range(4)])
    qwT = np.ascontiguousarray(q_w[qperm].T)
    kwT = np.ascontiguousarray(kv_w[0:512][kperm].T)
    vwT = np.ascontiguousarray(kv_w[512:1024].T)
    cwT = np.ascontiguousarray(np.asarray(inputs["L1_attn_c_w"], np.float32).T)
    pairs = np.arange(64, dtype=np.float32)
    freqs = 1.0 / 10000.0 ** (2.0 * pairs / 128.0)
    pos = np.arange(T, dtype=np.float32)
    ang = pos[:, None] * freqs[None, :]
    cosf = np.cos(ang).astype(np.float32)
    sinf = np.sin(ang).astype(np.float32)
    jj = np.arange(HI)
    gq = jj // 512; rem = jj % 512; sc = rem // 128; cc = rem % 128
    perm_cols = 4 * (gq * 128 + cc) + sc
    ic = jj // 128; c2 = jj % 128
    perm_rows = 4 * ((ic % 16) * 128 + c2) + ic // 16
    wg_p = np.asarray(inputs["L1_ffn_w_gate"], np.float32)[:, perm_cols]
    wu_p = np.asarray(inputs["L1_ffn_w_up"], np.float32)[:, perm_cols]
    wd_p = np.ascontiguousarray(np.asarray(inputs["L1_ffn_w_down"], np.float32)[perm_rows, :])
    def wimg_f(wp):
        out = np.empty((256 * 128, 512), np.float32)
        for g in range(16):
            for dc in range(16):
                out[(g * 16 + dc) * 128:(g * 16 + dc + 1) * 128, :] = \
                    wp[dc * 128:(dc + 1) * 128, g * 512:(g + 1) * 512]
        return np.ascontiguousarray(out)
    wg_i = wimg_f(wg_p)
    wu_i = wimg_f(wu_p)
    def rows_img(mat, nrow, w):
        out = np.empty((128, nrow * w), np.float32)
        for t_ in range(nrow):
            out[:, t_ * w:(t_ + 1) * w] = mat[t_ * 128:(t_ + 1) * 128, :]
        return out
    LW = np.concatenate([
        _img_lhsT(qwT, 16, np.float32),
        _img_lhsT(kwT, 4, np.float32),
        rows_img(cwT, 16, 2048),
        rows_img(vwT, 16, 512),
        rows_img(wg_i, 256, 512),
        rows_img(wu_i, 256, 512),
        rows_img(wd_p, 64, 2048),
    ], axis=1)
    lw_sl = [np.ascontiguousarray(LW[:, c * 116 * 512:(c + 1) * 116 * 512])
             for c in range(NCORES)]
    prep = {
        "x": x, "wkvv": wkvv, "bkvv": bkvv, "identb": identb, "ident32": ident32,
        "w1cs": w1cs, "w2cs": w2cs, "w3rs": w3rs,
        "lw": lw_sl, "cosf": cosf, "sinf": sinf,
    }
    return prep

def _core_mask(c):
    kq = c % 4
    mask = np.ones((128, 512), np.float32)
    ii = np.arange(128)
    for tq in range(4):
        mm = mask[:, tq * 128:(tq + 1) * 128]
        if tq == kq:
            mm[ii[:, None] > ii[None, :]] = 0.0
        elif tq > kq:
            mm[:] = 0.0
    return mask.astype(BF)

_NC_CACHE = {}

def kernel(**inputs):
    P = _prep_all(inputs)
    maps = []
    for c in range(NCORES):
        rows = core_rows(c) % T
        maps.append({
            "x_img": img_tokens(P["x"], c),
            "wkvv": P["wkvv"], "bkvv": P["bkvv"],
            "identb": P["identb"], "ident32": P["ident32"],
            "w1cs": P["w1cs"][c], "w2cs": P["w2cs"][c], "w3rs": P["w3rs"][c],
            "lw": P["lw"][c],
            "cosd": np.ascontiguousarray(P["cosf"][rows].T),
            "sind": np.ascontiguousarray(P["sinf"][rows].T),
            "mask": _core_mask(c),
        })
    if "nc" not in _NC_CACHE:
        nc = bass.Bass("TRN2", target_bir_lowering=False)
        build_mega(nc)
        _NC_CACHE["nc"] = nc
    res = run_bass_kernel_spmd(_NC_CACHE["nc"], maps, core_ids=list(range(NCORES)))
    out = np.empty((2 * T, D), np.float32)
    for c in range(NCORES):
        out[core_rows(c)] = unimg_tokens(res.results[c]["xo"], D)
    return out.reshape(2, T, D)
